# version 17
# baseline (speedup 1.0000x reference)
import contextlib
import numpy as np
import ml_dtypes
import concourse.bass as bass
import concourse.mybir as mybir
from concourse.bass import IndirectOffsetOnAxis
from concourse.bass_utils import run_bass_kernel_spmd

F32 = mybir.dt.float32
BF16 = mybir.dt.bfloat16
I32 = mybir.dt.int32
AF = mybir.ActivationFunctionType
ALU = mybir.AluOpType
AX = mybir.AxisListType

N_CORES = 8
B_PER_CORE = 2
SEQ = 2048
T_CORE = B_PER_CORE * SEQ
D_MODEL = 1024
D_INNER = 2048
CHUNK = 512

bf16 = ml_dtypes.bfloat16



ENGINES = ("sync", "scalar", "vector", "tensor", "gpsimd")


class Tok:
    __slots__ = ("sem", "val", "dma", "key")

    def __init__(self, sem, val, dma=False, key=None):
        self.sem = sem
        self.val = val
        self.dma = dma
        self.key = key


class Prog:

    def __init__(self, nc, stack):
        self.nc = nc
        self.stack = stack
        self.queues = {e: [] for e in ENGINES}
        self.sems = {e: stack.enter_context(nc.semaphore(f"s_{e}")) for e in ENGINES}
        self.counts = {e: 0 for e in ENGINES}
        self.streams = {}
        self.waited = {e: {} for e in ENGINES}

    def _stream(self, key):
        st = self.streams.get(key)
        if st is None:
            sem = self.stack.enter_context(self.nc.semaphore(f"d_{len(self.streams)}"))
            st = [sem, 0, None]
            self.streams[key] = st
        return st

    def drain(self, eng):
        self.queues[eng].append(lambda e: e.drain())

    def wait(self, eng, tok):
        if tok is None:
            return
        seen = self.waited[eng]
        sid = id(tok.sem)
        if seen.get(sid, 0) >= tok.val:
            return
        seen[sid] = tok.val
        sem = tok.sem
        val = tok.val
        self.queues[eng].append(lambda e: e.wait_ge(sem, val))

    def emit(self, eng, fn, deps=(), mark=False, dma=False, stream=None):
        flat = []
        for d in deps:
            if isinstance(d, (list, tuple)):
                flat.extend(x for x in d if x is not None)
            elif d is not None:
                flat.append(d)
        deps = flat
        if dma:
            key = stream if stream is not None else ("q", eng)
            st = self._stream(key)
            if st[2] is not None:
                self.wait(eng, st[2])
            for d in deps:
                self.wait(eng, d)
            st[1] += 16
            sem, val = st[0], st[1]

            def wrapped(e, fn=fn, sem=sem):
                return fn(e).then_inc(sem, 16)

            self.queues[eng].append(wrapped)
            tok = Tok(sem, val, True, key)
            st[2] = tok
            return tok
        for d in deps:
            self.wait(eng, d)
        if mark:
            self.counts[eng] += 1
            sem = self.sems[eng]

            def wrapped2(e, fn=fn, sem=sem):
                return fn(e).then_inc(sem, 1)

            self.queues[eng].append(wrapped2)
            return Tok(sem, self.counts[eng], False, eng)
        self.queues[eng].append(fn)
        return None

    def barrier(self):
        for e in ("vector", "scalar", "gpsimd"):
            if self.queues[e]:
                self.drain(e)
        snapshot_counts = dict(self.counts)
        snapshot_streams = {k: st[1] for k, st in self.streams.items()}
        for e2 in ENGINES:
            q = self.queues[e2]
            for e1 in ENGINES:
                if e1 == e2 or snapshot_counts[e1] == 0:
                    continue
                sem = self.sems[e1]
                val = snapshot_counts[e1]
                if self.waited[e2].get(id(sem), 0) < val:
                    self.waited[e2][id(sem)] = val
                    q.append(lambda e, sem=sem, val=val: e.wait_ge(sem, val))
            for key, val in snapshot_streams.items():
                sem = self.streams[key][0]
                if val and self.waited[e2].get(id(sem), 0) < val:
                    self.waited[e2][id(sem)] = val
                    q.append(lambda e, sem=sem, val=val: e.wait_ge(sem, val))

    def final_waits(self, eng="sync"):
        def body(e):
            for key, st in self.streams.items():
                e.wait_ge(st[0], st[1])
            for en in ENGINES:
                if self.counts[en] and en != eng:
                    e.wait_ge(self.sems[en], self.counts[en])
        self.queues[eng].append(body)

    def finalize(self, block):
        for eng_name in ENGINES:
            q = self.queues[eng_name]
            if not q:
                continue
            section = getattr(block, eng_name)

            def make(q):
                def body(e):
                    for fn in q:
                        fn(e)
                return body
            section(make(q))



def emit_ln_to_fm(P, E, last, ctx, x_d, tok0, n_tiles=4):
    xt, ln_s1, ln_s2, ln_mean, ln_var, ln_msq, ln_rstd, ln_sq, xlnL, xln_bfL, xfm, eyebf, pTR = (
        ctx["xt"], ctx["ln_s1"], ctx["ln_s2"], ctx["ln_mean"], ctx["ln_var"], ctx["ln_msq"],
        ctx["ln_rstd"], ctx["ln_sq"], ctx["xln"], ctx["xln_bf"], ctx["xfm"], ctx["eyebf"], ctx["pTR"])
    if not isinstance(xlnL, list):
        xlnL = [xlnL, xlnL]
    if not isinstance(xln_bfL, list):
        xln_bfL = [xln_bfL, xln_bfL]
    AXX = AX.X
    t_xfm = []
    extra_dep = ctx.get("ln_extra_dep")
    for tt in range(n_tiles):
        row0 = tok0 + tt * 128
        buf = xt[tt % 2]
        xln = xlnL[tt % 2]
        xln_bf = xln_bfL[tt % 2]
        lnp = tt % 2
        tl = E("sync", lambda e, buf=buf, row0=row0: e.dma_start(
            out=buf[:], in_=x_d[row0:row0 + 128, :]),
            deps=[last.get(("xt", tt % 2))] if last.get(("xt", tt % 2)) else [],
            mark=True, dma=True, stream=("xt", tt % 2))
        ts1 = E("scalar", lambda e, buf=buf, tt=tt, xln=xln: e.activation(
            xln[:], buf[:], AF.Square, accum_out=ln_s2[:, tt:tt + 1]),
            deps=[tl] + [d for d in (last.get(("lnuse", tt)), last.get(("xln_rd", lnp)), extra_dep) if d],
            mark=True)
        ts2 = E("vector", lambda e, buf=buf, tt=tt: e.reduce_sum(
            ln_s1[:, tt:tt + 1], buf[:], axis=AXX), deps=[tl], mark=True)
        P.drain("vector")
        E("vector", lambda e, tt=tt: e.tensor_scalar_mul(
            ln_mean[:, tt:tt + 1], ln_s1[:, tt:tt + 1], 1.0 / D_MODEL), deps=[ts2])
        E("vector", lambda e, tt=tt: e.tensor_scalar_mul(
            ln_var[:, tt:tt + 1], ln_s2[:, tt:tt + 1], 1.0 / D_MODEL), deps=[ts1])
        P.drain("vector")
        E("vector", lambda e, tt=tt: e.tensor_tensor(
            ln_msq[:, tt:tt + 1], ln_mean[:, tt:tt + 1], ln_mean[:, tt:tt + 1], ALU.mult))
        P.drain("vector")
        E("vector", lambda e, tt=tt: e.tensor_sub(
            ln_var[:, tt:tt + 1], ln_var[:, tt:tt + 1], ln_msq[:, tt:tt + 1]))
        P.drain("vector")
        tve = E("vector", lambda e, tt=tt: e.tensor_scalar_add(
            ln_var[:, tt:tt + 1], ln_var[:, tt:tt + 1], 1e-5), mark=True)
        tsq = E("scalar", lambda e, tt=tt: e.activation(
            ln_rstd[:, tt:tt + 1], ln_var[:, tt:tt + 1], AF.Sqrt), deps=[tve], mark=True)
        trc = E("vector", lambda e, tt=tt: e.reciprocal(
            ln_rstd[:, tt:tt + 1], ln_rstd[:, tt:tt + 1]), deps=[tsq], mark=True)
        P.drain("vector")
        txl = E("vector", lambda e, buf=buf, tt=tt, xln=xln: e.tensor_scalar(
            xln[:], buf[:], ln_mean[:, tt:tt + 1], ln_rstd[:, tt:tt + 1],
            ALU.subtract, ALU.mult),
            deps=[trc] + ([last.get(("xlnrd", lnp))] if last.get(("xlnrd", lnp)) else []), mark=True)
        last[("lnuse", tt)] = txl
        txb = E("vector", lambda e, xln=xln, xln_bf=xln_bf: e.tensor_copy(out=xln_bf[:], in_=xln[:]),
                deps=[txl] + ([last.get(("xbf_store", lnp))] if last.get(("xbf_store", lnp)) else []),
                mark=True)
        last[("xln_rd", lnp)] = txb
        if ctx.get("store_bf_rows") is not None:
            dstd = ctx["store_bf_rows"]
            last[("xbf_store", lnp)] = E("sync", lambda e, dstd=dstd, row0=row0, xln_bf=xln_bf:
                e.dma_start(out=dstd[row0:row0 + 128, :], in_=xln_bf[:]), deps=[txb], mark=True,
                dma=True, stream=("xbf", lnp))
        if ctx.get("keep_tm") is not None:
            keep = ctx["keep_tm"][tt]
            tkp = E("vector", lambda e, keep=keep, xln=xln: e.tensor_copy(out=keep[:], in_=xln[:]),
                    deps=[d for d in (last.get(("keep_rd", tt)),) if d], mark=True)
            last[("xln_rd", lnp)] = tkp
        if ctx.get("dbg_xln_hook"):
            ctx["dbg_xln_hook"](tt, txb)
        tev = None
        ttr = None
        xfm32 = ctx.get("xfm32")
        pTRv = pTR if isinstance(pTR, list) else [pTR, pTR]
        pTRFv = ctx["pTRF"] if isinstance(ctx.get("pTRF"), list) else [ctx.get("pTRF")] * 2
        for kd in range(8):
            pv = pTRv[kd % 2]
            ttr = E("tensor", lambda e, kd=kd, pv=pv, xln_bf=xln_bf: e.transpose(
                pv[:], xln_bf[:, kd * 128:(kd + 1) * 128], eyebf[:]),
                deps=[txb] + [d for d in (ctx.get("pe_ready"), last.get(("pTRp", kd % 2))) if d],
                mark=True)
            tev = E("scalar", lambda e, kd=kd, tt=tt, pv=pv: e.copy(
                out=xfm[:, kd * CHUNK + tt * 128: kd * CHUNK + (tt + 1) * 128],
                in_=pv[:]), deps=[ttr], mark=True)
            last[("pTRp", kd % 2)] = tev
            t_xfm.append(tev)
            if xfm32 is not None:
                pf = pTRFv[kd % 2]
                ttr2 = E("tensor", lambda e, kd=kd, pf=pf, xln=xln: e.transpose(
                    pf[:], xln[:, kd * 128:(kd + 1) * 128], ctx["eyef"][:]),
                    deps=[txb] + [d for d in (last.get(("pTRFp", kd % 2)),) if d], mark=True)
                tev2 = E("scalar", lambda e, kd=kd, tt=tt, pf=pf: e.copy(
                    out=xfm32[:, kd * CHUNK + tt * 128: kd * CHUNK + (tt + 1) * 128],
                    in_=pf[:]), deps=[ttr2], mark=True)
                last[("pTRFp", kd % 2)] = tev2
                t_xfm.append(tev2)
                ttr = ttr2
        last[("xlnrd", lnp)] = ttr
        last[("xt", tt % 2)] = ttr
    return t_xfm


def emit_residual_add_store(P, E, last, ctx, x_d, out_d, tok0, src_fm, src_deps,
                            n_m=8, du=None):
    xres, eyef, pTRF = ctx["xres"], ctx["eyef"], ctx["pTRF"]
    t_xr = []
    for tt in range(4):
        row0 = tok0 + tt * 128
        t_xr.append(E("sync", lambda e, tt=tt, row0=row0: e.dma_start(
            out=xres[tt][:], in_=x_d[row0:row0 + 128, :]),
            deps=[last.get(("xres", tt))] if last.get(("xres", tt)) else [],
            mark=True, dma=True, stream=("xr", tt)))
    pTRFv = pTRF if isinstance(pTRF, list) else [pTRF, pTRF]
    for m in range(n_m):
        view = src_fm[m]
        ttr = None
        for tt in range(4):
            pf = pTRFv[tt % 2]
            ttr = E("tensor", lambda e, tt=tt, view=view, pf=pf: e.transpose(
                pf[:], view[:, tt * 128:(tt + 1) * 128], eyef[:]),
                deps=[src_deps[m]] + ([last.get(("pTRFr", tt % 2))] if last.get(("pTRFr", tt % 2)) else []),
                mark=True)
            tad = E("vector", lambda e, tt=tt, m=m, pf=pf: e.tensor_tensor(
                xres[tt][:, m * 128:(m + 1) * 128], xres[tt][:, m * 128:(m + 1) * 128],
                pf[:], ALU.add), deps=[ttr, t_xr[tt]], mark=True)
            last[("pTRFr", tt % 2)] = tad
            last[("xadd", tt, m)] = tad
        last[("src_fm_rd", m)] = ttr
    if du is not None:
        keep_list, dbc_t, scr = du
        for tt in range(4):
            tdu1 = E("vector", lambda e, tt=tt: e.tensor_tensor(
                scr[:], keep_list[tt][:], dbc_t[:], ALU.mult),
                deps=[last[("xadd", tt, n_m - 1)]], mark=True)
            tdu2 = E("vector", lambda e, tt=tt: e.tensor_tensor(
                xres[tt][:], xres[tt][:], scr[:], ALU.add), mark=True)
            last[("xadd", tt, n_m - 1)] = tdu2
            last[("keep_rd", tt)] = tdu2
    stores = []
    for tt in range(4):
        row0 = tok0 + tt * 128
        tst = E("sync", lambda e, tt=tt, row0=row0: e.dma_start(
            out=out_d[row0:row0 + 128, :], in_=xres[tt][:]),
            deps=[last[("xadd", tt, m)] for m in range(n_m)], mark=True, dma=True,
            stream=("xr", tt))
        last[("xres", tt)] = tst
        stores.append(tst)
    return stores


def emit_w_convert(P, E, pieces, stage_f, stage_b):
    half = stage_f.shape[1] // 2
    halfb = stage_b.shape[1] // 2
    prev_tc = {0: None, 1: None}
    prev_st = {0: None, 1: None}
    sts = []
    for i, (src, dst, width) in enumerate(pieces):
        par = i % 2
        assert width <= half and width <= halfb
        fview = stage_f[:, par * half: par * half + width]
        bview = stage_b[:, par * halfb: par * halfb + width]
        tl = E("sync", lambda e, src=src, fview=fview: e.dma_start(out=fview, in_=src),
               deps=[prev_tc[par]] if prev_tc[par] else [], mark=True, dma=True,
               stream=("wcl", par))
        tc = E("vector", lambda e, fview=fview, bview=bview: e.tensor_copy(out=bview, in_=fview),
               deps=[tl] + ([prev_st[par]] if prev_st[par] else []), mark=True)
        prev_tc[par] = tc
        prev_st[par] = E("sync", lambda e, dst=dst, bview=bview: e.dma_start(out=dst, in_=bview),
                         deps=[tc], mark=True, dma=True, stream=("wcs", par))
        sts.append(prev_st[par])
    return sts[-2:] if len(sts) >= 2 else sts



def emit_stage_A(nc, P, stack, d, dbg=False):
    x_d = d["x"]
    w_in_d = d["W_in"]
    convw_d = d["convw"]
    convb_d = d["convb"]
    wx_d = d["Wx"]
    alog_d = d["Alog"]
    dp_d = d["Dp"]
    w_out_d = d["W_out"]
    eyebf_d = d["eye_bf"]
    eyef_d = d["eye_f"]
    ones1_d = d["ones1"]
    oh16_d = d["oh16"]
    consts_d = d["consts"]
    x1_d = d["x1"]
    winbf_d = d["winbf"]
    woutbf_d = d["woutbf"]
    for _k in d:
        pass
    dbg_xc = d.get("dbg_xc")
    dbg_yg = d.get("dbg_yg")
    dbg_dr = d.get("dbg_dr")
    dbg_bc = d.get("dbg_bc")
    dbg_h = d.get("dbg_h")
    dbg_fm = d.get("dbg_fm")
    dbg_xln = d.get("dbg_xln")
    dbg_cols = d.get("dbg_cols")
    S = stack.enter_context
    sb = lambda _n, _s, _t: nc.sbuf_tensor("A_" + _n, _s, _t)
    ps = lambda _n, _s, _t: nc.psum_tensor("A_" + _n, _s, _t)

    convw = S(sb("sconvw", [128, 64], F32))
    convb = S(sb("sconvb", [128, 16], F32))
    wx = S(sb("swx", [128, 16 * 96], F32))
    alog = S(sb("salog", [128, 256], F32))
    a_all = S(sb("a_all", [128, 256], F32))
    dp = S(sb("sdp", [128, 16], F32))
    eyebf = S(sb("eyebf", [128, 128], BF16))
    eyef = S(sb("eyef", [128, 128], F32))
    ones1 = S(sb("sones1", [1, 128], F32))
    oh16 = S(sb("soh16", [16, 2048], F32))
    consts = S(sb("sconsts", [128, 4], F32))

    xt = [S(sb(f"xt{i}", [128, D_MODEL], F32)) for i in range(2)]
    ln_s1 = S(sb("ln_s1", [128, 4], F32))
    ln_s2 = S(sb("ln_s2", [128, 4], F32))
    ln_mean = S(sb("ln_mean", [128, 4], F32))
    ln_var = S(sb("ln_var", [128, 4], F32))
    ln_msq = S(sb("ln_msq", [128, 4], F32))
    ln_rstd = S(sb("ln_rstd", [128, 4], F32))
    xln = S(sb("xln", [128, D_MODEL], F32))
    ln_sq = xln
    xln_bf = S(sb("xln_bf", [128, D_MODEL], BF16))

    xfm = S(sb("xfm", [128, 8 * CHUNK], BF16))
    xc = S(sb("xc", [128, 16 * CHUNK], F32))
    wconv_f = xc[:, 0:4096]
    wconv_b = xc[:, 4096:6144].bitcast(BF16)
    hslab = S(sb("hslab", [128, 4 * CHUNK], F32))
    ygated = S(sb("ygated", [128, 16 * CHUNK], BF16))
    dbrep = S(sb("dbrep", [128, 16 * CHUNK], BF16))
    crep = S(sb("crep", [128, 16 * CHUNK], F32))
    xin_tmp = S(sb("xin_tmp", [128, CHUNK + 3], F32))
    tails = S(sb("tails", [128, 16 * 3], F32))
    carry = S(sb("carry", [128, 16 * 16], F32))
    sBC = S(sb("sBC", [16, 2 * CHUNK], F32))
    drep = S(sb("drep", [128, CHUNK], F32))
    sp_t1 = hslab[:, 0:CHUNK]
    sp_t2 = hslab[:, CHUNK:2 * CHUNK]
    sdraw = sp_t2[0:1, :]
    abar = [S(sb(f"abar{i}", [128, CHUNK], F32)) for i in range(3)]
    ubuf = [S(sb(f"ubuf{i}", [128, CHUNK], F32)) for i in range(3)]

    xcd_scr = S(sb("xcd_scr", [128, CHUNK], F32))
    yout_fm = S(sb("yout_fm", [128, CHUNK], F32))
    wins = [S(sb(f"wins{i}", [128, 8 * 128], BF16)) for i in range(2)]
    wouts = [S(sb(f"wouts{i}", [128, 16 * 128], BF16)) for i in range(2)]
    xres = [S(sb(f"xres{i}", [128, D_MODEL], F32)) for i in range(4)]

    pXI = S(ps("pXI", [128, 512], F32))
    pRES = S(ps("pRES", [128, 512], F32))
    pSSM = S(ps("pSSM", [128, 512], F32))
    pBC = pSSM
    pY = S(ps("pY", [128, 512], F32))
    pW = pY
    pTR = [S(ps("pTRa", [128, 128], BF16)), S(ps("pTRb", [128, 128], BF16))]
    pTRF = [S(ps("pTRFa", [128, 128], F32)), S(ps("pTRFb", [128, 128], F32))]

    E = P.emit
    last = {}

    t_consts = []
    for dst, src in [(convw, convw_d), (convb, convb_d), (alog, alog_d), (dp, dp_d),
                     (eyebf, eyebf_d), (eyef, eyef_d), (ones1, ones1_d), (oh16, oh16_d),
                     (consts, consts_d)]:
        t_consts.append(E("sync", lambda e, dst=dst, src=src: e.dma_start(out=dst[:], in_=src),
                          mark=True, dma=True))
    t_consts.append(E("sync", lambda e: e.dma_start(
        out=wx[:], in_=wx_d.rearrange("(dt p) c -> p dt c", p=128)), mark=True, dma=True))

    tA1 = E("scalar", lambda e: e.activation(a_all[:], alog[:], AF.Exp), deps=t_consts, mark=True)
    tA = E("vector", lambda e: e.tensor_scalar_mul(a_all[:], a_all[:], -1.0), deps=[tA1], mark=True)

    pieces = []
    for k in range(8):
        for hh in range(2):
            pieces.append((w_in_d[k * 128:(k + 1) * 128, hh * 2048:(hh + 1) * 2048],
                           winbf_d[k * 128:(k + 1) * 128, hh * 2048:(hh + 1) * 2048], 2048))
    for k in range(16):
        pieces.append((w_out_d[k * 128:(k + 1) * 128, :], woutbf_d[k * 128:(k + 1) * 128, :], 1024))
    t_wconv_last = emit_w_convert(P, E, pieces, wconv_f, wconv_b)

    last_stores = []

    for b in range(B_PER_CORE):
        tz = E("vector", lambda e: e.memset(carry[:], 0.0), mark=True)
        for cch in range(4):
            ch = b * 4 + cch
            tok0 = b * SEQ + cch * CHUNK

            t_xfm = []
            for tt in range(4):
                row0 = tok0 + tt * 128
                buf = xt[tt % 2]
                tl = E("sync", lambda e, buf=buf, row0=row0: e.dma_start(
                    out=buf[:], in_=x_d[row0:row0 + 128, :]),
                    deps=[last.get(("xt", tt % 2))] if last.get(("xt", tt % 2)) else [],
                    mark=True, dma=True)
                col = lambda t_, tt=tt: t_[:, tt:tt + 1]
                ts1 = E("scalar", lambda e, buf=buf, tt=tt: e.activation(
                    ln_sq[:], buf[:], AF.Square, accum_out=ln_s2[:, tt:tt + 1]),
                    deps=[tl] + ([last.get(("lnuse", tt))] if last.get(("lnuse", tt)) else [])
                    + ([last.get("xln_rd")] if last.get("xln_rd") else []),
                    mark=True)
                ts2 = E("vector", lambda e, buf=buf, tt=tt: e.reduce_sum(
                    ln_s1[:, tt:tt + 1], buf[:], axis=AX.X), deps=[tl], mark=True)
                P.drain("vector")
                E("vector", lambda e, tt=tt: e.tensor_scalar_mul(
                    ln_mean[:, tt:tt + 1], ln_s1[:, tt:tt + 1], 1.0 / D_MODEL), deps=[ts2])
                E("vector", lambda e, tt=tt: e.tensor_scalar_mul(
                    ln_var[:, tt:tt + 1], ln_s2[:, tt:tt + 1], 1.0 / D_MODEL), deps=[ts1])
                P.drain("vector")
                E("vector", lambda e, tt=tt: e.tensor_tensor(
                    ln_msq[:, tt:tt + 1], ln_mean[:, tt:tt + 1], ln_mean[:, tt:tt + 1], ALU.mult))
                P.drain("vector")
                E("vector", lambda e, tt=tt: e.tensor_sub(
                    ln_var[:, tt:tt + 1], ln_var[:, tt:tt + 1], ln_msq[:, tt:tt + 1]))
                P.drain("vector")
                tve = E("vector", lambda e, tt=tt: e.tensor_scalar_add(
                    ln_var[:, tt:tt + 1], ln_var[:, tt:tt + 1], 1e-5), mark=True)
                tsq = E("scalar", lambda e, tt=tt: e.activation(
                    ln_rstd[:, tt:tt + 1], ln_var[:, tt:tt + 1], AF.Sqrt), deps=[tve], mark=True)
                trc = E("vector", lambda e, tt=tt: e.reciprocal(
                    ln_rstd[:, tt:tt + 1], ln_rstd[:, tt:tt + 1]), deps=[tsq], mark=True)
                P.drain("vector")
                txl = E("vector", lambda e, buf=buf, tt=tt: e.tensor_scalar(
                    xln[:], buf[:], ln_mean[:, tt:tt + 1], ln_rstd[:, tt:tt + 1],
                    ALU.subtract, ALU.mult),
                    deps=[trc] + ([last.get("xlnrd")] if last.get("xlnrd") else []), mark=True)
                last[("lnuse", tt)] = txl
                txb = E("vector", lambda e: e.tensor_copy(out=xln_bf[:], in_=xln[:]),
                        deps=[txl], mark=True)
                last["xln_rd"] = txb
                if dbg and ch == 0 and tt == 3:
                    E("sync", lambda e: e.dma_start(out=dbg_xln, in_=xln[:]), deps=[txb], mark=True, dma=True)
                    E("sync", lambda e: e.dma_start(out=dbg_cols[:, 0:4], in_=ln_s1[:]), deps=[txb], mark=True, dma=True)
                    E("sync", lambda e: e.dma_start(out=dbg_cols[:, 4:8], in_=ln_s2[:]), deps=[txb], mark=True, dma=True)
                    E("sync", lambda e: e.dma_start(out=dbg_cols[:, 8:12], in_=ln_mean[:]), deps=[txb], mark=True, dma=True)
                    E("sync", lambda e: e.dma_start(out=dbg_cols[:, 12:16], in_=ln_var[:]), deps=[txb], mark=True, dma=True)
                    E("sync", lambda e: e.dma_start(out=dbg_cols[:, 16:20], in_=ln_rstd[:]), deps=[txb], mark=True, dma=True)
                last[("xt", tt % 2)] = txb
                tev = None
                for kd in range(8):
                    pv = pTR[kd % 2]
                    ttr = E("tensor", lambda e, kd=kd, pv=pv: e.transpose(
                        pv[:], xln_bf[:, kd * 128:(kd + 1) * 128], eyebf[:]),
                        deps=[txb, tA, t_wconv_last] +
                        ([last.get(("pTRp", kd % 2))] if last.get(("pTRp", kd % 2)) else []),
                        mark=True)
                    tev = E("scalar", lambda e, kd=kd, tt=tt, pv=pv: e.copy(
                        out=xfm[:, kd * CHUNK + tt * 128: kd * CHUNK + (tt + 1) * 128],
                        in_=pv[:]), deps=[ttr], mark=True)
                    last[("pTRp", kd % 2)] = tev
                    t_xfm.append(tev)
                last["xlnrd"] = ttr
                last[("xt", tt % 2)] = ttr

            t_xc = {}
            t_ssm_last = None
            for dt in range(16):
                wbuf = wins[dt % 2]
                twl = E("sync", lambda e, dt=dt, wbuf=wbuf: e.dma_start(
                    out=wbuf[:],
                    in_=winbf_d.rearrange("(k p) m -> p k m", p=128)[:, :, dt * 128:(dt + 1) * 128]),
                    deps=[t_wconv_last] + ([last.get(("win", dt % 2))] if last.get(("win", dt % 2)) else []),
                    mark=True, dma=True, stream=("win", dt % 2))
                tmm = None
                for k in range(8):
                    tmm = E("tensor", lambda e, k=k, wbuf=wbuf: e.matmul(
                        pXI[:], wbuf[:, k * 128:(k + 1) * 128], xfm[:, k * CHUNK:(k + 1) * CHUNK],
                        start=(k == 0), stop=(k == 7)),
                        deps=([twl] + t_xfm + [last.get("pXI")] if k == 0 else []),
                        mark=(k == 7))
                last[("win", dt % 2)] = tmm
                tev = E("scalar", lambda e: e.copy(out=xin_tmp[:, 3:3 + CHUNK], in_=pXI[:]),
                        deps=[tmm] + ([last.get("xin_rd")] if last.get("xin_rd") else []), mark=True)
                last["pXI"] = tev
                if cch == 0:
                    ttl = E("vector", lambda e: e.memset(xin_tmp[:, 0:3], 0.0), deps=[tev], mark=True)
                else:
                    ttl = E("vector", lambda e, dt=dt: e.tensor_copy(
                        out=xin_tmp[:, 0:3], in_=tails[:, dt * 3:(dt + 1) * 3]), deps=[tev], mark=True)
                P.drain("vector")
                tc0 = E("vector", lambda e, dt=dt: e.tensor_scalar(
                    sp_t1, xin_tmp[:, 0:CHUNK], convw[:, dt * 4:dt * 4 + 1],
                    convb[:, dt:dt + 1], ALU.mult, ALU.add), deps=[tev, ttl], mark=True)
                tcl = tc0
                for j in range(1, 4):
                    tcl = E("vector", lambda e, dt=dt, j=j: e.scalar_tensor_tensor(
                        sp_t1, xin_tmp[:, j:j + CHUNK], convw[:, dt * 4 + j:dt * 4 + j + 1],
                        sp_t1, ALU.mult, ALU.add), mark=(j == 3))
                tnt = E("vector", lambda e, dt=dt: e.tensor_copy(
                    out=tails[:, dt * 3:(dt + 1) * 3], in_=xin_tmp[:, CHUNK:CHUNK + 3]),
                    deps=[tcl], mark=True)
                last["xin_rd"] = tnt
                txc = E("scalar", lambda e, dt=dt: e.activation(
                    xc[:, dt * CHUNK:(dt + 1) * CHUNK], sp_t1, AF.Silu),
                    deps=[tcl, t_wconv_last] + ([last.get(("xc", dt))] if last.get(("xc", dt)) else []),
                    mark=True)
                t_xc[dt] = txc
                t_ssm_last = E("tensor", lambda e, dt=dt: e.matmul(
                    pSSM[0:96, :], wx[:, dt * 96:(dt + 1) * 96], xc[:, dt * CHUNK:(dt + 1) * CHUNK],
                    start=(dt == 0), stop=(dt == 15), skip_group_check=True),
                    deps=[txc] + ([last.get("pSSM")] if dt == 0 and last.get("pSSM") else []),
                    mark=(dt == 15))
                twr = E("sync", lambda e, dt=dt: e.dma_start(
                    out=wins[dt % 2][:],
                    in_=winbf_d.rearrange("(k p) m -> p k m", p=128)[:, :, 2048 + dt * 128:2048 + (dt + 1) * 128]),
                    deps=[t_wconv_last] + ([last.get(("win", dt % 2))] if last.get(("win", dt % 2)) else []),
                    mark=True, dma=True, stream=("win", dt % 2))
                tmr = None
                for k in range(8):
                    tmr = E("tensor", lambda e, k=k, dt=dt: e.matmul(
                        pRES[:], wins[dt % 2][:, k * 128:(k + 1) * 128],
                        xfm[:, k * CHUNK:(k + 1) * CHUNK], start=(k == 0), stop=(k == 7)),
                        deps=([twr] + ([last.get("pRES")] if last.get("pRES") else [])
                              if k == 0 else []), mark=(k == 7))
                last[("win", dt % 2)] = tmr
                tsr = E("scalar", lambda e, dt=dt: e.activation(
                    ygated[:, dt * CHUNK:(dt + 1) * CHUNK], pRES[:], AF.Silu),
                    deps=[tmr] + ([last.get(("yg", dt))] if last.get(("yg", dt)) else []), mark=True)
                last["pRES"] = tsr
                last[("sres", dt)] = tsr

            teB = E("scalar", lambda e: e.copy(out=sBC[:, 0:CHUNK], in_=pSSM[0:16, :]),
                    deps=[t_ssm_last] + ([last.get("sBC_rd")] if last.get("sBC_rd") else []), mark=True)
            teC = E("scalar", lambda e: e.copy(out=sBC[:, CHUNK:2 * CHUNK], in_=pSSM[32:48, :]), mark=True)
            teD = E("scalar", lambda e: e.copy(out=sdraw, in_=pSSM[64:65, :]), mark=True)
            last["pSSM"] = teD
            tbd = E("tensor", lambda e: e.matmul(pBC[:], ones1[:], sdraw, start=True, stop=True),
                    deps=[teD] + ([last.get("pBC")] if last.get("pBC") else []), mark=True)
            tp1 = E("scalar", lambda e: e.activation(sp_t1, pBC[:], AF.Abs), deps=[tbd], mark=True)
            tp2 = E("scalar", lambda e: e.activation(sp_t2, sp_t1, AF.Exp, scale=-1.0), mark=True)
            tp3 = E("scalar", lambda e: e.activation(sp_t2, sp_t2, AF.Ln, bias=consts[:, 0:1]),
                    mark=True)
            tp4 = E("vector", lambda e: e.tensor_scalar_max(sp_t1, pBC[:], 0.0),
                    deps=[tbd, tp2], mark=True)
            tdr = E("vector", lambda e: e.tensor_add(drep[:], sp_t1, sp_t2),
                    deps=[tp3, tp4] + ([last.get("drep_rd")] if last.get("drep_rd") else []), mark=True)
            last["pBC"] = tp4

            t_rep = {}
            for n in range(16):
                tb1 = E("tensor", lambda e, n=n: e.matmul(
                    pBC[:], oh16[:, n * 128:(n + 1) * 128], sBC[:, 0:CHUNK], start=True, stop=True),
                    deps=[teB, last["pBC"], tdr], mark=True)
                tb2 = E("vector", lambda e, n=n: e.tensor_tensor(
                    dbrep[:, n * CHUNK:(n + 1) * CHUNK], pBC[:], drep[:], ALU.mult),
                    deps=[tb1] + ([last.get(("dbrep", n))] if last.get(("dbrep", n)) else []), mark=True)
                tc1 = E("tensor", lambda e, n=n: e.matmul(
                    pBC[:], oh16[:, n * 128:(n + 1) * 128], sBC[:, CHUNK:2 * CHUNK], start=True, stop=True),
                    deps=[teC, tb2], mark=True)
                tc2 = E("scalar", lambda e, n=n: e.copy(
                    out=crep[:, n * CHUNK:(n + 1) * CHUNK], in_=pBC[:]),
                    deps=[tc1] + ([last.get(("crep", n))] if last.get(("crep", n)) else []), mark=True)
                last["pBC"] = tc2
                t_rep[n] = (tb2, tc2)
            last["sBC_rd"] = t_rep[15][1]
            last["pSSM"] = t_rep[15][1]

            t_drep_rd = None
            for dt in range(16):
                t_y_last = None
                for n in range(16):
                    par = n % 3
                    hb = n % 4
                    ta = E("scalar", lambda e, dt=dt, n=n, par=par: e.activation(
                        abar[par][:], drep[:], AF.Exp, scale=a_all[:, dt * 16 + n:dt * 16 + n + 1]),
                        deps=[tdr] + ([last.get(("abar_rd", par))] if last.get(("abar_rd", par)) else []),
                        mark=True)
                    tu = E("gpsimd", lambda e, dt=dt, n=n, par=par: e.tensor_tensor(
                        ubuf[par][:], xc[:, dt * CHUNK:(dt + 1) * CHUNK],
                        dbrep[:, n * CHUNK:(n + 1) * CHUNK], ALU.mult),
                        deps=[t_xc[dt], t_rep[n][0]] +
                        ([last.get(("ubuf_rd", par))] if last.get(("ubuf_rd", par)) else []),
                        mark=True)
                    th = E("vector", lambda e, dt=dt, n=n, par=par, hb=hb: e.tensor_tensor_scan(
                        hslab[:, hb * CHUNK:(hb + 1) * CHUNK], abar[par][:], ubuf[par][:],
                        carry[:, dt * 16 + n:dt * 16 + n + 1], ALU.mult, ALU.add),
                        deps=[ta, tu], mark=True)
                    last[("abar_rd", par)] = th
                    last[("ubuf_rd", par)] = th
                    tt_ = E("vector", lambda e, dt=dt, n=n, par=par, hb=hb: e.tensor_tensor(
                        ubuf[par][:], hslab[:, hb * CHUNK:(hb + 1) * CHUNK],
                        crep[:, n * CHUNK:(n + 1) * CHUNK], ALU.mult),
                        deps=[th, t_rep[n][1]], mark=True)
                    t_y_last = E("tensor", lambda e, n=n, par=par: e.matmul(
                        pY[:], eyef[:], ubuf[par][:], start=(n == 0), stop=(n == 15)),
                        deps=[tt_] + ([last.get("pY")] if n == 0 and last.get("pY") else []),
                        mark=True)
                    last[("ubuf_rd", par)] = t_y_last
                    if n % 4 == 3 and n < 15:
                        g = n // 4
                        E("vector", lambda e, dt=dt, g=g: e.tensor_copy(
                            out=carry[:, dt * 16 + g * 4:dt * 16 + (g + 1) * 4],
                            in_=hslab[:, CHUNK - 1:4 * CHUNK:CHUNK]))
                        P.drain("vector")
                tcar = E("vector", lambda e, dt=dt: e.tensor_copy(
                    out=carry[:, dt * 16 + 12:(dt + 1) * 16],
                    in_=hslab[:, CHUNK - 1:4 * CHUNK:CHUNK]), mark=True)
                P.drain("vector")
                if dt == 15:
                    prev_scan_done = tcar
                tq1 = E("vector", lambda e, dt=dt: e.tensor_scalar_mul(
                    xcd_scr[:], xc[:, dt * CHUNK:(dt + 1) * CHUNK], dp[:, dt:dt + 1]),
                    deps=[t_xc[dt]] + ([last.get("xcd_rd")] if last.get("xcd_rd") else []), mark=True)
                tq2 = E("vector", lambda e: e.tensor_add(xcd_scr[:], xcd_scr[:], pY[:]),
                        deps=[tq1, t_y_last], mark=True)
                last["pY"] = tq2
                tq3 = E("vector", lambda e, dt=dt: e.tensor_tensor(
                    ygated[:, dt * CHUNK:(dt + 1) * CHUNK], xcd_scr[:],
                    ygated[:, dt * CHUNK:(dt + 1) * CHUNK], ALU.mult),
                    deps=[tq2, last[("sres", dt)]], mark=True)
                last["xcd_rd"] = tq3
                last[("yg", dt)] = tq3
                last[("xc", dt)] = tq1
                last["drep_rd"] = ta
            t_xr = []
            for tt in range(4):
                row0 = tok0 + tt * 128
                t_xr.append(E("sync", lambda e, tt=tt, row0=row0: e.dma_start(
                    out=xres[tt][:], in_=x_d[row0:row0 + 128, :]),
                    deps=[last.get(("xres", tt))] if last.get(("xres", tt)) else [],
                    mark=True, dma=True))
            yg_deps = [last[("yg", dtk)] for dtk in range(16)]
            for m in range(8):
                wbuf = wouts[m % 2]
                twl = E("sync", lambda e, m=m, wbuf=wbuf: e.dma_start(
                    out=wbuf[:],
                    in_=woutbf_d.rearrange("(k p) m -> p k m", p=128)[:, :, m * 128:(m + 1) * 128]),
                    deps=[t_wconv_last] + ([last.get(("wout", m % 2))] if last.get(("wout", m % 2)) else []),
                    mark=True, dma=True, stream=("wo", m % 2))
                tmm = None
                for k in range(16):
                    tmm = E("tensor", lambda e, k=k, m=m, wbuf=wbuf: e.matmul(
                        pW[:], wbuf[:, k * 128:(k + 1) * 128], ygated[:, k * CHUNK:(k + 1) * CHUNK],
                        start=(k == 0), stop=(k == 15)),
                        deps=([twl] + yg_deps + ([last.get("pW")] if last.get("pW") else [])
                              if k == 0 else []),
                        mark=(k == 15))
                last[("wout", m % 2)] = tmm
                tev = E("scalar", lambda e: e.copy(out=yout_fm[:], in_=pW[:]),
                        deps=[tmm] + ([last.get("yout_rd")] if last.get("yout_rd") else []), mark=True)
                last["pW"] = tev
                ttr = None
                for tt in range(4):
                    pf = pTRF[tt % 2]
                    ttr = E("tensor", lambda e, tt=tt, pf=pf: e.transpose(
                        pf[:], yout_fm[:, tt * 128:(tt + 1) * 128], eyef[:]),
                        deps=[tev] + ([last.get(("pTRFr", tt % 2))] if last.get(("pTRFr", tt % 2)) else []),
                        mark=True)
                    tad = E("vector", lambda e, tt=tt, m=m, pf=pf: e.tensor_tensor(
                        xres[tt][:, m * 128:(m + 1) * 128], xres[tt][:, m * 128:(m + 1) * 128],
                        pf[:], ALU.add), deps=[ttr, t_xr[tt]], mark=True)
                    last[("pTRFr", tt % 2)] = tad
                    last[("xadd", tt, m)] = tad
                last["yout_rd"] = ttr
            for tt in range(4):
                row0 = tok0 + tt * 128
                tst = E("sync", lambda e, tt=tt, row0=row0: e.dma_start(
                    out=x1_d[row0:row0 + 128, :], in_=xres[tt][:]),
                    deps=[last[("xadd", tt, m)] for m in range(8)], mark=True, dma=True)
                last[("xres", tt)] = tst
                last_stores.append(tst)





def host_prep_A(inputs):
    W_x = np.asarray(inputs["L0_attn_W_x"], np.float32)
    wx_pad = np.zeros((D_INNER, 96), np.float32)
    wx_pad[:, 0:16] = W_x[:, 1:17]
    wx_pad[:, 32:48] = W_x[:, 17:33]
    wx_pad[:, 64] = W_x[:, 0]
    convw = np.ascontiguousarray(
        np.asarray(inputs["L0_attn_conv_w"], np.float32).reshape(16, 128, 4)
        .transpose(1, 0, 2).reshape(128, 64))
    convb = np.ascontiguousarray(np.asarray(inputs["L0_attn_conv_b"], np.float32).reshape(16, 128).T)
    alog = np.ascontiguousarray(
        np.asarray(inputs["L0_attn_A_log"], np.float32).reshape(16, 128, 16)
        .transpose(1, 0, 2).reshape(128, 256))
    dp = np.ascontiguousarray(np.asarray(inputs["L0_attn_D_param"], np.float32).reshape(16, 128).T)
    oh16 = np.zeros((16, 2048), np.float32)
    for n in range(16):
        oh16[n, n * 128:(n + 1) * 128] = 1.0
    consts = np.zeros((128, 4), np.float32)
    consts[:, 0] = 1.0
    common = {
        "W_in": np.ascontiguousarray(np.asarray(inputs["L0_attn_W_in"], np.float32)),
        "convw": convw, "convb": convb, "Wx": wx_pad, "Alog": alog, "Dp": dp,
        "W_out": np.ascontiguousarray(np.asarray(inputs["L0_attn_W_out"], np.float32)),
        "eye_bf": np.eye(128, dtype=bf16),
        "eye_f": np.eye(128, dtype=np.float32),
        "ones1": np.ones((1, 128), np.float32),
        "oh16": oh16, "consts": consts,
    }
    x = np.asarray(inputs["L0_attn_x"], np.float32)
    in_maps = []
    for c in range(N_CORES):
        m = dict(common)
        m["x"] = np.ascontiguousarray(x[2 * c:2 * c + 2].reshape(T_CORE, D_MODEL))
        in_maps.append(m)
    return in_maps


def emit_stage_B(nc, P, stack, d, dbg=False):
    x1_d = d["x1"]
    wq_d = d["WqT"]
    rt_d = d["RtT"]
    lv_d = d["LV"]
    wo_d = d["WoT"]
    bias_d = d["wout_b"]
    eyebf_d = d["eye_bf"]
    eyef_d = d["eye_f"]
    onesw_d = d["onesw"]
    x2_d = d["x2"]
    wqbf_d = d["wqbf"]
    rtbf_d = d["rtbf"]
    lvbf_d = d["lvbf"]
    wobf_d = d["wobf"]
    for _k in d:
        pass
    S = stack.enter_context
    sb = lambda _n, _s, _t: nc.sbuf_tensor("B_" + _n, _s, _t)
    ps = lambda _n, _s, _t: nc.psum_tensor("B_" + _n, _s, _t)

    eyebf = S(sb("s_eyebf", [128, 128], BF16))
    eyef = S(sb("s_eyef", [128, 128], F32))
    onesw = S(sb("s_onesw", [128, 128], F32))
    bias = S(sb("s_bias", [128, 8], F32))
    wq_sb = S(sb("wq_sb", [128, 8 * 1024], BF16))
    rt_sb = S(sb("rt_sb", [128, 8 * 256], BF16))
    lv_sb = S(sb("lv_sb", [128, 2 * 1024], BF16))
    wo_sb = S(sb("wo_sb", [128, 8 * 1024], BF16))

    xt = [S(sb(f"xt{i}", [128, D_MODEL], F32)) for i in range(2)]
    ln_s1 = S(sb("ln_s1", [128, 4], F32))
    ln_s2 = S(sb("ln_s2", [128, 4], F32))
    ln_mean = S(sb("ln_mean", [128, 4], F32))
    ln_var = S(sb("ln_var", [128, 4], F32))
    ln_msq = S(sb("ln_msq", [128, 4], F32))
    ln_rstd = S(sb("ln_rstd", [128, 4], F32))
    xln = [S(sb("xln0", [128, D_MODEL], F32)), S(sb("xln1", [128, D_MODEL], F32))]
    xln_bf = [S(sb("xln_bf0", [128, D_MODEL], BF16)), S(sb("xln_bf1", [128, D_MODEL], BF16))]
    xfm = S(sb("xfm", [128, 8 * CHUNK], BF16))
    q_bf = S(sb("q_bf", [128, 8 * CHUNK], BF16))
    ret_bf = S(sb("ret_bf", [128, 8 * CHUNK], BF16))
    gfm = [S(sb(f"gfm{i}", [128, CHUNK], F32)) for i in range(2)]
    gtm = S(sb("gtm", [128, 256], F32))
    gl = S(sb("gl", [128, 256], F32))
    lp = S(sb("lp", [128, 256], F32))
    lp_bf = S(sb("lp_bf", [128, 256], BF16))
    lpfm = S(sb("lpfm", [128, 2 * CHUNK], BF16))
    youtB = S(sb("youtB", [128, 8 * CHUNK], F32))
    xres = [S(sb(f"xres{i}", [128, D_MODEL], F32)) for i in range(4)]

    pQ = S(ps("pQ", [128, 512], F32))
    pG = S(ps("pG", [128, 512], F32))
    pR = S(ps("pR", [128, 512], F32))
    pW = S(ps("pW", [128, 512], F32))
    pTR = [S(ps("pTRa", [128, 128], BF16)), S(ps("pTRb", [128, 128], BF16))]
    pTRF = [S(ps("pTRFa", [128, 128], F32)), S(ps("pTRFb", [128, 128], F32))]

    E = P.emit
    last = {}

    t_consts = []
    for dst, srcd in [(eyebf, eyebf_d), (eyef, eyef_d), (onesw, onesw_d), (bias, bias_d)]:
        t_consts.append(E("sync", lambda e, dst=dst, srcd=srcd: e.dma_start(out=dst[:], in_=srcd),
                          mark=True, dma=True))

    stage_f = youtB[:, 0:2048]
    stage_b_ = youtB[:, 2048:3072].bitcast(BF16)
    pieces = []
    for k in range(8):
        pieces.append((wq_d[k * 128:(k + 1) * 128, :], wqbf_d[k * 128:(k + 1) * 128, :], 1024))
    for k in range(8):
        pieces.append((rt_d[k * 128:(k + 1) * 128, :], rtbf_d[k * 128:(k + 1) * 128, :], 256))
    for k in range(2):
        pieces.append((lv_d[k * 128:(k + 1) * 128, :], lvbf_d[k * 128:(k + 1) * 128, :], 1024))
    for k in range(8):
        pieces.append((wo_d[k * 128:(k + 1) * 128, :], wobf_d[k * 128:(k + 1) * 128, :], 1024))
    t_conv = emit_w_convert(P, E, pieces, stage_f, stage_b_)
    t_res = []
    for dst, srcd, kk in [(wq_sb, wqbf_d, 8), (rt_sb, rtbf_d, 8), (lv_sb, lvbf_d, 2), (wo_sb, wobf_d, 8)]:
        t_res.append(E("sync", lambda e, dst=dst, srcd=srcd, kk=kk: e.dma_start(
            out=dst[:], in_=srcd.rearrange("(k p) m -> p k m", p=128)), deps=[t_conv],
            mark=True, dma=True))

    ctx = {"xt": xt, "ln_s1": ln_s1, "ln_s2": ln_s2, "ln_mean": ln_mean, "ln_var": ln_var,
           "ln_msq": ln_msq, "ln_rstd": ln_rstd, "ln_sq": xln[0], "xln": xln, "xln_bf": xln_bf,
           "xfm": xfm, "eyebf": eyebf, "eyef": eyef, "pTR": pTR, "pTRF": pTRF, "xres": xres}

    for ch in range(8):
        tok0 = ch * CHUNK
        t_xfm = emit_ln_to_fm(P, E, last, ctx, x1_d, tok0)
        t_q = []
        for m in range(8):
            tmm = None
            for k in range(8):
                tmm = E("tensor", lambda e, k=k, m=m: e.matmul(
                    pQ[:], wq_sb[:, k * 1024 + m * 128: k * 1024 + (m + 1) * 128],
                    xfm[:, k * CHUNK:(k + 1) * CHUNK], start=(k == 0), stop=(k == 7)),
                    deps=(t_xfm + t_res + ([last.get("pQ")] if last.get("pQ") else [])
                          if k == 0 else []), mark=(k == 7))
            tev = E("scalar", lambda e, m=m: e.copy(
                out=q_bf[:, m * CHUNK:(m + 1) * CHUNK], in_=pQ[:]),
                deps=[tmm] + ([last.get(("q", m))] if last.get(("q", m)) else []), mark=True)
            last["pQ"] = tev
            t_q.append(tev)
        t_g = []
        for mg in range(2):
            tmm = None
            for k in range(8):
                tmm = E("tensor", lambda e, k=k, mg=mg: e.matmul(
                    pG[:], rt_sb[:, k * 256 + mg * 128: k * 256 + (mg + 1) * 128],
                    q_bf[:, k * CHUNK:(k + 1) * CHUNK], start=(k == 0), stop=(k == 7)),
                    deps=(t_q + ([last.get("pG")] if last.get("pG") else []) if k == 0 else []),
                    mark=(k == 7))
            tev = E("scalar", lambda e, mg=mg: e.activation(
                gfm[mg][:], pG[:], AF.Sigmoid),
                deps=[tmm] + ([last.get(("g", mg))] if last.get(("g", mg)) else []), mark=True)
            last["pG"] = tev
            t_g.append(tev)
        for m in range(8):
            last[("q", m)] = t_g[-1]
        t_lp_ev = []
        for tt in range(4):
            tgt = None
            for mg in range(2):
                pf = pTRF[mg % 2]
                ttr = E("tensor", lambda e, tt=tt, mg=mg, pf=pf: e.transpose(
                    pf[:], gfm[mg][:, tt * 128:(tt + 1) * 128], eyef[:]),
                    deps=[t_g[mg]] + ([last.get(("pTRFg", mg % 2))] if last.get(("pTRFg", mg % 2)) else []),
                    mark=True)
                tgt = E("vector", lambda e, mg=mg, pf=pf: e.tensor_copy(
                    out=gtm[:, mg * 128:(mg + 1) * 128], in_=pf[:]),
                    deps=[ttr] + ([last.get("gtm_rd")] if last.get("gtm_rd") else []), mark=True)
                last[("pTRFg", mg % 2)] = tgt
            tgl = E("vector", lambda e: e.tensor_scalar(
                gl[:], gtm[:], -1.0, 1.0, ALU.mult, ALU.add), deps=[tgt], mark=True)
            P.drain("vector")
            E("vector", lambda e: e.tensor_single_scalar(
                lp[:, 0:128], onesw[:], gtm[:, 0:1], ALU.subtract), deps=t_consts)
            tl0 = E("vector", lambda e: e.tensor_scalar_mul(
                lp[:, 128:256], onesw[:], gtm[:, 0:1]), mark=True)
            tlv = tl0
            for lv_ in range(1, 8):
                n = 1 << lv_
                half = 128 >> lv_
                node0 = n - 1
                P.drain("vector")
                lpv = lp[:].rearrange("p (n two h) -> p n two h", n=n, two=2, h=half)
                E("vector", lambda e, n=n, half=half, node0=node0, lpv=lpv: e.tensor_tensor(
                    lpv[:, :, 1, :], lpv[:, :, 1, :],
                    gtm[:, node0:node0 + n].unsqueeze(2).broadcast_to((128, n, half)), ALU.mult))
                P.drain("vector")
                tlv = E("vector", lambda e, n=n, half=half, node0=node0, lpv=lpv: e.tensor_tensor(
                    lpv[:, :, 0, :], lpv[:, :, 0, :],
                    gl[:, node0:node0 + n].unsqueeze(2).broadcast_to((128, n, half)), ALU.mult),
                    mark=True)
            last["gtm_rd"] = tlv
            tlb = E("vector", lambda e: e.tensor_copy(out=lp_bf[:], in_=lp[:]), deps=[tlv], mark=True)
            for kl in range(2):
                pv = pTR[kl % 2]
                ttr = E("tensor", lambda e, kl=kl, pv=pv: e.transpose(
                    pv[:], lp_bf[:, kl * 128:(kl + 1) * 128], eyebf[:]),
                    deps=[tlb] + ([last.get(("pTRlp", kl % 2))] if last.get(("pTRlp", kl % 2)) else []),
                    mark=True)
                tev = E("scalar", lambda e, kl=kl, tt=tt, pv=pv: e.copy(
                    out=lpfm[:, kl * CHUNK + tt * 128: kl * CHUNK + (tt + 1) * 128],
                    in_=pv[:]), deps=[ttr], mark=True)
                last[("pTRlp", kl % 2)] = tev
                t_lp_ev.append(tev)
        t_ret = []
        for m in range(8):
            tmm = None
            for k in range(2):
                tmm = E("tensor", lambda e, k=k, m=m: e.matmul(
                    pR[:], lv_sb[:, k * 1024 + m * 128: k * 1024 + (m + 1) * 128],
                    lpfm[:, k * CHUNK:(k + 1) * CHUNK], start=(k == 0), stop=(k == 1)),
                    deps=(t_lp_ev + ([last.get("pR")] if last.get("pR") else [])
                          if k == 0 else []), mark=(k == 1))
            tev = E("scalar", lambda e, m=m: e.copy(
                out=ret_bf[:, m * CHUNK:(m + 1) * CHUNK], in_=pR[:]),
                deps=[tmm] + ([last.get(("ret", m))] if last.get(("ret", m)) else []), mark=True)
            last["pR"] = tev
            t_ret.append(tev)
        t_yev = []
        for m in range(8):
            tmm = None
            for k in range(8):
                tmm = E("tensor", lambda e, k=k, m=m: e.matmul(
                    pW[:], wo_sb[:, k * 1024 + m * 128: k * 1024 + (m + 1) * 128],
                    ret_bf[:, k * CHUNK:(k + 1) * CHUNK], start=(k == 0), stop=(k == 7)),
                    deps=(t_ret + ([last.get("pW")] if last.get("pW") else []) if k == 0 else []),
                    mark=(k == 7))
            tev = E("scalar", lambda e, m=m: e.activation(
                youtB[:, m * CHUNK:(m + 1) * CHUNK], pW[:], AF.Identity, bias=bias[:, m:m + 1]),
                deps=[tmm] + ([last.get(("src_fm_rd", m))] if last.get(("src_fm_rd", m)) else []),
                mark=True)
            last["pW"] = tev
            t_yev.append(tev)
        for m in range(8):
            last[("ret", m)] = t_yev[-1]
        src_views = [youtB[:, m * CHUNK:(m + 1) * CHUNK] for m in range(8)]
        emit_residual_add_store(P, E, last, ctx, x1_d, x2_d, tok0, src_views, t_yev)




def emit_stage_C(nc, P, stack, d, dbg=False):
    x2_d = d["x2"]
    bwt_d = d["BwT"]
    cwt_d = d["CwT"]
    dvec_d = d["Dvec"]
    apar_d = d["apar"]
    ang1_d = d["ang1"]
    ang2_d = d["ang2"]
    eyebf_d = d["eye_bf"]
    eyef_d = d["eye_f"]
    ones1_d = d["ones1"]
    x3_d = d["x3"]
    bwtbf_d = d["bwtbf"]
    cwtbf_d = d["cwtbf"]
    for _k in d:
        pass
    dbg_w = d.get("dbg_w")
    dbg_tre = d.get("dbg_tre")
    dbg_tim = d.get("dbg_tim")
    dbg_bu = d.get("dbg_bu")
    dbg_sre = d.get("dbg_sre")
    dbg_h = d.get("dbg_h")
    dbg_oc = d.get("dbg_oc")
    dbg_vre = d.get("dbg_vre")
    dbg_vim = d.get("dbg_vim")
    S = stack.enter_context
    sb = lambda _n, _s, _t: nc.sbuf_tensor("C_" + _n, _s, _t)
    ps = lambda _n, _s, _t: nc.psum_tensor("C_" + _n, _s, _t)

    eyebf = S(sb("s_eyebf", [128, 128], BF16))
    eyef = S(sb("s_eyef", [128, 128], F32))
    ones1 = S(sb("s_ones1", [1, 128], F32))
    apar = S(sb("s_apar", [128, 2], F32))
    womega = S(sb("womega", [128, 16], F32))
    bwt_sb = S(sb("bwt_sb", [128, 8 * 512], BF16))
    cwt_sb = S(sb("cwt_sb", [128, 4 * 1024], BF16))
    dbc = S(sb("dbc", [128, D_MODEL], F32))
    Tre = [S(sb(f"Tre{i}", [128, CHUNK], F32)) for i in range(4)]
    Tim = [S(sb(f"Tim{i}", [128, CHUNK], F32)) for i in range(4)]
    Tcre = [S(sb(f"Tcre{i}", [128, CHUNK], F32)) for i in range(4)]
    Tcim = [S(sb(f"Tcim{i}", [128, CHUNK], F32)) for i in range(4)]
    tsc1 = S(sb("tsc1", [128, 1024], F32))
    tsc2 = S(sb("tsc2", [128, 1024], F32))
    qcol = S(sb("qcol", [128, 8], F32))
    ocols = [S(sb(f"ocols{i}", [128, 6], F32)) for i in range(4)]

    xt = [S(sb(f"xt{i}", [128, D_MODEL], F32)) for i in range(2)]
    ln_s1 = S(sb("ln_s1", [128, 4], F32))
    ln_s2 = S(sb("ln_s2", [128, 4], F32))
    ln_mean = S(sb("ln_mean", [128, 4], F32))
    ln_var = S(sb("ln_var", [128, 4], F32))
    ln_msq = S(sb("ln_msq", [128, 4], F32))
    ln_rstd = S(sb("ln_rstd", [128, 4], F32))
    xln = [S(sb("xln0", [128, D_MODEL], F32)), S(sb("xln1", [128, D_MODEL], F32))]
    xln_bf = [S(sb("xln_bf0", [128, D_MODEL], BF16)), S(sb("xln_bf1", [128, D_MODEL], BF16))]
    xfm = S(sb("xfm", [128, 8 * CHUNK], BF16))
    xlntm = [S(sb(f"xlntm{i}", [128, D_MODEL], F32)) for i in range(4)]
    bu = S(sb("bu", [128, 4 * CHUNK], F32))
    vre = S(sb("vre", [128, 4 * CHUNK], F32))
    vim = S(sb("vim", [128, 4 * CHUNK], F32))
    angs = vre[:, 0:2048]
    ksin = vim[:, 0:1024]
    sre = S(sb("sre", [128, 4 * CHUNK], F32))
    sim = S(sb("sim", [128, 4 * CHUNK], F32))
    carryS = S(sb("carryS", [128, 8], F32))
    onesb = S(sb("onesb", [128, CHUNK], F32))
    hre_bf = S(sb("hre_bf", [128, 4 * CHUNK], BF16))
    youtC = S(sb("youtC", [128, 8 * CHUNK], F32))
    dscr = S(sb("dscr", [128, D_MODEL], F32))
    xres = [S(sb(f"xres{i}", [128, D_MODEL], F32)) for i in range(4)]

    pB = S(ps("pB", [128, 512], F32))
    pY = S(ps("pY", [128, 512], F32))
    pW = S(ps("pW", [128, 512], F32))
    pTR = [S(ps("pTRa", [128, 128], BF16)), S(ps("pTRb", [128, 128], BF16))]
    pTRF = [S(ps("pTRFa", [128, 128], F32)), S(ps("pTRFb", [128, 128], F32))]

    E = P.emit
    last = {}

    t_consts = []
    for dst, srcd in [(eyebf, eyebf_d), (eyef, eyef_d), (ones1, ones1_d), (apar, apar_d)]:
        t_consts.append(E("sync", lambda e, dst=dst, srcd=srcd: e.dma_start(out=dst[:], in_=srcd),
                          mark=True, dma=True))
    t_consts.append(E("sync", lambda e: e.dma_start(
        out=angs[:, 0:1024], in_=ang1_d.rearrange("(k p) m -> p k m", p=128)), mark=True, dma=True))
    t_consts.append(E("sync", lambda e: e.dma_start(
        out=angs[:, 1024:2048], in_=ang2_d.rearrange("(k p) m -> p k m", p=128)), mark=True, dma=True))

    stage_f = youtC[:, 0:2048]
    stage_bv = youtC[:, 2048:3072].bitcast(BF16)
    pieces = []
    for k in range(8):
        pieces.append((bwt_d[k * 128:(k + 1) * 128, :], bwtbf_d[k * 128:(k + 1) * 128, :], 512))
    for k in range(4):
        pieces.append((cwt_d[k * 128:(k + 1) * 128, :], cwtbf_d[k * 128:(k + 1) * 128, :], 1024))
    t_conv = emit_w_convert(P, E, pieces, stage_f, stage_bv)
    t_res = []
    for dst, srcd in [(bwt_sb, bwtbf_d), (cwt_sb, cwtbf_d)]:
        t_res.append(E("sync", lambda e, dst=dst, srcd=srcd: e.dma_start(
            out=dst[:], in_=srcd.rearrange("(k p) m -> p k m", p=128)), deps=[t_conv],
            mark=True, dma=True))
    tks2 = []
    for k in range(2):
        t1 = E("scalar", lambda e, k=k: e.activation(
            ksin[:, k * 512:(k + 1) * 512], angs[:, k * 512:(k + 1) * 512], AF.Sin),
            deps=t_consts, mark=True)
        t2 = E("scalar", lambda e, k=k: e.activation(
            tsc1[:, 0:512], angs[:, 1024 + k * 512: 1024 + (k + 1) * 512], AF.Sin),
            deps=[d for d in (tks2[-1] if tks2 else None,) if d], mark=True)
        t3 = E("vector", lambda e, k=k: e.tensor_add(
            ksin[:, k * 512:(k + 1) * 512], ksin[:, k * 512:(k + 1) * 512], tsc1[:, 0:512]),
            deps=[t1, t2], mark=True)
        t4 = E("vector", lambda e, k=k: e.tensor_scalar_mul(
            ksin[:, k * 512:(k + 1) * 512], ksin[:, k * 512:(k + 1) * 512], -1.0), mark=True)
        tks2.append(t4)
    t_om = []
    for m in range(4):
        tmm = None
        for k in range(2):
            tmm = E("tensor", lambda e, k=k, m=m: e.matmul(
                pY[:, 0:1], ksin[:, k * 512 + m * 128: k * 512 + (m + 1) * 128],
                apar[:, k:k + 1], start=(k == 0), stop=(k == 1)),
                deps=(tks2 + t_consts + ([last.get("pYom")] if last.get("pYom") else [])
                      if k == 0 else []), mark=(k == 1))
        tev = E("vector", lambda e, m=m: e.tensor_copy(out=womega[:, m:m + 1], in_=pY[:, 0:1]),
                deps=[tmm], mark=True)
        last["pYom"] = tev
        t_om.append(tev)
    P.drain("vector")
    tw1 = E("vector", lambda e: e.tensor_tensor(womega[:, 12:16], womega[:, 0:4], womega[:, 0:4], ALU.mult),
            deps=t_om, mark=True)
    P.drain("vector")
    E("vector", lambda e: e.tensor_scalar_add(womega[:, 12:16], womega[:, 12:16], 1.0))
    P.drain("vector")
    trc = E("vector", lambda e: e.reciprocal(womega[:, 12:16], womega[:, 12:16]), mark=True)
    P.drain("vector")
    E("vector", lambda e: e.tensor_scalar(tsc1[:, 0:4], womega[:, 0:4], -2.0, None, ALU.mult, ALU.bypass))
    P.drain("vector")
    E("vector", lambda e: e.tensor_tensor(womega[:, 8:12], tsc1[:, 0:4], womega[:, 12:16], ALU.mult))
    P.drain("vector")
    E("vector", lambda e: e.tensor_tensor(tsc1[:, 4:8], womega[:, 0:4], womega[:, 0:4], ALU.mult))
    P.drain("vector")
    E("vector", lambda e: e.tensor_scalar(tsc1[:, 4:8], tsc1[:, 4:8], -1.0, 1.0, ALU.mult, ALU.add))
    P.drain("vector")
    teig = E("vector", lambda e: e.tensor_tensor(womega[:, 4:8], tsc1[:, 4:8], womega[:, 12:16], ALU.mult),
             mark=True)
    P.drain("vector")
    t_tab = []
    for nt in range(4):
        tz1 = E("vector", lambda e, nt=nt: e.memset(Tre[nt][:, 0:1], 1.0), deps=[teig])
        tz2 = E("vector", lambda e, nt=nt: e.memset(Tim[nt][:, 0:1], 0.0))
        P.drain("vector")
        E("vector", lambda e, nt=nt: e.tensor_copy(out=Tre[nt][:, 1:2], in_=womega[:, 4 + nt:5 + nt]))
        E("vector", lambda e, nt=nt: e.tensor_copy(out=Tim[nt][:, 1:2], in_=womega[:, 8 + nt:9 + nt]))
        E("vector", lambda e, nt=nt: e.tensor_copy(out=qcol[:, 0:1], in_=womega[:, 4 + nt:5 + nt]))
        tq = E("vector", lambda e, nt=nt: e.tensor_copy(out=qcol[:, 1:2], in_=womega[:, 8 + nt:9 + nt]),
               mark=True)
        P.drain("vector")
        for j in range(1, 9):
            w = 1 << j
            E("vector", lambda e: e.tensor_tensor(qcol[:, 2:3], qcol[:, 0:1], qcol[:, 0:1], ALU.mult))
            E("vector", lambda e: e.tensor_tensor(qcol[:, 3:4], qcol[:, 1:2], qcol[:, 1:2], ALU.mult))
            E("vector", lambda e: e.tensor_tensor(qcol[:, 4:5], qcol[:, 0:1], qcol[:, 1:2], ALU.mult))
            P.drain("vector")
            E("vector", lambda e: e.tensor_sub(qcol[:, 0:1], qcol[:, 2:3], qcol[:, 3:4]))
            E("vector", lambda e: e.tensor_scalar_mul(qcol[:, 1:2], qcol[:, 4:5], 2.0))
            P.drain("vector")
            E("vector", lambda e, nt=nt, w=w: e.tensor_scalar_mul(tsc1[:, 0:w], Tre[nt][:, 0:w], qcol[:, 0:1]))
            E("vector", lambda e, nt=nt, w=w: e.tensor_scalar_mul(tsc2[:, 0:w], Tim[nt][:, 0:w], qcol[:, 1:2]))
            P.drain("vector")
            E("vector", lambda e, nt=nt, w=w: e.tensor_sub(Tre[nt][:, w:2 * w], tsc1[:, 0:w], tsc2[:, 0:w]))
            E("vector", lambda e, nt=nt, w=w: e.tensor_scalar_mul(tsc1[:, 0:w], Tre[nt][:, 0:w], qcol[:, 1:2]))
            E("vector", lambda e, nt=nt, w=w: e.tensor_scalar_mul(tsc2[:, 0:w], Tim[nt][:, 0:w], qcol[:, 0:1]))
            P.drain("vector")
            E("vector", lambda e, nt=nt, w=w: e.tensor_add(Tim[nt][:, w:2 * w], tsc1[:, 0:w], tsc2[:, 0:w]))
            P.drain("vector")
        P.drain("vector")
        E("vector", lambda e, nt=nt: e.tensor_scalar_mul(qcol[:, 2:3], Tre[nt][:, 511:512], 1.0))
        E("vector", lambda e, nt=nt: e.tensor_scalar_mul(qcol[:, 3:4], Tim[nt][:, 511:512], 1.0))
        P.drain("vector")
        E("vector", lambda e, nt=nt: e.tensor_tensor(qcol[:, 4:5], qcol[:, 2:3], womega[:, 4 + nt:5 + nt], ALU.mult))
        E("vector", lambda e, nt=nt: e.tensor_tensor(qcol[:, 5:6], qcol[:, 3:4], womega[:, 8 + nt:9 + nt], ALU.mult))
        E("vector", lambda e, nt=nt: e.tensor_tensor(qcol[:, 6:7], qcol[:, 2:3], womega[:, 8 + nt:9 + nt], ALU.mult))
        E("vector", lambda e, nt=nt: e.tensor_tensor(qcol[:, 7:8], qcol[:, 3:4], womega[:, 4 + nt:5 + nt], ALU.mult))
        P.drain("vector")
        E("vector", lambda e, nt=nt: e.tensor_sub(ocol[:, 2 * nt if False else 0:1], qcol[:, 4:5], qcol[:, 5:6]) if False else e.tensor_sub(tsc1[:, 0:1], qcol[:, 4:5], qcol[:, 5:6]))
        E("vector", lambda e, nt=nt: e.tensor_add(tsc1[:, 1:2], qcol[:, 6:7], qcol[:, 7:8]))
        P.drain("vector")
        E("vector", lambda e, nt=nt: e.tensor_copy(out=ocols[nt][:, 0:1], in_=tsc1[:, 0:1]))
        tt2 = E("vector", lambda e, nt=nt: e.tensor_copy(out=ocols[nt][:, 1:2], in_=tsc1[:, 1:2]), mark=True)
        P.drain("vector")
        E("vector", lambda e, nt=nt: e.tensor_tensor(qcol[:, 2:3], ocols[nt][:, 0:1], ocols[nt][:, 0:1], ALU.mult))
        E("vector", lambda e, nt=nt: e.tensor_tensor(qcol[:, 3:4], ocols[nt][:, 1:2], ocols[nt][:, 1:2], ALU.mult))
        E("vector", lambda e, nt=nt: e.tensor_tensor(qcol[:, 4:5], ocols[nt][:, 0:1], ocols[nt][:, 1:2], ALU.mult))
        P.drain("vector")
        E("vector", lambda e, nt=nt: e.tensor_sub(ocols[nt][:, 2:3], qcol[:, 2:3], qcol[:, 3:4]))
        E("vector", lambda e, nt=nt: e.tensor_scalar_mul(ocols[nt][:, 3:4], qcol[:, 4:5], 2.0))
        P.drain("vector")
        E("vector", lambda e, nt=nt: e.tensor_tensor(qcol[:, 2:3], ocols[nt][:, 2:3], ocols[nt][:, 0:1], ALU.mult))
        E("vector", lambda e, nt=nt: e.tensor_tensor(qcol[:, 3:4], ocols[nt][:, 3:4], ocols[nt][:, 1:2], ALU.mult))
        E("vector", lambda e, nt=nt: e.tensor_tensor(qcol[:, 4:5], ocols[nt][:, 2:3], ocols[nt][:, 1:2], ALU.mult))
        E("vector", lambda e, nt=nt: e.tensor_tensor(qcol[:, 5:6], ocols[nt][:, 3:4], ocols[nt][:, 0:1], ALU.mult))
        P.drain("vector")
        E("vector", lambda e, nt=nt: e.tensor_sub(ocols[nt][:, 4:5], qcol[:, 2:3], qcol[:, 3:4]))
        ttab = E("vector", lambda e, nt=nt: e.tensor_add(ocols[nt][:, 5:6], qcol[:, 4:5], qcol[:, 5:6]), mark=True)
        P.drain("vector")
        t_tab.append(ttab)
    tz = E("vector", lambda e: e.memset(onesb[:], 1.0), mark=True)
    tdl = E("sync", lambda e: e.dma_start(out=dscr[0:1, :], in_=dvec_d), mark=True, dma=True)
    tdm1 = E("tensor", lambda e: e.matmul(pB[:], ones1[:], dscr[0:1, 0:512], start=True, stop=True),
             deps=[tdl], mark=True)
    tdm1e = E("scalar", lambda e: e.copy(out=dbc[:, 0:512], in_=pB[:]), deps=[tdm1], mark=True)
    tdm2 = E("tensor", lambda e: e.matmul(pB[:], ones1[:], dscr[0:1, 512:1024], start=True, stop=True),
             deps=[tdm1e], mark=True)
    tdm2e = E("scalar", lambda e: e.copy(out=dbc[:, 512:1024], in_=pB[:]), deps=[tdm2], mark=True)
    last["pB"] = tdm2e

    ctx = {"xt": xt, "ln_s1": ln_s1, "ln_s2": ln_s2, "ln_mean": ln_mean, "ln_var": ln_var,
           "ln_msq": ln_msq, "ln_rstd": ln_rstd, "ln_sq": xln[0], "xln": xln, "xln_bf": xln_bf,
           "xfm": xfm, "eyebf": eyebf, "eyef": eyef, "pTR": pTR, "pTRF": pTRF, "xres": xres,
           "keep_tm": xlntm}

    for b in range(B_PER_CORE):
        tzc = E("vector", lambda e: e.memset(carryS[:], 0.0), mark=True)
        for cch in range(4):
            tok0 = b * SEQ + cch * CHUNK
            t_xfm = emit_ln_to_fm(P, E, last, ctx, x2_d, tok0)
            if cch == 0:
                tbl_re, tbl_im = Tre, Tim
                t_tbl = t_tab
            else:
                tbl_re, tbl_im = Tcre, Tcim
                t_tbl = []
                oc = cch - 1
                for nt in range(4):
                    d0 = [t_tab[nt]] + ([last.get(("tc", nt))] if last.get(("tc", nt)) else [])
                    E("vector", lambda e, nt=nt, oc=oc: e.tensor_scalar_mul(
                        tsc1[:, 0:512], Tre[nt][:], ocols[nt][:, 2 * oc:2 * oc + 1]), deps=d0)
                    E("vector", lambda e, nt=nt, oc=oc: e.tensor_scalar_mul(
                        tsc2[:, 0:512], Tim[nt][:], ocols[nt][:, 2 * oc + 1:2 * oc + 2]))
                    P.drain("vector")
                    E("vector", lambda e, nt=nt: e.tensor_sub(Tcre[nt][:], tsc1[:, 0:512], tsc2[:, 0:512]))
                    E("vector", lambda e, nt=nt, oc=oc: e.tensor_scalar_mul(
                        tsc1[:, 0:512], Tre[nt][:], ocols[nt][:, 2 * oc + 1:2 * oc + 2]))
                    E("vector", lambda e, nt=nt, oc=oc: e.tensor_scalar_mul(
                        tsc2[:, 0:512], Tim[nt][:], ocols[nt][:, 2 * oc:2 * oc + 1]))
                    P.drain("vector")
                    ttc = E("vector", lambda e, nt=nt: e.tensor_add(
                        Tcim[nt][:], tsc1[:, 0:512], tsc2[:, 0:512]), mark=True)
                    t_tbl.append(ttc)
            t_bu = []
            for mn in range(4):
                tmm = None
                for k in range(8):
                    tmm = E("tensor", lambda e, k=k, mn=mn: e.matmul(
                        pB[:], bwt_sb[:, k * 512 + mn * 128: k * 512 + (mn + 1) * 128],
                        xfm[:, k * CHUNK:(k + 1) * CHUNK], start=(k == 0), stop=(k == 7)),
                        deps=(t_xfm + t_res + [last.get("pB")] if k == 0 else []), mark=(k == 7))
                tev = E("scalar", lambda e, mn=mn: e.copy(
                    out=bu[:, mn * CHUNK:(mn + 1) * CHUNK], in_=pB[:]),
                    deps=[tmm] + ([last.get(("bu", mn))] if last.get(("bu", mn)) else []), mark=True)
                last["pB"] = tev
                t_bu.append(tev)
            t_v = []
            for mn in range(4):
                tv1 = E("vector", lambda e, mn=mn, T=tbl_re[mn]: e.tensor_tensor(
                    vre[:, mn * CHUNK:(mn + 1) * CHUNK], bu[:, mn * CHUNK:(mn + 1) * CHUNK],
                    T[:], ALU.mult),
                    deps=[t_bu[mn]] + t_tbl + ([last.get(("v", mn))] if last.get(("v", mn)) else []),
                    mark=True)
                tv2 = E("vector", lambda e, mn=mn, T=tbl_im[mn]: e.scalar_tensor_tensor(
                    vim[:, mn * CHUNK:(mn + 1) * CHUNK], bu[:, mn * CHUNK:(mn + 1) * CHUNK],
                    -1.0, T[:], ALU.mult, ALU.mult), mark=True)
                t_v.append(tv2)
            t_dmp = []
            if dbg and b == 0 and cch == 0:
                t_dmp.append(E("sync", lambda e: e.dma_start(out=dbg_vre, in_=vre[:]), deps=t_v, mark=True, dma=True))
                t_dmp.append(E("sync", lambda e: e.dma_start(out=dbg_vim, in_=vim[:]), deps=t_v, mark=True, dma=True))
            t_s = []
            for mn in range(4):
                ts1_ = E("vector", lambda e, mn=mn: e.tensor_tensor_scan(
                    sre[:, mn * CHUNK:(mn + 1) * CHUNK], onesb[:], vre[:, mn * CHUNK:(mn + 1) * CHUNK],
                    carryS[:, mn:mn + 1], ALU.mult, ALU.add), deps=[t_v[mn], tz, tzc] + t_dmp, mark=True)
                ts2_ = E("vector", lambda e, mn=mn: e.tensor_tensor_scan(
                    sim[:, mn * CHUNK:(mn + 1) * CHUNK], onesb[:], vim[:, mn * CHUNK:(mn + 1) * CHUNK],
                    carryS[:, 4 + mn:5 + mn], ALU.mult, ALU.add), mark=True)
                t_s.append(ts2_)
            P.drain("vector")
            tcar1 = E("vector", lambda e: e.tensor_copy(
                out=carryS[:, 0:4], in_=sre[:, CHUNK - 1:4 * CHUNK:CHUNK]), deps=t_s, mark=True)
            tcar2 = E("vector", lambda e: e.tensor_copy(
                out=carryS[:, 4:8], in_=sim[:, CHUNK - 1:4 * CHUNK:CHUNK]), mark=True)
            P.drain("vector")
            t_h = []
            for mn in range(4):
                th1 = E("vector", lambda e, mn=mn, T=tbl_re[mn]: e.tensor_tensor(
                    vre[:, mn * CHUNK:(mn + 1) * CHUNK], T[:],
                    sre[:, mn * CHUNK:(mn + 1) * CHUNK], ALU.mult), deps=[tcar1, tcar2], mark=True)
                th2 = E("vector", lambda e, mn=mn, T=tbl_im[mn]: e.scalar_tensor_tensor(
                    vim[:, mn * CHUNK:(mn + 1) * CHUNK], T[:],
                    -1.0, sim[:, mn * CHUNK:(mn + 1) * CHUNK], ALU.mult, ALU.mult), mark=True)
                th3 = E("vector", lambda e, mn=mn: e.tensor_tensor(
                    hre_bf[:, mn * CHUNK:(mn + 1) * CHUNK], vre[:, mn * CHUNK:(mn + 1) * CHUNK],
                    vim[:, mn * CHUNK:(mn + 1) * CHUNK], ALU.add),
                    deps=[th2] + ([last.get(("h", mn))] if last.get(("h", mn)) else []), mark=True)
                last[("v", mn)] = th3
                t_h.append(th3)
            t_yev = []
            for m in range(8):
                tmm = None
                for k in range(4):
                    tmm = E("tensor", lambda e, k=k, m=m: e.matmul(
                        pW[:], cwt_sb[:, k * 1024 + m * 128: k * 1024 + (m + 1) * 128],
                        hre_bf[:, k * CHUNK:(k + 1) * CHUNK], start=(k == 0), stop=(k == 3)),
                        deps=(t_h + ([last.get("pW")] if last.get("pW") else []) if k == 0 else []),
                        mark=(k == 3))
                tev = E("scalar", lambda e, m=m: e.copy(
                    out=youtC[:, m * CHUNK:(m + 1) * CHUNK], in_=pW[:]),
                    deps=[tmm] + ([last.get(("src_fm_rd", m))] if last.get(("src_fm_rd", m)) else []),
                    mark=True)
                last["pW"] = tev
                t_yev.append(tev)
            for mn in range(4):
                last[("h", mn)] = t_yev[-1]
                last[("bu", mn)] = t_v[mn]
                last[("tc", mn)] = t_h[mn]
            if dbg and b == 0 and cch == 0:
                for nap, tile in [(dbg_w, womega[:]), (dbg_tre, Tre[0][:]), (dbg_tim, Tim[0][:]),
                                  (dbg_bu, bu[:]), (dbg_sre, sre[:]), (dbg_h, hre_bf[:]),
                                  (dbg_oc, ocols[0][:])]:
                    E("sync", lambda e, nap=nap, tile=tile: e.dma_start(out=nap, in_=tile),
                      deps=t_yev, mark=True, dma=True)
            src_views = [youtC[:, m * CHUNK:(m + 1) * CHUNK] for m in range(8)]
            stores = emit_residual_add_store(P, E, last, ctx, x2_d, x3_d, tok0, src_views, t_yev,
                                             du=(xlntm, dbc, dscr))



def emit_stage_D(nc, P, stack, d, dbg=False):
    x3_d = d["x3"]
    rw_d = d["rw"]
    sgw_d = d["sgw"]
    suw_d = d["suw"]
    sdw_d = d["sdw"]
    eyebf_d = d["eye_bf"]
    eyef_d = d["eye_f"]
    ones1_d = d["ones1"]
    onescol_d = d["onescol"]
    iota64_d = d["iota64"]
    trils_d = d["trilS"]
    gidx_d = d["gidx"]
    pout_d = d["pout"]
    xnbf_d = d["xnbf"]
    counts_d = d["counts"]
    ek_d = d["ek"]
    pl_d = d["pl"]
    wt_d = d["wt"]
    mapl_d = d["mapl"]
    rwbf_d = d["rwbf"]
    sgwbf_d = d["sgwbf"]
    suwbf_d = d["suwbf"]
    sdwbf_d = d["sdwbf"]
    for _k in d:
        pass
    S = stack.enter_context
    sb = lambda _n, _s, _t: nc.sbuf_tensor("D_" + _n, _s, _t)
    ps = lambda _n, _s, _t: nc.psum_tensor("D_" + _n, _s, _t)

    eyebf = S(sb("s_eyebf", [128, 128], BF16))
    eyef = S(sb("s_eyef", [128, 128], F32))
    ones1 = S(sb("s_ones1", [1, 128], F32))
    onescol = S(sb("s_onescol", [128, 1], F32))
    iota64 = S(sb("s_iota64", [128, 64], F32))
    trils = S(sb("s_trils", [128, 128], F32))
    gidx = S(sb("s_gidx", [128, 32], F32))
    rw_sb = S(sb("rw_sb", [128, 8 * 64], F32))
    sgw_sb = S(sb("sgw_sb", [128, 8 * 512], BF16))
    suw_sb = S(sb("suw_sb", [128, 8 * 512], BF16))
    sdw_sb = S(sb("sdw_sb", [128, 4 * 1024], BF16))

    xt = [S(sb(f"xt{i}", [128, D_MODEL], F32)) for i in range(2)]
    ln_s1 = S(sb("ln_s1", [128, 4], F32))
    ln_s2 = S(sb("ln_s2", [128, 4], F32))
    ln_mean = S(sb("ln_mean", [128, 4], F32))
    ln_var = S(sb("ln_var", [128, 4], F32))
    ln_msq = S(sb("ln_msq", [128, 4], F32))
    ln_rstd = S(sb("ln_rstd", [128, 4], F32))
    xln = [S(sb("xln0", [128, D_MODEL], F32)), S(sb("xln1", [128, D_MODEL], F32))]
    xln_bf = [S(sb("xln_bf0", [128, D_MODEL], BF16)), S(sb("xln_bf1", [128, D_MODEL], BF16))]
    xfm = S(sb("xfm", [128, 8 * CHUNK], BF16))
    xfm32 = S(sb("xfm32", [128, 8 * CHUNK], F32))
    sgsil = S(sb("sgsil", [128, 4 * CHUNK], F32))
    supf = S(sb("supf", [128, 4 * CHUNK], F32))
    sgu_bf = S(sb("sgu_bf", [128, 4 * CHUNK], BF16))
    sh_fm = S(sb("sh_fm", [128, 8 * CHUNK], F32))
    xres = [S(sb(f"xres{i}", [128, D_MODEL], F32)) for i in range(4)]
    scores = S(sb("scores", [128, 64], F32))
    gs = S(sb("gs", [128, 64], F32))
    gmask = S(sb("gmask", [128, 8], F32))
    ms = S(sb("ms", [128, 64], F32))
    hits = S(sb("hits", [128, 8 * 64], F32))
    sel = S(sb("sel", [128, 64], F32))
    cumpos = S(sb("cumpos", [128, 64], F32))
    carry_sb = S(sb("carry_sb", [1, 64], F32))
    ekb = S(sb("ekb", [128, 8], F32))
    plb = S(sb("plb", [128, 8], F32))
    wb = S(sb("wb", [128, 8], F32))
    wcols = S(sb("wcols", [128, 4], F32))
    ttracc = S(sb("ttracc", [128, 1], F32))
    scr64 = S(sb("scr64", [128, 64], F32))
    c32k = S(sb("c32k", [128, 8], F32))
    validb = S(sb("validb", [128, 8], F32))
    slotf = S(sb("slotf", [128, 8], F32))
    sloti = S(sb("sloti", [128, 8], I32))
    init1e9 = S(sb("init1e9", [128, 257], F32))

    pSG = S(ps("pSG", [128, 512], F32))
    pUP = pSG
    pW = S(ps("pW", [128, 512], F32))
    pSMALL = S(ps("pSMALL", [128, 512], F32))
    pRT = pSMALL[:, 0:64]
    pPOS = pSMALL[:, 64:128]
    pCARt = S(ps("pCAR", [1, 64], F32))
    pCAR = pCARt[:]
    pTR = [S(ps("pTRa", [128, 128], BF16)), S(ps("pTRb", [128, 128], BF16))]
    pTRF = [S(ps("pTRFa", [128, 128], F32)), S(ps("pTRFb", [128, 128], F32))]

    E = P.emit
    last = {}

    t_consts = []
    for dst, srcd in [(eyebf, eyebf_d), (eyef, eyef_d), (ones1, ones1_d), (onescol, onescol_d),
                      (iota64, iota64_d), (trils, trils_d), (gidx, gidx_d)]:
        t_consts.append(E("sync", lambda e, dst=dst, srcd=srcd: e.dma_start(out=dst[:], in_=srcd),
                          mark=True, dma=True))
    tmz = E("vector", lambda e: e.memset(init1e9[:], 1.0e9), mark=True)
    tmi = E("sync", lambda e: e.dma_start(
        out=mapl_d.rearrange("(a b) c -> a (b c)", a=128), in_=init1e9[:]), deps=[tmz],
        mark=True, dma=True)
    tcz = E("vector", lambda e: e.memset(c32k[:], 32768.0), mark=True)

    stage_f = sh_fm[:, 0:2048]
    stage_bv = sh_fm[:, 2048:3072].bitcast(BF16)
    pieces = []
    for k in range(8):
        pieces.append((sgw_d[k * 128:(k + 1) * 128, :], sgwbf_d[k * 128:(k + 1) * 128, :], 512))
    for k in range(8):
        pieces.append((suw_d[k * 128:(k + 1) * 128, :], suwbf_d[k * 128:(k + 1) * 128, :], 512))
    for k in range(4):
        pieces.append((sdw_d[k * 128:(k + 1) * 128, :], sdwbf_d[k * 128:(k + 1) * 128, :], 1024))
    t_conv = emit_w_convert(P, E, pieces, stage_f, stage_bv)
    t_res = []
    t_res.append(E("sync", lambda e: e.dma_start(
        out=rw_sb[:], in_=rw_d.rearrange("(k p) m -> p k m", p=128)), mark=True, dma=True))
    for dst, srcd in [(sgw_sb, sgwbf_d), (suw_sb, suwbf_d), (sdw_sb, sdwbf_d)]:
        t_res.append(E("sync", lambda e, dst=dst, srcd=srcd: e.dma_start(
            out=dst[:], in_=srcd.rearrange("(k p) m -> p k m", p=128)), deps=[t_conv],
            mark=True, dma=True))

    ctx = {"xt": xt, "ln_s1": ln_s1, "ln_s2": ln_s2, "ln_mean": ln_mean, "ln_var": ln_var,
           "ln_msq": ln_msq, "ln_rstd": ln_rstd, "ln_sq": xln[0], "xln": xln, "xln_bf": xln_bf,
           "xfm": xfm, "eyebf": eyebf, "eyef": eyef, "pTR": pTR, "pTRF": pTRF, "xres": xres,
           "store_bf_rows": xnbf_d, "xfm32": xfm32}

    t_car_last = None
    D_ = P.drain
    for ch in range(8):
        tok0 = ch * CHUNK
        t_xfm = emit_ln_to_fm(P, E, last, ctx, x3_d, tok0)
        t_sg = []
        for m in range(4):
            tmm = None
            for k in range(8):
                tmm = E("tensor", lambda e, k=k, m=m: e.matmul(
                    pSG[:], sgw_sb[:, k * 512 + m * 128: k * 512 + (m + 1) * 128],
                    xfm[:, k * CHUNK:(k + 1) * CHUNK], start=(k == 0), stop=(k == 7)),
                    deps=(t_xfm + t_res + [t for t in (last.get("pSG"), last.get("pUP")) if t]
                          if k == 0 else []), mark=(k == 7))
            tev = E("scalar", lambda e, m=m: e.activation(
                sgsil[:, m * CHUNK:(m + 1) * CHUNK], pSG[:], AF.Silu),
                deps=[tmm] + ([last.get(("sg", m))] if last.get(("sg", m)) else []), mark=True)
            last["pSG"] = tev
            last["pUP"] = tev
            t_sg.append(tev)
        t_up = []
        for m in range(4):
            tmm = None
            for k in range(8):
                tmm = E("tensor", lambda e, k=k, m=m: e.matmul(
                    pUP[:], suw_sb[:, k * 512 + m * 128: k * 512 + (m + 1) * 128],
                    xfm[:, k * CHUNK:(k + 1) * CHUNK], start=(k == 0), stop=(k == 7)),
                    deps=([last.get("pUP"), last.get("pSG")] if k == 0 else []), mark=(k == 7))
            tev = E("scalar", lambda e, m=m: e.copy(
                out=supf[:, m * CHUNK:(m + 1) * CHUNK], in_=pUP[:]),
                deps=[tmm] + ([last.get(("up", m))] if last.get(("up", m)) else []), mark=True)
            last["pUP"] = tev
            last["pSG"] = tev
            t_up.append(tev)
        t_gu = []
        for m in range(4):
            tgu = E("vector", lambda e, m=m: e.tensor_tensor(
                sgu_bf[:, m * CHUNK:(m + 1) * CHUNK], sgsil[:, m * CHUNK:(m + 1) * CHUNK],
                supf[:, m * CHUNK:(m + 1) * CHUNK], ALU.mult),
                deps=[t_sg[m], t_up[m]] + ([last.get(("gu", m))] if last.get(("gu", m)) else []),
                mark=True)
            t_gu.append(tgu)
        for m in range(4):
            last[("sg", m)] = t_gu[m]
            last[("up", m)] = t_gu[m]
        t_yev = []
        for m in range(8):
            tmm = None
            for k in range(4):
                tmm = E("tensor", lambda e, k=k, m=m: e.matmul(
                    pW[:], sdw_sb[:, k * 1024 + m * 128: k * 1024 + (m + 1) * 128],
                    sgu_bf[:, k * CHUNK:(k + 1) * CHUNK], start=(k == 0), stop=(k == 3)),
                    deps=(t_gu + ([last.get("pW")] if last.get("pW") else []) if k == 0 else []),
                    mark=(k == 3))
            tev = E("scalar", lambda e, m=m: e.copy(
                out=sh_fm[:, m * CHUNK:(m + 1) * CHUNK], in_=pW[:]),
                deps=[tmm] + ([last.get(("src_fm_rd", m))] if last.get(("src_fm_rd", m)) else []),
                mark=True)
            last["pW"] = tev
            t_yev.append(tev)
        for m in range(4):
            last[("gu", m)] = t_yev[-1]

        for tt in range(4):
            tile = ch * 4 + tt
            tmm = None
            for k in range(8):
                tmm = E("tensor", lambda e, k=k, tt=tt: e.matmul(
                    pRT, xfm32[:, k * CHUNK + tt * 128: k * CHUNK + (tt + 1) * 128],
                    rw_sb[:, k * 64:(k + 1) * 64], start=(k == 0), stop=(k == 7)),
                    deps=(t_xfm + ([last.get("pRT")] if last.get("pRT") else []) if k == 0 else []),
                    mark=(k == 7))
            tsc = E("scalar", lambda e: e.activation(scores[:], pRT, AF.Sigmoid),
                    deps=[tmm] + ([last.get("scores_rd")] if last.get("scores_rd") else []),
                    mark=True)
            last["pRT"] = tsc
            D_("vector")
            E("vector", lambda e: e.tensor_tensor(
                gs[:, 0:32].rearrange("p (g x) -> p g x", g=8),
                scores[:].rearrange("p (g x) -> p g x", g=8)[:, :, 0:4],
                scores[:].rearrange("p (g x) -> p g x", g=8)[:, :, 4:8], ALU.max), deps=[tsc])
            D_("vector")
            E("vector", lambda e: e.tensor_tensor(
                gs[:, 32:48].rearrange("p (g x) -> p g x", g=8),
                gs[:, 0:32].rearrange("p (g x) -> p g x", g=8)[:, :, 0:2],
                gs[:, 0:32].rearrange("p (g x) -> p g x", g=8)[:, :, 2:4], ALU.max))
            D_("vector")
            E("vector", lambda e: e.tensor_tensor(
                gs[:, 48:56].rearrange("p (g x) -> p g x", g=8),
                gs[:, 32:48].rearrange("p (g x) -> p g x", g=8)[:, :, 0:1],
                gs[:, 32:48].rearrange("p (g x) -> p g x", g=8)[:, :, 1:2], ALU.max))
            D_("vector")
            E("vector", lambda e: e.memset(gmask[:], 0.0))
            D_("vector")
            for it in range(4):
                E("vector", lambda e: e.reduce_max(gs[:, 56:57], gs[:, 48:56], axis=AX.X))
                D_("vector")
                E("vector", lambda e: e.tensor_single_scalar(gs[:, 32:40], gs[:, 48:56],
                                                             gs[:, 56:57], ALU.is_ge))
                D_("vector")
                E("vector", lambda e: e.tensor_add(gmask[:], gmask[:], gs[:, 32:40]))
                E("vector", lambda e: e.scalar_tensor_tensor(
                    gs[:, 48:56], gs[:, 32:40], -1.0e9, gs[:, 48:56], ALU.mult, ALU.add))
                D_("vector")
            tms = E("vector", lambda e: e.tensor_tensor(
                ms[:].rearrange("p (g x) -> p g x", g=8),
                scores[:].rearrange("p (g x) -> p g x", g=8),
                gmask[:].unsqueeze(2).broadcast_to((128, 8, 8)), ALU.mult), mark=True)
            last["scores_rd"] = tms
            D_("vector")
            tselz = E("vector", lambda e: e.memset(sel[:], 0.0), mark=True)
            D_("vector")
            for kk in range(8):
                E("vector", lambda e: e.reduce_max(gs[:, 56:57], ms[:], axis=AX.X))
                D_("vector")
                E("vector", lambda e, kk=kk: e.tensor_single_scalar(
                    hits[:, kk * 64:(kk + 1) * 64], ms[:], gs[:, 56:57], ALU.is_ge))
                D_("vector")
                E("vector", lambda e, kk=kk: e.tensor_copy(out=wb[:, kk:kk + 1], in_=gs[:, 56:57]))
                E("vector", lambda e, kk=kk: e.tensor_add(
                    sel[:], sel[:], hits[:, kk * 64:(kk + 1) * 64]))
                E("vector", lambda e, kk=kk: e.scalar_tensor_tensor(
                    ms[:], hits[:, kk * 64:(kk + 1) * 64], -1.0e9, ms[:], ALU.mult, ALU.add))
                D_("vector")
                E("vector", lambda e, kk=kk: e.tensor_tensor(
                    scr64[:], hits[:, kk * 64:(kk + 1) * 64], iota64[:], ALU.mult), deps=t_consts)
                D_("vector")
                E("vector", lambda e, kk=kk: e.reduce_sum(ekb[:, kk:kk + 1], scr64[:], axis=AX.X))
                D_("vector")
            tsel = E("vector", lambda e: e.tensor_copy(out=scr64[:], in_=sel[:]), mark=True)
            if tile == 0:
                tce = E("vector", lambda e: e.memset(carry_sb[:], 0.0), mark=True)
            else:
                tce = E("scalar", lambda e: e.copy(out=carry_sb[:], in_=pCAR),
                        deps=([t_car_last] if t_car_last else []) +
                             ([last.get("carry_rd")] if last.get("carry_rd") else []), mark=True)
            tp1 = E("tensor", lambda e: e.matmul(pPOS, trils[:], sel[:], start=True, stop=False,
                                                 skip_group_check=True),
                    deps=[tsel, tce] + t_consts + ([last.get("pPOS")] if last.get("pPOS") else []),
                    mark=True)
            tp2 = E("tensor", lambda e: e.matmul(pPOS, ones1[:], carry_sb[:], start=False,
                                                 stop=True, skip_group_check=True),
                    deps=[tce], mark=True)
            tcp = E("scalar", lambda e: e.copy(out=cumpos[:], in_=pPOS),
                    deps=[tp2] + ([last.get("cumpos_rd")] if last.get("cumpos_rd") else []),
                    mark=True)
            last["pPOS"] = tcp
            t_car_last = E("tensor", lambda e, tile=tile: e.matmul(
                pCAR, onescol[:], sel[:], start=(tile == 0), stop=(tile == 31),
                skip_group_check=True),
                deps=[tsel, tcp] + t_consts, mark=True)
            last["carry_rd"] = t_car_last
            for kk in range(8):
                E("vector", lambda e, kk=kk: e.tensor_tensor(
                    scr64[:], hits[:, kk * 64:(kk + 1) * 64], cumpos[:], ALU.mult), deps=[tcp])
                D_("vector")
                E("vector", lambda e, kk=kk: e.reduce_sum(plb[:, kk:kk + 1], scr64[:], axis=AX.X))
                D_("vector")
            E("vector", lambda e: e.reduce_sum(wcols[:, 0:1], wb[:], axis=AX.X))
            D_("vector")
            E("vector", lambda e: e.tensor_scalar_max(wcols[:, 1:2], wcols[:, 0:1], 1e-9))
            D_("vector")
            E("vector", lambda e: e.reciprocal(wcols[:, 1:2], wcols[:, 1:2]))
            D_("vector")
            twt = E("vector", lambda e: e.tensor_scalar(
                wb[:], wb[:], wcols[:, 1:2], 2.5, ALU.mult, ALU.mult), mark=True)
            D_("vector")
            tval = E("vector", lambda e: e.tensor_single_scalar(validb[:], plb[:], 512.0, ALU.is_lt),
                     mark=True)
            D_("vector")
            E("vector", lambda e: e.tensor_scalar(slotf[:], ekb[:], 512.0, None, ALU.mult, ALU.bypass))
            D_("vector")
            E("vector", lambda e: e.tensor_add(slotf[:], slotf[:], plb[:]))
            D_("vector")
            E("vector", lambda e: e.tensor_scalar_add(slotf[:], slotf[:], -32768.0))
            D_("vector")
            E("vector", lambda e: e.tensor_tensor(slotf[:], slotf[:], validb[:], ALU.mult), deps=[tval])
            D_("vector")
            E("vector", lambda e: e.tensor_scalar_add(slotf[:], slotf[:], 32768.0))
            D_("vector")
            tsl = E("vector", lambda e: e.tensor_copy(out=sloti[:], in_=slotf[:]), mark=True)
            tscat = None
            for kk in range(8):
                tscat = E("gpsimd", lambda e, kk=kk, tile=tile: e.indirect_dma_start(
                    out=mapl_d, out_offset=IndirectOffsetOnAxis(sloti[:, kk:kk + 1], 0),
                    in_=gidx[:, tile:tile + 1], in_offset=None),
                    deps=[tsl, tmi] + t_consts, mark=True, dma=True)
            last["sloti_rd"] = tscat
            row0 = tile * 128
            tse = E("sync", lambda e, row0=row0: e.dma_start(out=ek_d[row0:row0 + 128, :], in_=ekb[:]),
                    deps=[t_car_last], mark=True, dma=True)
            tsp = E("sync", lambda e, row0=row0: e.dma_start(out=pl_d[row0:row0 + 128, :], in_=plb[:]),
                    deps=[tval], mark=True, dma=True)
            tsw = E("sync", lambda e, row0=row0: e.dma_start(out=wt_d[row0:row0 + 128, :], in_=wb[:]),
                    deps=[twt], mark=True, dma=True)
            last["scores_rd"] = tse
            last["cumpos_rd"] = tsp
            last["wb_guard"] = tsw
            P.wait("vector", tse)
            P.wait("vector", tsp)
            P.wait("vector", tsw)
            P.wait("vector", tscat)
        src_views = [sh_fm[:, m * CHUNK:(m + 1) * CHUNK] for m in range(8)]
        emit_residual_add_store(P, E, last, ctx, x3_d, pout_d, tok0, src_views, t_yev)

    tcc = E("scalar", lambda e: e.copy(out=carry_sb[:], in_=pCAR), deps=[t_car_last], mark=True)
    E("sync", lambda e: e.dma_start(out=counts_d, in_=carry_sb[:]), deps=[tcc], mark=True, dma=True)




def build_stage_E():
    nc = bass.Bass("TRN2", target_bir_lowering=False, debug=False)
    stack = contextlib.ExitStack()

    cnt_my_d = nc.dram_tensor("cnt_my", [8, 8], F32, kind="ExternalInput").ap()
    cnt_all_d = nc.dram_tensor("cnt_all", [8, 64], F32, kind="ExternalInput").ap()
    premask_d = nc.dram_tensor("premask", [8, 1], F32, kind="ExternalInput").ap()
    lowincl_d = nc.dram_tensor("lowincl", [8, 8], F32, kind="ExternalInput").ap()
    mapl_all_d = nc.dram_tensor("mapl_all", [8 * 32896, 1], F32, kind="ExternalInput").ap()
    xall_d = nc.dram_tensor("xall", [32769, D_MODEL], BF16, kind="ExternalInput").ap()
    gw_d = nc.dram_tensor("gw", [8 * D_MODEL, 512], F32, kind="ExternalInput").ap()
    uw_d = nc.dram_tensor("uw", [8 * D_MODEL, 512], F32, kind="ExternalInput").ap()
    dw_d = nc.dram_tensor("dw", [8 * 512, D_MODEL], F32, kind="ExternalInput").ap()
    ek_d = nc.dram_tensor("ek", [T_CORE, 8], F32, kind="ExternalInput").ap()
    pl_d = nc.dram_tensor("pl", [T_CORE, 8], F32, kind="ExternalInput").ap()
    wt_d = nc.dram_tensor("wt", [T_CORE, 8], F32, kind="ExternalInput").ap()
    eyebf_d = nc.dram_tensor("eye_bf", [128, 128], BF16, kind="ExternalInput").ap()
    ones1_d = nc.dram_tensor("ones1", [1, 128], F32, kind="ExternalInput").ap()
    ppcol_d = nc.dram_tensor("ppcol", [128, 4], F32, kind="ExternalInput").ap()
    geb512_d = nc.dram_tensor("geb512", [128, 8], F32, kind="ExternalInput").ap()
    iota64_d = nc.dram_tensor("iota64", [128, 64], F32, kind="ExternalInput").ap()
    offtab_d = nc.dram_tensor("offtab", [64, 1], F32, kind="Internal").ap()
    wout_d = nc.dram_tensor("wout", [T_CORE, D_MODEL], BF16, kind="ExternalOutput").ap()
    slotid_d = nc.dram_tensor("slotid", [T_CORE, 8], F32, kind="ExternalOutput").ap()
    wv_d = nc.dram_tensor("wv", [T_CORE, 8], F32, kind="ExternalOutput").ap()

    S = stack.enter_context
    sb = nc.sbuf_tensor
    ps = nc.psum_tensor

    eyebf = S(sb("s_eyebf", [128, 128], BF16))
    ones1 = S(sb("s_ones1", [1, 128], F32))
    ppcol = S(sb("s_ppcol", [128, 4], F32))
    geb512 = S(sb("s_geb512", [128, 8], F32))
    iota64 = S(sb("s_iota64", [128, 64], F32))
    cnt_my = S(sb("sb_cnt_my", [8, 8], F32))
    cnt_all = S(sb("sb_cnt_all", [8, 64], F32))
    premask = S(sb("sb_premask", [8, 1], F32))
    lowincl = S(sb("sb_lowincl", [8, 8], F32))
    cum_sb = S(sb("cum_sb", [8, 8], F32))
    cumb = S(sb("cumb", [128, 7 * 8], F32))
    countb = S(sb("countb", [128, 7 * 8], F32))
    offb64 = S(sb("offb64", [128, 64], F32))
    offrow = S(sb("offrow", [1, 64], F32))
    rowflat = S(sb("rowflat", [1, 112], F32))
    hitk = S(sb("hitk", [128, 8], F32))
    offs = S(sb("offs", [128, 8], F32))
    ksum = S(sb("ksum", [128, 8], F32))
    scr8 = S(sb("scr8", [128, 8], F32))
    gcols = S(sb("gcols", [128, 4 * 8], F32))
    gci = S(sb("gci", [128, 4 * 8], I32))
    mvf = S(sb("mvf", [128, 1], F32))
    gwe = [S(sb(f"gwe{i}", [128, 8 * 512], BF16)) for i in range(2)]
    uwe = [S(sb(f"uwe{i}", [128, 8 * 512], BF16)) for i in range(2)]
    dwe = [S(sb(f"dwe{i}", [128, 8 * 512], BF16)) for i in range(2)]
    disp_tm = [S(sb(f"disp_tm{i}", [128, D_MODEL], BF16)) for i in range(2)]
    disp_fm = S(sb("disp_fm", [128, 8 * 512], BF16))
    gsil = S(sb("gsil", [128, 4 * 512], F32))
    upf = S(sb("upf", [128, 4 * 512], F32))
    gu_bf = S(sb("gu_bf", [128, 4 * 512], BF16))
    wout_fm = S(sb("wout_fm", [128, 8 * 512], BF16))
    wout_tm = S(sb("wout_tm", [128, D_MODEL], BF16))
    ekb = S(sb("ekb", [128, 8], F32))
    ekbi = S(sb("ekbi", [128, 8], I32))
    offg = S(sb("offg", [128, 8], F32))
    plb = S(sb("plb", [128, 8], F32))
    wb = S(sb("wb", [128, 8], F32))
    ohk = S(sb("ohk", [128, 64], F32))
    scr64 = S(sb("scr64", [128, 64], F32))
    offcol = S(sb("offcol", [128, 1], F32))
    posg = S(sb("posg", [128, 8], F32))
    validb = S(sb("validb", [128, 8], F32))
    slotf = S(sb("slotf", [128, 8], F32))
    wstg = [S(sb(f"wstg{i}", [128, 8 * 512], F32)) for i in range(3)]

    pA = S(ps("pA", [128, 512], F32))
    pB2 = S(ps("pB2", [128, 512], F32))
    pC8 = S(ps("pC8", [8, 64], F32))
    pR1 = S(ps("pR1", [1, 64], F32))
    pBC = S(ps("pBC", [128, 64], F32))
    pTR = S(ps("pTR", [128, 128], BF16))

    P = Prog(nc, stack)
    E = P.emit
    last = {}
    D_ = P.drain

    t_consts = []
    for dst, srcd in [(eyebf, eyebf_d), (ones1, ones1_d), (ppcol, ppcol_d), (geb512, geb512_d),
                      (iota64, iota64_d), (cnt_my, cnt_my_d), (cnt_all, cnt_all_d),
                      (premask, premask_d), (lowincl, lowincl_d)]:
        t_consts.append(E("sync", lambda e, dst=dst, srcd=srcd: e.dma_start(out=dst[:], in_=srcd),
                          mark=True, dma=True))


    tq1 = E("tensor", lambda e: e.matmul(pC8[:, 0:8], lowincl[:], cnt_my[:], start=True, stop=True),
            deps=t_consts, mark=True)
    tq2 = E("scalar", lambda e: e.copy(out=cum_sb[:], in_=pC8[:, 0:8]), deps=[tq1], mark=True)
    tq3 = E("tensor", lambda e: e.matmul(pR1[:], premask[:], cnt_all[:], start=True, stop=True),
            deps=t_consts, mark=True)
    tq4 = E("scalar", lambda e: e.copy(out=offrow[:], in_=pR1[:]), deps=[tq3], mark=True)
    trow1 = E("sync", lambda e: e.dma_start(out=rowflat[0:1, 0:56], in_=cum_sb[0:7, :]),
              deps=[tq2], mark=True, dma=True)
    trow2 = E("sync", lambda e: e.dma_start(out=rowflat[0:1, 56:112], in_=cnt_my[0:7, :]),
              deps=t_consts, mark=True, dma=True)
    tb_toks = []
    tb1 = E("tensor", lambda e: e.matmul(pBC[:, 0:56], ones1[:], rowflat[0:1, 0:56],
                                         start=True, stop=True), deps=[trow1, trow2], mark=True)
    tb2 = E("scalar", lambda e: e.copy(out=cumb[:], in_=pBC[:, 0:56]), deps=[tb1], mark=True)
    tb3 = E("tensor", lambda e: e.matmul(pBC[:, 0:56], ones1[:], rowflat[0:1, 56:112],
                                         start=True, stop=True), deps=[tb2], mark=True)
    tb4 = E("scalar", lambda e: e.copy(out=countb[:], in_=pBC[:, 0:56]), deps=[tb3], mark=True)
    tb_toks.append(tb4)
    tb5 = E("tensor", lambda e: e.matmul(pBC[:], ones1[:], offrow[:], start=True, stop=True),
            deps=[tq4, tb4], mark=True)
    tb6 = E("scalar", lambda e: e.copy(out=offb64[:], in_=pBC[:]), deps=[tb5], mark=True)
    t_offtab = E("sync", lambda e: e.dma_start(out=offtab_d[0:64, 0:1], in_=offrow[0:1, 0:64]),
                 deps=[tq4], mark=True, dma=True, stream="sot")

    t_gci = []
    for jp in range(4):
        tz1 = E("vector", lambda e: e.memset(offs[:], 0.0), deps=tb_toks, mark=True)
        E("vector", lambda e: e.memset(ksum[:], 0.0))
        D_("vector")
        for k in range(7):
            E("vector", lambda e, k=k, jp=jp: e.tensor_single_scalar(
                hitk[:], cumb[:, k * 8:(k + 1) * 8], ppcol[:, jp:jp + 1], ALU.is_le))
            D_("vector")
            E("vector", lambda e, k=k: e.tensor_add(ksum[:], ksum[:], hitk[:]))
            E("vector", lambda e, k=k: e.tensor_tensor(scr8[:], hitk[:], countb[:, k * 8:(k + 1) * 8],
                                                       ALU.mult))
            D_("vector")
            E("vector", lambda e: e.tensor_add(offs[:], offs[:], scr8[:]))
            D_("vector")
        E("vector", lambda e, jp=jp: e.tensor_scalar(
            offs[:], offs[:], -1.0, ppcol[:, jp:jp + 1], ALU.mult, ALU.add))
        D_("vector")
        E("vector", lambda e: e.tensor_scalar(ksum[:], ksum[:], 32896.0, None, ALU.mult, ALU.bypass))
        D_("vector")
        E("vector", lambda e: e.tensor_add(ksum[:], ksum[:], geb512[:]))
        D_("vector")
        tgc = E("vector", lambda e, jp=jp: e.tensor_add(gcols[:, jp * 8:(jp + 1) * 8], ksum[:], offs[:]),
                mark=True)
        t_gci.append(tgc)
    D_("vector")
    tgi = E("vector", lambda e: e.tensor_copy(out=gci[:], in_=gcols[:]), deps=t_gci, mark=True)
    D_("vector")
    t_gidx = []
    for jp in range(4):
        for el in range(8):
            tg = E("gpsimd", lambda e, jp=jp, el=el: e.indirect_dma_start(
                out=mvf[:], out_offset=None, in_=mapl_all_d,
                in_offset=IndirectOffsetOnAxis(gci[:, jp * 8 + el:jp * 8 + el + 1], 0)),
                deps=[tgi] + ([last.get("mvf")] if last.get("mvf") else []), mark=True, dma=True)
            tmn = E("vector", lambda e, jp=jp, el=el: e.tensor_scalar_min(
                gcols[:, jp * 8 + el:jp * 8 + el + 1], mvf[:], 32768.0), deps=[tg], mark=True)
            last["mvf"] = tmn
            t_gidx.append(tmn)
    D_("vector")
    tgi2 = E("vector", lambda e: e.tensor_copy(out=gci[:], in_=gcols[:]), deps=t_gidx, mark=True)

    def emit_slot_tile(tile):
        row0 = tile * 128
        tl1 = E("sync", lambda e, row0=row0: e.dma_start(out=ekb[:], in_=ek_d[row0:row0 + 128, :]),
                deps=[last.get("ekb_rd")] if last.get("ekb_rd") else [], mark=True, dma=True,
                stream="sl1")
        tl2 = E("sync", lambda e, row0=row0: e.dma_start(out=plb[:], in_=pl_d[row0:row0 + 128, :]),
                mark=True, dma=True, stream="sl2")
        tl3 = E("sync", lambda e, row0=row0: e.dma_start(out=wb[:], in_=wt_d[row0:row0 + 128, :]),
                mark=True, dma=True, stream="sl3")
        tki = E("vector", lambda e: e.tensor_copy(out=ekbi[:], in_=ekb[:]),
                deps=[tl1] + ([last.get("ekbi_rd")] if last.get("ekbi_rd") else []), mark=True)
        D_("vector")
        tgo = []
        for kk in range(8):
            tgo.append(E("gpsimd", lambda e, kk=kk: e.indirect_dma_start(
                out=offg[:, kk:kk + 1], out_offset=None, in_=offtab_d,
                in_offset=IndirectOffsetOnAxis(ekbi[:, kk:kk + 1], 0)),
                deps=[tki, t_offtab] + ([last.get("offg_rd")] if last.get("offg_rd") else []),
                mark=True, dma=True, stream=("sg", kk % 2)))
        tpg = E("vector", lambda e: e.tensor_tensor(posg[:], plb[:], offg[:], ALU.add),
                deps=tgo + [tl2], mark=True)
        last["ekbi_rd"] = tgo[-1]
        last["offg_rd"] = tpg
        D_("vector")
        tvd = E("vector", lambda e: e.tensor_single_scalar(validb[:], posg[:], 512.0, ALU.is_lt),
                mark=True)
        D_("vector")
        E("vector", lambda e: e.tensor_scalar(slotf[:], ekb[:], 512.0, None, ALU.mult, ALU.bypass))
        D_("vector")
        E("vector", lambda e: e.tensor_add(slotf[:], slotf[:], posg[:]))
        D_("vector")
        E("vector", lambda e: e.tensor_scalar_add(slotf[:], slotf[:], -32768.0))
        D_("vector")
        E("vector", lambda e: e.tensor_tensor(slotf[:], slotf[:], validb[:], ALU.mult))
        D_("vector")
        tsl = E("vector", lambda e: e.tensor_scalar_add(slotf[:], slotf[:], 32768.0), mark=True)
        D_("vector")
        twv = E("vector", lambda e: e.tensor_tensor(wb[:], wb[:], validb[:], ALU.mult),
                deps=[tl3, tvd], mark=True)
        ts1 = E("sync", lambda e, row0=row0: e.dma_start(out=slotid_d[row0:row0 + 128, :], in_=slotf[:]),
                deps=[tsl], mark=True, dma=True, stream="ss1")
        ts2 = E("sync", lambda e, row0=row0: e.dma_start(out=wv_d[row0:row0 + 128, :], in_=wb[:]),
                deps=[twv], mark=True, dma=True, stream="ss2")
        last["ekb_rd"] = ts2
        P.wait("vector", ts1)
        P.wait("vector", ts2)

    for el in range(8):
        wslot = el % 2
        twl1 = E("sync", lambda e, el=el: e.dma_start(
            out=wstg[0][:],
            in_=gw_d[el * 1024:(el + 1) * 1024, :].rearrange("(k p) m -> p k m", p=128)),
            deps=([last.get("wstg_g")] if last.get("wstg_g") else []),
            mark=True, dma=True, stream="egw")
        tw1 = E("vector", lambda e, el=el: e.tensor_copy(out=gwe[el % 2][:], in_=wstg[0][:]),
                deps=[twl1] + ([last.get(("gwe", el % 2))] if last.get(("gwe", el % 2)) else []),
                mark=True)
        last["wstg_g"] = tw1
        twl2 = E("sync", lambda e, el=el: e.dma_start(
            out=wstg[1][:],
            in_=uw_d[el * 1024:(el + 1) * 1024, :].rearrange("(k p) m -> p k m", p=128)),
            deps=([last.get("wstg_u")] if last.get("wstg_u") else []),
            mark=True, dma=True, stream="euw")
        tw2 = E("vector", lambda e, el=el: e.tensor_copy(out=uwe[el % 2][:], in_=wstg[1][:]),
                deps=[twl2] + ([last.get(("uwe", el % 2))] if last.get(("uwe", el % 2)) else []),
                mark=True)
        last["wstg_u"] = tw2
        twl3 = E("sync", lambda e, el=el: e.dma_start(
            out=wstg[2][:],
            in_=dw_d[el * 512:(el + 1) * 512, :].rearrange("(k p) m -> p k m", p=128)),
            deps=([last.get("wstg_d")] if last.get("wstg_d") else []),
            mark=True, dma=True, stream="edw")
        tw3 = E("vector", lambda e, el=el: e.tensor_copy(out=dwe[el % 2][:], in_=wstg[2][:]),
                deps=[twl3] + ([last.get(("dwe", el % 2))] if last.get(("dwe", el % 2)) else []),
                mark=True)
        last["wstg_d"] = tw3
        t_disp = []
        for jp in range(4):
            tg = E("gpsimd", lambda e, jp=jp, el=el: e.indirect_dma_start(
                out=disp_tm[jp % 2][:], out_offset=None, in_=xall_d,
                in_offset=IndirectOffsetOnAxis(gci[:, jp * 8 + el:jp * 8 + el + 1], 0)),
                deps=[tgi2] + ([last.get(("disp_tm", jp % 2))] if last.get(("disp_tm", jp % 2)) else []),
                mark=True, dma=True, stream=("dsp", jp % 2))
            ttr = None
            for kd in range(8):
                ttr = E("tensor", lambda e, kd=kd, jp=jp: e.transpose(
                    pTR[:], disp_tm[jp % 2][:, kd * 128:(kd + 1) * 128], eyebf[:]),
                    deps=[tg] + ([last.get("pTRe")] if last.get("pTRe") else []), mark=True)
                tev = E("scalar", lambda e, kd=kd, jp=jp: e.copy(
                    out=disp_fm[:, kd * 512 + jp * 128: kd * 512 + (jp + 1) * 128],
                    in_=pTR[:]), deps=[ttr], mark=True)
                last["pTRe"] = tev
                t_disp.append(tev)
            last[("disp_tm", jp % 2)] = ttr
        t_ge = []
        for m in range(4):
            tmm = None
            for k in range(8):
                tmm = E("tensor", lambda e, k=k, m=m, el=el: e.matmul(
                    pA[:], gwe[el % 2][:, k * 512 + m * 128: k * 512 + (m + 1) * 128],
                    disp_fm[:, k * 512:(k + 1) * 512], start=(k == 0), stop=(k == 7)),
                    deps=([tw1] + t_disp + ([last.get("pA")] if last.get("pA") else [])
                          if k == 0 else []), mark=(k == 7))
            tev = E("scalar", lambda e, m=m: e.activation(
                gsil[:, m * 512:(m + 1) * 512], pA[:], AF.Silu),
                deps=[tmm] + ([last.get(("gsil", m))] if last.get(("gsil", m)) else []),
                mark=True)
            last["pA"] = tev
            t_ge.append(tev)
        t_ue = []
        for m in range(4):
            tmm = None
            for k in range(8):
                tmm = E("tensor", lambda e, k=k, m=m, el=el: e.matmul(
                    pB2[:], uwe[el % 2][:, k * 512 + m * 128: k * 512 + (m + 1) * 128],
                    disp_fm[:, k * 512:(k + 1) * 512], start=(k == 0), stop=(k == 7)),
                    deps=([tw2] + ([last.get("pB2")] if last.get("pB2") else []) if k == 0 else []),
                    mark=(k == 7))
            tev = E("scalar", lambda e, m=m: e.copy(
                out=upf[:, m * 512:(m + 1) * 512], in_=pB2[:]),
                deps=[tmm] + ([last.get(("upf", m))] if last.get(("upf", m)) else []),
                mark=True)
            last["pB2"] = tev
            t_ue.append(tev)
        t_gue = []
        for m in range(4):
            tgu = E("vector", lambda e, m=m: e.tensor_tensor(
                gu_bf[:, m * 512:(m + 1) * 512], gsil[:, m * 512:(m + 1) * 512],
                upf[:, m * 512:(m + 1) * 512], ALU.mult),
                deps=[t_ge[m], t_ue[m]] + ([last.get(("gue", m))] if last.get(("gue", m)) else []),
                mark=True)
            t_gue.append(tgu)
        for m in range(4):
            last[("gsil", m)] = t_gue[m]
            last[("upf", m)] = t_gue[m]
        last[("gwe", el % 2)] = t_ge[-1]
        last[("uwe", el % 2)] = t_ue[-1]
        t_we = []
        for md in range(8):
            tmm = None
            for k in range(4):
                tmm = E("tensor", lambda e, k=k, md=md, el=el: e.matmul(
                    pA[:], dwe[el % 2][:, k * 1024 + md * 128: k * 1024 + (md + 1) * 128],
                    gu_bf[:, k * 512:(k + 1) * 512], start=(k == 0), stop=(k == 3)),
                    deps=([tw3] + t_gue + ([last.get("pA")] if last.get("pA") else [])
                          if k == 0 else []), mark=(k == 3))
            tev = E("scalar", lambda e, md=md: e.copy(
                out=wout_fm[:, md * 512:(md + 1) * 512], in_=pA[:]),
                deps=[tmm] + ([last.get("wfm_all")] if last.get("wfm_all") else []), mark=True)
            last["pA"] = tev
            t_we.append(tev)
        for m in range(4):
            last[("gue", m)] = t_we[-1]
        last[("dwe", el % 2)] = t_we[-1]
        for jp in range(4):
            t_evs = []
            for md in range(8):
                ttr = E("tensor", lambda e, md=md, jp=jp: e.transpose(
                    pTR[:], wout_fm[:, md * 512 + jp * 128: md * 512 + (jp + 1) * 128], eyebf[:]),
                    deps=[t_we[md]] + ([last.get("pTRe")] if last.get("pTRe") else []), mark=True)
                tev = E("scalar", lambda e, md=md: e.copy(
                    out=wout_tm[:, md * 128:(md + 1) * 128], in_=pTR[:]),
                    deps=[ttr] + ([last.get("wtm_rd")] if last.get("wtm_rd") else []), mark=True)
                last["pTRe"] = tev
                t_evs.append(tev)
                if md == 7:
                    last[("wfm", jp)] = ttr
            tst = E("sync", lambda e, jp=jp, el=el: e.dma_start(
                out=wout_d[el * 512 + jp * 128: el * 512 + (jp + 1) * 128, :],
                in_=wout_tm[:]), deps=t_evs, mark=True, dma=True)
            last["wtm_rd"] = tst
        last["wfm_all"] = last["pTRe"]
        for tile in range(el * 4, (el + 1) * 4):
            emit_slot_tile(tile)

    P.final_waits()

    with nc.Block() as block:
        P.finalize(block)
    stack.close()
    return nc


def host_prep_E(inputs, counts_list, mapl_list, xnbf_list, ek_list, pl_list, wt_list):
    counts_all = np.concatenate(counts_list, axis=0)
    mapl_all = np.concatenate(mapl_list, axis=0)
    zero_row = np.zeros((1, D_MODEL), bf16)
    xall = np.concatenate([np.asarray(x) for x in xnbf_list] + [zero_row], axis=0)
    lowincl = (np.arange(8)[:, None] <= np.arange(8)[None, :]).astype(np.float32)
    iota64 = np.broadcast_to(np.arange(64, dtype=np.float32), (128, 64)).copy()
    ppcol = np.zeros((128, 4), np.float32)
    for jp in range(4):
        ppcol[:, jp] = jp * 128 + np.arange(128)
    gw = np.asarray(inputs["L1_ffn_gate_w"], np.float32)
    uw = np.asarray(inputs["L1_ffn_up_w"], np.float32)
    dw = np.asarray(inputs["L1_ffn_down_w"], np.float32)
    in_maps = []
    for c in range(N_CORES):
        m = {
            "cnt_my": np.ascontiguousarray(counts_all[:, c * 8:(c + 1) * 8]),
            "cnt_all": counts_all,
            "premask": (np.arange(8)[:, None] < c).astype(np.float32),
            "lowincl": lowincl,
            "mapl_all": mapl_all,
            "xall": xall,
            "gw": np.ascontiguousarray(gw[c * 8:(c + 1) * 8].reshape(8 * D_MODEL, 512)),
            "uw": np.ascontiguousarray(uw[c * 8:(c + 1) * 8].reshape(8 * D_MODEL, 512)),
            "dw": np.ascontiguousarray(dw[c * 8:(c + 1) * 8].reshape(8 * 512, D_MODEL)),
            "ek": np.asarray(ek_list[c]), "pl": np.asarray(pl_list[c]), "wt": np.asarray(wt_list[c]),
            "eye_bf": np.eye(128, dtype=bf16),
            "ones1": np.ones((1, 128), np.float32),
            "ppcol": ppcol,
            "geb512": np.broadcast_to(((c * 8 + np.arange(8)) * 512).astype(np.float32), (128, 8)).copy(),
            "iota64": iota64,
        }
        in_maps.append(m)
    return in_maps


def run_stage_E(inputs, stage_d_results):
    nc = build_stage_E()
    in_maps = host_prep_E(
        inputs,
        [np.asarray(r["counts"]) for r in stage_d_results],
        [np.asarray(r["mapl"]) for r in stage_d_results],
        [np.asarray(r["xnbf"]) for r in stage_d_results],
        [np.asarray(r["ek"]) for r in stage_d_results],
        [np.asarray(r["pl"]) for r in stage_d_results],
        [np.asarray(r["wt"]) for r in stage_d_results])
    res = run_bass_kernel_spmd(nc, in_maps, core_ids=list(range(N_CORES)))
    return res.results



def build_stage_F():
    nc = bass.Bass("TRN2", target_bir_lowering=False, debug=False)
    stack = contextlib.ExitStack()

    woutall_d = nc.dram_tensor("woutall", [8 * T_CORE 
+ 1, D_MODEL], BF16, kind="ExternalInput").ap()
    slotid_d = nc.dram_tensor("slotid", [T_CORE, 8], F32, kind="ExternalInput").ap()
    wv_d = nc.dram_tensor("wv", [T_CORE, 8], F32, kind="ExternalInput").ap()
    pout_d = nc.dram_tensor("pout", [T_CORE, D_MODEL], F32, kind="ExternalInput").ap()
    out_d = nc.dram_tensor("out", [T_CORE, D_MODEL], F32, kind="ExternalOutput").ap()

    S = stack.enter_context
    sb = nc.sbuf_tensor

    slotb = S(sb("slotb", [128, 8], F32))
    sloti = S(sb("sloti", [128, 8], I32))
    wvb = S(sb("wvb", [128, 8], F32))
    gk = [S(sb(f"gk{i}", [128, D_MODEL], BF16)) for i in range(8)]
    gkf = [S(sb(f"gkf{i}", [128, D_MODEL], F32)) for i in range(8)]
    acc = S(sb("acc", [128, D_MODEL], F32))
    pob = S(sb("pob", [128, D_MODEL], F32))

    P = Prog(nc, stack)
    E = P.emit
    last = {}
    D_ = P.drain

    for tile in range(32):
        row0 = tile * 128
        tl1 = E("sync", lambda e, row0=row0: e.dma_start(out=slotb[:], in_=slotid_d[row0:row0 + 128, :]),
                deps=[last.get("slot_rd")] if last.get("slot_rd") else [], mark=True, dma=True,
                stream="fsl")
        tl2 = E("sync", lambda e, row0=row0: e.dma_start(out=wvb[:], in_=wv_d[row0:row0 + 128, :]),
                deps=[last.get("wv_rd")] if last.get("wv_rd") else [], mark=True, dma=True,
                stream="fwv")
        tl3 = E("sync", lambda e, row0=row0: e.dma_start(out=pob[:], in_=pout_d[row0:row0 + 128, :]),
                deps=[last.get("pob_rd")] if last.get("pob_rd") else [], mark=True, dma=True,
                stream="fpo")
        tsi = E("vector", lambda e: e.tensor_copy(out=sloti[:], in_=slotb[:]), deps=[tl1], mark=True)
        D_("vector")
        tacc = None
        for kk in range(8):
            tg = E("gpsimd", lambda e, kk=kk: e.indirect_dma_start(
                out=gk[kk][:], out_offset=None, in_=woutall_d,
                in_offset=IndirectOffsetOnAxis(sloti[:, kk:kk + 1], 0)),
                deps=[tsi] + ([last.get(("gk", kk))] if last.get(("gk", kk)) else []),
                mark=True, dma=True, stream=("gk", kk))
            tcv = E("scalar", lambda e, kk=kk: e.copy(out=gkf[kk][:], in_=gk[kk][:]),
                    deps=[tg] + ([last.get(("gkf", kk))] if last.get(("gkf", kk)) else []),
                    mark=True)
            last[("gk", kk)] = tcv
            if kk == 0:
                tacc = E("vector", lambda e, kk=kk: e.tensor_scalar_mul(
                    acc[:], gkf[kk][:], wvb[:, kk:kk + 1]), deps=[tl2, tcv], mark=True)
            else:
                tacc = E("vector", lambda e, kk=kk: e.scalar_tensor_tensor(
                    acc[:], gkf[kk][:], wvb[:, kk:kk + 1], acc[:], ALU.mult, ALU.add),
                    deps=[tcv], mark=True)
            last[("gkf", kk)] = tacc
        tfin = E("vector", lambda e: e.tensor_tensor(acc[:], acc[:], pob[:], ALU.add),
                 deps=[tl3, tacc], mark=True)
        last["pob_rd"] = tfin
        tst = E("sync", lambda e, row0=row0: e.dma_start(out=out_d[row0:row0 + 128, :], in_=acc[:]),
                deps=[tfin], mark=True, dma=True, stream="fst")
        last["slot_rd"] = tst
        last["wv_rd"] = tst
        P.wait("vector", tst)

    P.final_waits()

    with nc.Block() as block:
        P.finalize(block)
    stack.close()
    return nc


def run_stage_F(inputs, stage_e_results, pout_list):
    nc = build_stage_F()
    zero_row = np.zeros((1, D_MODEL), bf16)
    woutall = np.concatenate([np.asarray(r["wout"]) for r in stage_e_results] + [zero_row], axis=0)
    in_maps = []
    for c in range(N_CORES):
        in_maps.append({
            "woutall": woutall,
            "slotid": np.asarray(stage_e_results[c]["slotid"]),
            "wv": np.asarray(stage_e_results[c]["wv"]),
            "pout": np.asarray(pout_list[c]),
        })
    res = run_bass_kernel_spmd(nc, in_maps, core_ids=list(range(N_CORES)))
    return [np.asarray(r["out"]) for r in res.results]




def build_fused_ABCD():
    nc = bass.Bass("TRN2", target_bir_lowering=False, debug=False)
    master = contextlib.ExitStack()
    F = F32; B_ = BF16
    d = {}
    def din(name, shape, dt=F32):
        d[name] = nc.dram_tensor(name, shape, dt, kind="ExternalInput").ap()
    def dout(name, shape, dt=F32):
        d[name] = nc.dram_tensor(name, shape, dt, kind="ExternalOutput").ap()
    def dint(name, shape, dt=F32):
        d[name] = nc.dram_tensor(name, shape, dt, kind="Internal").ap()
    din("x", [T_CORE, D_MODEL]); din("W_in", [D_MODEL, 2 * D_INNER])
    din("convw", [128, 64]); din("convb", [128, 16]); din("Wx", [D_INNER, 96])
    din("Alog", [128, 256]); din("Dp", [128, 16]); din("W_out", [D_INNER, D_MODEL])
    din("eye_bf", [128, 128], B_); din("eye_f", [128, 128]); din("ones1", [1, 128])
    din("oh16", [16, 2048]); din("consts", [128, 4])
    din("WqT", [D_MODEL, D_MODEL]); din("RtT", [D_MODEL, 256]); din("LV", [256, D_MODEL])
    din("WoT", [D_MODEL, D_MODEL]); din("wout_b", [128, 8]); din("onesw", [128, 128])
    din("BwT", [D_MODEL, 512]); din("CwT", [512, D_MODEL]); din("Dvec", [1, D_MODEL])
    din("apar", [128, 2]); din("ang1", [256, 512]); din("ang2", [256, 512])
    din("rw", [D_MODEL, 64]); din("sgw", [D_MODEL, 512]); din("suw", [D_MODEL, 512])
    din("sdw", [512, D_MODEL]); din("onescol", [128, 1]); din("iota64", [128, 64])
    din("trilS", [128, 128]); din("gidx", [128, 32])
    dout("pout", [T_CORE, D_MODEL]); dout("xnbf", [T_CORE, D_MODEL], B_)
    dout("counts", [1, 64]); dout("ek", [T_CORE, 8]); dout("pl", [T_CORE, 8])
    dout("wt", [T_CORE, 8]); dout("mapl", [32896, 1])
    dint("x1", [T_CORE, D_MODEL]); dint("x2", [T_CORE, D_MODEL]); dint("x3", [T_CORE, D_MODEL])
    dint("winbf", [D_MODEL, 2 * D_INNER], B_); dint("woutbf", [D_INNER, D_MODEL], B_)
    dint("wqbf", [D_MODEL, D_MODEL], B_); dint("rtbf", [D_MODEL, 256], B_)
    dint("lvbf", [256, D_MODEL], B_); dint("wobf", [D_MODEL, D_MODEL], B_)
    dint("bwtbf", [D_MODEL, 512], B_); dint("cwtbf", [512, D_MODEL], B_)
    dint("rwbf", [D_MODEL, 64], B_); dint("sgwbf", [D_MODEL, 512], B_)
    dint("suwbf", [D_MODEL, 512], B_); dint("sdwbf", [512, D_MODEL], B_)
    P = Prog(nc, master)
    for emit_fn in (emit_stage_A, emit_stage_B, emit_stage_C, emit_stage_D):
        sub = contextlib.ExitStack()
        emit_fn(nc, P, sub, d)
        P.barrier()
        sub.close()
    P.final_waits()
    with nc.Block() as block:
        P.finalize(block)
    master.close()
    return nc


def host_prep_ABCD(inputs):
    a_maps = host_prep_A(inputs)
    rt = np.asarray(inputs["L0_ffn_routers"], np.float32)
    rt_pad = np.zeros((D_MODEL, 256), np.float32)
    rt_pad[:, 0:255] = rt.T
    j = np.arange(256, dtype=np.int64)[:, None]
    m = np.arange(512, dtype=np.int64)[None, :]
    def ang(r):
        rm = np.mod(r, 512)
        rm = np.where(rm >= 256, rm - 512, rm)
        return (rm.astype(np.float64) * (2 * np.pi / 512)).astype(np.float32)
    ang1 = ang((j + 1) * m)
    ang2 = ang(j * m)
    ang2[0, :] = 0.0
    apar = np.zeros((128, 2), np.float32)
    apar[:, 0] = np.asarray(inputs["L1_attn_a_params"], np.float32)[0:128]
    apar[:, 1] = np.asarray(inputs["L1_attn_a_params"], np.float32)[128:256]
    iota64 = np.broadcast_to(np.arange(64, dtype=np.float32), (128, 64)).copy()
    trils = (np.arange(128)[:, None] < np.arange(128)[None, :]).astype(np.float32)
    common = {
        "WqT": np.ascontiguousarray(np.asarray(inputs["L0_ffn_Wq"], np.float32).T),
        "RtT": rt_pad,
        "LV": np.ascontiguousarray(np.asarray(inputs["L0_ffn_leaf_values"], np.float32)),
        "WoT": np.ascontiguousarray(np.asarray(inputs["L0_ffn_Wout_w"], np.float32).T),
        "wout_b": np.ascontiguousarray(np.asarray(inputs["L0_ffn_Wout_b"], np.float32).reshape(8, 128).T),
        "onesw": np.ones((128, 128), np.float32),
        "BwT": np.ascontiguousarray(np.asarray(inputs["L1_attn_B_w"], np.float32).T),
        "CwT": np.ascontiguousarray(np.asarray(inputs["L1_attn_C_w"], np.float32).T),
        "Dvec": np.ascontiguousarray(np.asarray(inputs["L1_attn_D"], np.float32).reshape(1, D_MODEL)),
        "apar": apar, "ang1": ang1, "ang2": ang2,
        "rw": np.ascontiguousarray(np.asarray(inputs["L1_ffn_router_w"], np.float32)),
        "sgw": np.ascontiguousarray(np.asarray(inputs["L1_ffn_shared_gate_w"], np.float32)),
        "suw": np.ascontiguousarray(np.asarray(inputs["L1_ffn_shared_up_w"], np.float32)),
        "sdw": np.ascontiguousarray(np.asarray(inputs["L1_ffn_shared_down_w"], np.float32)),
        "onescol": np.ones((128, 1), np.float32),
        "iota64": iota64, "trilS": trils,
    }
    in_maps = []
    for c in range(N_CORES):
        mm = dict(a_maps[c])
        mm.update(common)
        g = np.zeros((128, 32), np.float32)
        for tile in range(32):
            g[:, tile] = c * T_CORE + tile * 128 + np.arange(128)
        mm["gidx"] = g
        in_maps.append(mm)
    return in_maps


def run_fused_ABCD(inputs):
    nc = build_fused_ABCD()
    in_maps = host_prep_ABCD(inputs)
    res = run_bass_kernel_spmd(nc, in_maps, core_ids=list(range(N_CORES)))
    return res.results


def kernel(**inputs):
    rd = run_fused_ABCD(inputs)
    re_ = run_stage_E(inputs, rd)
    out_shards = run_stage_F(inputs, re_, [np.asarray(r["pout"]) for r in rd])
    out = np.stack([s.reshape(B_PER_CORE, SEQ, D_MODEL) for s in out_shards])
    return np.ascontiguousarray(out.reshape(16, SEQ, D_MODEL))


# version 18
# speedup vs baseline: 1.0269x; 1.0269x over previous
import contextlib
import numpy as np
import ml_dtypes
import concourse.bass as bass
import concourse.mybir as mybir
from concourse.bass import IndirectOffsetOnAxis
from concourse.bass_utils import run_bass_kernel_spmd

F32 = mybir.dt.float32
BF16 = mybir.dt.bfloat16
I32 = mybir.dt.int32
AF = mybir.ActivationFunctionType
ALU = mybir.AluOpType
AX = mybir.AxisListType

N_CORES = 8
B_PER_CORE = 2
SEQ = 2048
T_CORE = B_PER_CORE * SEQ
D_MODEL = 1024
D_INNER = 2048
CHUNK = 512

bf16 = ml_dtypes.bfloat16



ENGINES = ("sync", "scalar", "vector", "tensor", "gpsimd")


class Tok:
    __slots__ = ("sem", "val", "dma", "key")

    def __init__(self, sem, val, dma=False, key=None):
        self.sem = sem
        self.val = val
        self.dma = dma
        self.key = key


class Prog:

    def __init__(self, nc, stack):
        self.nc = nc
        self.stack = stack
        self.queues = {e: [] for e in ENGINES}
        self.sems = {e: stack.enter_context(nc.semaphore(f"s_{e}")) for e in ENGINES}
        self.counts = {e: 0 for e in ENGINES}
        self.streams = {}
        self.waited = {e: {} for e in ENGINES}

    def _stream(self, key):
        st = self.streams.get(key)
        if st is None:
            sem = self.stack.enter_context(self.nc.semaphore(f"d_{len(self.streams)}"))
            st = [sem, 0, None]
            self.streams[key] = st
        return st

    def drain(self, eng):
        self.queues[eng].append(lambda e: e.drain())

    def wait(self, eng, tok):
        if tok is None:
            return
        seen = self.waited[eng]
        sid = id(tok.sem)
        if seen.get(sid, 0) >= tok.val:
            return
        seen[sid] = tok.val
        sem = tok.sem
        val = tok.val
        self.queues[eng].append(lambda e: e.wait_ge(sem, val))

    def emit(self, eng, fn, deps=(), mark=False, dma=False, stream=None):
        flat = []
        for d in deps:
            if isinstance(d, (list, tuple)):
                flat.extend(x for x in d if x is not None)
            elif d is not None:
                flat.append(d)
        deps = flat
        if dma:
            key = stream if stream is not None else ("q", eng)
            st = self._stream(key)
            if st[2] is not None:
                self.wait(eng, st[2])
            for d in deps:
                self.wait(eng, d)
            st[1] += 16
            sem, val = st[0], st[1]

            def wrapped(e, fn=fn, sem=sem):
                return fn(e).then_inc(sem, 16)

            self.queues[eng].append(wrapped)
            tok = Tok(sem, val, True, key)
            st[2] = tok
            return tok
        for d in deps:
            self.wait(eng, d)
        if mark:
            self.counts[eng] += 1
            sem = self.sems[eng]

            def wrapped2(e, fn=fn, sem=sem):
                return fn(e).then_inc(sem, 1)

            self.queues[eng].append(wrapped2)
            return Tok(sem, self.counts[eng], False, eng)
        self.queues[eng].append(fn)
        return None

    def barrier(self):
        for e in ("vector", "scalar", "gpsimd"):
            if self.queues[e]:
                self.drain(e)
        snapshot_counts = dict(self.counts)
        snapshot_streams = {k: st[1] for k, st in self.streams.items()}
        for e2 in ENGINES:
            q = self.queues[e2]
            for e1 in ENGINES:
                if e1 == e2 or snapshot_counts[e1] == 0:
                    continue
                sem = self.sems[e1]
                val = snapshot_counts[e1]
                if self.waited[e2].get(id(sem), 0) < val:
                    self.waited[e2][id(sem)] = val
                    q.append(lambda e, sem=sem, val=val: e.wait_ge(sem, val))
            for key, val in snapshot_streams.items():
                sem = self.streams[key][0]
                if val and self.waited[e2].get(id(sem), 0) < val:
                    self.waited[e2][id(sem)] = val
                    q.append(lambda e, sem=sem, val=val: e.wait_ge(sem, val))

    def final_waits(self, eng="sync"):
        def body(e):
            for key, st in self.streams.items():
                e.wait_ge(st[0], st[1])
            for en in ENGINES:
                if self.counts[en] and en != eng:
                    e.wait_ge(self.sems[en], self.counts[en])
        self.queues[eng].append(body)

    def finalize(self, block):
        for eng_name in ENGINES:
            q = self.queues[eng_name]
            if not q:
                continue
            section = getattr(block, eng_name)

            def make(q):
                def body(e):
                    for fn in q:
                        fn(e)
                return body
            section(make(q))



def emit_ln_to_fm(P, E, last, ctx, x_d, tok0, n_tiles=4):
    xt, ln_s1, ln_s2, ln_mean, ln_var, ln_msq, ln_rstd, ln_sq, xlnL, xln_bfL, xfm, eyebf, pTR = (
        ctx["xt"], ctx["ln_s1"], ctx["ln_s2"], ctx["ln_mean"], ctx["ln_var"], ctx["ln_msq"],
        ctx["ln_rstd"], ctx["ln_sq"], ctx["xln"], ctx["xln_bf"], ctx["xfm"], ctx["eyebf"], ctx["pTR"])
    if not isinstance(xlnL, list):
        xlnL = [xlnL, xlnL]
    if not isinstance(xln_bfL, list):
        xln_bfL = [xln_bfL, xln_bfL]
    AXX = AX.X
    t_xfm = []
    extra_dep = ctx.get("ln_extra_dep")
    for tt in range(n_tiles):
        row0 = tok0 + tt * 128
        buf = xt[tt % 2]
        xln = xlnL[tt % 2]
        xln_bf = xln_bfL[tt % 2]
        lnp = tt % 2
        tl = E("sync", lambda e, buf=buf, row0=row0: e.dma_start(
            out=buf[:], in_=x_d[row0:row0 + 128, :]),
            deps=[last.get(("xt", tt % 2))] if last.get(("xt", tt % 2)) else [],
            mark=True, dma=True, stream=("xt", tt % 2))
        ts1 = E("scalar", lambda e, buf=buf, tt=tt, xln=xln: e.activation(
            xln[:], buf[:], AF.Square, accum_out=ln_s2[:, tt:tt + 1]),
            deps=[tl] + [d for d in (last.get(("lnuse", tt)), last.get(("xln_rd", lnp)), extra_dep) if d],
            mark=True)
        ts2 = E("vector", lambda e, buf=buf, tt=tt: e.reduce_sum(
            ln_s1[:, tt:tt + 1], buf[:], axis=AXX), deps=[tl], mark=True)
        P.drain("vector")
        E("vector", lambda e, tt=tt: e.tensor_scalar_mul(
            ln_mean[:, tt:tt + 1], ln_s1[:, tt:tt + 1], 1.0 / D_MODEL), deps=[ts2])
        E("vector", lambda e, tt=tt: e.tensor_scalar_mul(
            ln_var[:, tt:tt + 1], ln_s2[:, tt:tt + 1], 1.0 / D_MODEL), deps=[ts1])
        P.drain("vector")
        E("vector", lambda e, tt=tt: e.tensor_tensor(
            ln_msq[:, tt:tt + 1], ln_mean[:, tt:tt + 1], ln_mean[:, tt:tt + 1], ALU.mult))
        P.drain("vector")
        E("vector", lambda e, tt=tt: e.tensor_sub(
            ln_var[:, tt:tt + 1], ln_var[:, tt:tt + 1], ln_msq[:, tt:tt + 1]))
        P.drain("vector")
        tve = E("vector", lambda e, tt=tt: e.tensor_scalar_add(
            ln_var[:, tt:tt + 1], ln_var[:, tt:tt + 1], 1e-5), mark=True)
        tsq = E("scalar", lambda e, tt=tt: e.activation(
            ln_rstd[:, tt:tt + 1], ln_var[:, tt:tt + 1], AF.Sqrt), deps=[tve], mark=True)
        trc = E("vector", lambda e, tt=tt: e.reciprocal(
            ln_rstd[:, tt:tt + 1], ln_rstd[:, tt:tt + 1]), deps=[tsq], mark=True)
        P.drain("vector")
        txl = E("vector", lambda e, buf=buf, tt=tt, xln=xln: e.tensor_scalar(
            xln[:], buf[:], ln_mean[:, tt:tt + 1], ln_rstd[:, tt:tt + 1],
            ALU.subtract, ALU.mult),
            deps=[trc] + ([last.get(("xlnrd", lnp))] if last.get(("xlnrd", lnp)) else []), mark=True)
        last[("lnuse", tt)] = txl
        txb = E("vector", lambda e, xln=xln, xln_bf=xln_bf: e.tensor_copy(out=xln_bf[:], in_=xln[:]),
                deps=[txl] + ([last.get(("xbf_store", lnp))] if last.get(("xbf_store", lnp)) else []),
                mark=True)
        last[("xln_rd", lnp)] = txb
        if ctx.get("store_bf_rows") is not None:
            dstd = ctx["store_bf_rows"]
            last[("xbf_store", lnp)] = E("sync", lambda e, dstd=dstd, row0=row0, xln_bf=xln_bf:
                e.dma_start(out=dstd[row0:row0 + 128, :], in_=xln_bf[:]), deps=[txb], mark=True,
                dma=True, stream=("xbf", lnp))
        if ctx.get("keep_tm") is not None:
            keep = ctx["keep_tm"][tt]
            tkp = E("vector", lambda e, keep=keep, xln=xln: e.tensor_copy(out=keep[:], in_=xln[:]),
                    deps=[d for d in (last.get(("keep_rd", tt)),) if d], mark=True)
            last[("xln_rd", lnp)] = tkp
        if ctx.get("dbg_xln_hook"):
            ctx["dbg_xln_hook"](tt, txb)
        tev = None
        ttr = None
        xfm32 = ctx.get("xfm32")
        pTRv = pTR if isinstance(pTR, list) else [pTR, pTR]
        pTRFv = ctx["pTRF"] if isinstance(ctx.get("pTRF"), list) else [ctx.get("pTRF")] * 2
        for kd in range(8):
            pv = pTRv[kd % 2]
            ttr = E("tensor", lambda e, kd=kd, pv=pv, xln_bf=xln_bf: e.transpose(
                pv[:], xln_bf[:, kd * 128:(kd + 1) * 128], eyebf[:]),
                deps=[txb] + [d for d in (ctx.get("pe_ready"), last.get(("pTRp", kd % 2))) if d],
                mark=True)
            tev = E("scalar", lambda e, kd=kd, tt=tt, pv=pv: e.copy(
                out=xfm[:, kd * CHUNK + tt * 128: kd * CHUNK + (tt + 1) * 128],
                in_=pv[:]), deps=[ttr], mark=True)
            last[("pTRp", kd % 2)] = tev
            t_xfm.append(tev)
            if xfm32 is not None:
                pf = pTRFv[kd % 2]
                ttr2 = E("tensor", lambda e, kd=kd, pf=pf, xln=xln: e.transpose(
                    pf[:], xln[:, kd * 128:(kd + 1) * 128], ctx["eyef"][:]),
                    deps=[txb] + [d for d in (last.get(("pTRFp", kd % 2)),) if d], mark=True)
                tev2 = E("scalar", lambda e, kd=kd, tt=tt, pf=pf: e.copy(
                    out=xfm32[:, kd * CHUNK + tt * 128: kd * CHUNK + (tt + 1) * 128],
                    in_=pf[:]), deps=[ttr2], mark=True)
                last[("pTRFp", kd % 2)] = tev2
                t_xfm.append(tev2)
                ttr = ttr2
        last[("xlnrd", lnp)] = ttr
        last[("xt", tt % 2)] = ttr
    return t_xfm


def emit_residual_add_store(P, E, last, ctx, x_d, out_d, tok0, src_fm, src_deps,
                            n_m=8, du=None):
    xres, eyef, pTRF = ctx["xres"], ctx["eyef"], ctx["pTRF"]
    t_xr = []
    for tt in range(4):
        row0 = tok0 + tt * 128
        t_xr.append(E("sync", lambda e, tt=tt, row0=row0: e.dma_start(
            out=xres[tt][:], in_=x_d[row0:row0 + 128, :]),
            deps=[last.get(("xres", tt))] if last.get(("xres", tt)) else [],
            mark=True, dma=True, stream=("xr", tt)))
    pTRFv = pTRF if isinstance(pTRF, list) else [pTRF, pTRF]
    for m in range(n_m):
        view = src_fm[m]
        ttr = None
        for tt in range(4):
            pf = pTRFv[tt % 2]
            ttr = E("tensor", lambda e, tt=tt, view=view, pf=pf: e.transpose(
                pf[:], view[:, tt * 128:(tt + 1) * 128], eyef[:]),
                deps=[src_deps[m]] + ([last.get(("pTRFr", tt % 2))] if last.get(("pTRFr", tt % 2)) else []),
                mark=True)
            tad = E("vector", lambda e, tt=tt, m=m, pf=pf: e.tensor_tensor(
                xres[tt][:, m * 128:(m + 1) * 128], xres[tt][:, m * 128:(m + 1) * 128],
                pf[:], ALU.add), deps=[ttr, t_xr[tt]], mark=True)
            last[("pTRFr", tt % 2)] = tad
            last[("xadd", tt, m)] = tad
        last[("src_fm_rd", m)] = ttr
    if du is not None:
        keep_list, dbc_t, scr = du
        for tt in range(4):
            tdu1 = E("vector", lambda e, tt=tt: e.tensor_tensor(
                scr[:], keep_list[tt][:], dbc_t[:], ALU.mult),
                deps=[last[("xadd", tt, n_m - 1)]], mark=True)
            tdu2 = E("vector", lambda e, tt=tt: e.tensor_tensor(
                xres[tt][:], xres[tt][:], scr[:], ALU.add), mark=True)
            last[("xadd", tt, n_m - 1)] = tdu2
            last[("keep_rd", tt)] = tdu2
    stores = []
    for tt in range(4):
        row0 = tok0 + tt * 128
        tst = E("sync", lambda e, tt=tt, row0=row0: e.dma_start(
            out=out_d[row0:row0 + 128, :], in_=xres[tt][:]),
            deps=[last[("xadd", tt, m)] for m in range(n_m)], mark=True, dma=True,
            stream=("xr", tt))
        last[("xres", tt)] = tst
        stores.append(tst)
    return stores


def emit_w_convert(P, E, pieces, stage_f, stage_b):
    half = stage_f.shape[1] // 2
    halfb = stage_b.shape[1] // 2
    prev_tc = {0: None, 1: None}
    prev_st = {0: None, 1: None}
    sts = []
    for i, (src, dst, width) in enumerate(pieces):
        par = i % 2
        assert width <= half and width <= halfb
        fview = stage_f[:, par * half: par * half + width]
        bview = stage_b[:, par * halfb: par * halfb + width]
        tl = E("sync", lambda e, src=src, fview=fview: e.dma_start(out=fview, in_=src),
               deps=[prev_tc[par]] if prev_tc[par] else [], mark=True, dma=True,
               stream=("wcl", par))
        tc = E("vector", lambda e, fview=fview, bview=bview: e.tensor_copy(out=bview, in_=fview),
               deps=[tl] + ([prev_st[par]] if prev_st[par] else []), mark=True)
        prev_tc[par] = tc
        prev_st[par] = E("sync", lambda e, dst=dst, bview=bview: e.dma_start(out=dst, in_=bview),
                         deps=[tc], mark=True, dma=True, stream=("wcs", par))
        sts.append(prev_st[par])
    return sts[-2:] if len(sts) >= 2 else sts



def emit_stage_A(nc, P, stack, d, dbg=False):
    x_d = d["x"]
    w_in_d = d["W_in"]
    convw_d = d["convw"]
    convb_d = d["convb"]
    wx_d = d["Wx"]
    alog_d = d["Alog"]
    dp_d = d["Dp"]
    w_out_d = d["W_out"]
    eyebf_d = d["eye_bf"]
    eyef_d = d["eye_f"]
    ones1_d = d["ones1"]
    oh16_d = d["oh16"]
    consts_d = d["consts"]
    x1_d = d["x1"]
    winbf_d = d["winbf"]
    woutbf_d = d["woutbf"]
    for _k in d:
        pass
    dbg_xc = d.get("dbg_xc")
    dbg_yg = d.get("dbg_yg")
    dbg_dr = d.get("dbg_dr")
    dbg_bc = d.get("dbg_bc")
    dbg_h = d.get("dbg_h")
    dbg_fm = d.get("dbg_fm")
    dbg_xln = d.get("dbg_xln")
    dbg_cols = d.get("dbg_cols")
    S = stack.enter_context
    sb = lambda _n, _s, _t: nc.sbuf_tensor("A_" + _n, _s, _t)
    ps = lambda _n, _s, _t: nc.psum_tensor("A_" + _n, _s, _t)

    convw = S(sb("sconvw", [128, 64], F32))
    convb = S(sb("sconvb", [128, 16], F32))
    wx = S(sb("swx", [128, 16 * 96], F32))
    alog = S(sb("salog", [128, 256], F32))
    a_all = S(sb("a_all", [128, 256], F32))
    dp = S(sb("sdp", [128, 16], F32))
    eyebf = S(sb("eyebf", [128, 128], BF16))
    eyef = S(sb("eyef", [128, 128], F32))
    ones1 = S(sb("sones1", [1, 128], F32))
    oh16 = S(sb("soh16", [16, 2048], F32))
    consts = S(sb("sconsts", [128, 4], F32))

    xt = [S(sb(f"xt{i}", [128, D_MODEL], F32)) for i in range(2)]
    ln_s1 = S(sb("ln_s1", [128, 4], F32))
    ln_s2 = S(sb("ln_s2", [128, 4], F32))
    ln_mean = S(sb("ln_mean", [128, 4], F32))
    ln_var = S(sb("ln_var", [128, 4], F32))
    ln_msq = S(sb("ln_msq", [128, 4], F32))
    ln_rstd = S(sb("ln_rstd", [128, 4], F32))
    xln = S(sb("xln", [128, D_MODEL], F32))
    ln_sq = xln
    xln_bf = S(sb("xln_bf", [128, D_MODEL], BF16))

    xfm = S(sb("xfm", [128, 8 * CHUNK], BF16))
    xc = S(sb("xc", [128, 16 * CHUNK], F32))
    wconv_f = xc[:, 0:4096]
    wconv_b = xc[:, 4096:6144].bitcast(BF16)
    hslab = S(sb("hslab", [128, 4 * CHUNK], F32))
    ygated = S(sb("ygated", [128, 16 * CHUNK], BF16))
    dbrep = S(sb("dbrep", [128, 16 * CHUNK], BF16))
    crep = S(sb("crep", [128, 16 * CHUNK], F32))
    xin_tmp = S(sb("xin_tmp", [128, CHUNK + 3], F32))
    tails = S(sb("tails", [128, 16 * 3], F32))
    carry = S(sb("carry", [128, 16 * 16], F32))
    sBC = S(sb("sBC", [16, 2 * CHUNK], F32))
    drep = S(sb("drep", [128, CHUNK], F32))
    sp_t1 = hslab[:, 0:CHUNK]
    sp_t2 = hslab[:, CHUNK:2 * CHUNK]
    sdraw = sp_t2[0:1, :]
    abar = [S(sb(f"abar{i}", [128, CHUNK], F32)) for i in range(3)]
    ubuf = [S(sb(f"ubuf{i}", [128, CHUNK], F32)) for i in range(3)]

    xcd_scr = S(sb("xcd_scr", [128, CHUNK], F32))
    yout_fm = S(sb("yout_fm", [128, CHUNK], F32))
    wins = [S(sb(f"wins{i}", [128, 8 * 128], BF16)) for i in range(2)]
    wouts = [S(sb(f"wouts{i}", [128, 16 * 128], BF16)) for i in range(2)]
    xres = [S(sb(f"xres{i}", [128, D_MODEL], F32)) for i in range(4)]

    pXI = S(ps("pXI", [128, 512], F32))
    pRES = S(ps("pRES", [128, 512], F32))
    pSSM = S(ps("pSSM", [128, 512], F32))
    pBC = pSSM
    pY = S(ps("pY", [128, 512], F32))
    pW = pY
    pTR = [S(ps("pTRa", [128, 128], BF16)), S(ps("pTRb", [128, 128], BF16))]
    pTRF = [S(ps("pTRFa", [128, 128], F32)), S(ps("pTRFb", [128, 128], F32))]

    E = P.emit
    last = {}

    t_consts = []
    for dst, src in [(convw, convw_d), (convb, convb_d), (alog, alog_d), (dp, dp_d),
                     (eyebf, eyebf_d), (eyef, eyef_d), (ones1, ones1_d), (oh16, oh16_d),
                     (consts, consts_d)]:
        t_consts.append(E("sync", lambda e, dst=dst, src=src: e.dma_start(out=dst[:], in_=src),
                          mark=True, dma=True))
    t_consts.append(E("sync", lambda e: e.dma_start(
        out=wx[:], in_=wx_d.rearrange("(dt p) c -> p dt c", p=128)), mark=True, dma=True))

    tA1 = E("scalar", lambda e: e.activation(a_all[:], alog[:], AF.Exp), deps=t_consts, mark=True)
    tA = E("vector", lambda e: e.tensor_scalar_mul(a_all[:], a_all[:], -1.0), deps=[tA1], mark=True)

    pieces = []
    for k in range(8):
        for hh in range(2):
            pieces.append((w_in_d[k * 128:(k + 1) * 128, hh * 2048:(hh + 1) * 2048],
                           winbf_d[k * 128:(k + 1) * 128, hh * 2048:(hh + 1) * 2048], 2048))
    for k in range(16):
        pieces.append((w_out_d[k * 128:(k + 1) * 128, :], woutbf_d[k * 128:(k + 1) * 128, :], 1024))
    t_wconv_last = emit_w_convert(P, E, pieces, wconv_f, wconv_b)

    last_stores = []

    for b in range(B_PER_CORE):
        tz = E("vector", lambda e: e.memset(carry[:], 0.0), mark=True)
        for cch in range(4):
            ch = b * 4 + cch
            tok0 = b * SEQ + cch * CHUNK

            t_xfm = []
            for tt in range(4):
                row0 = tok0 + tt * 128
                buf = xt[tt % 2]
                tl = E("sync", lambda e, buf=buf, row0=row0: e.dma_start(
                    out=buf[:], in_=x_d[row0:row0 + 128, :]),
                    deps=[last.get(("xt", tt % 2))] if last.get(("xt", tt % 2)) else [],
                    mark=True, dma=True)
                col = lambda t_, tt=tt: t_[:, tt:tt + 1]
                ts1 = E("scalar", lambda e, buf=buf, tt=tt: e.activation(
                    ln_sq[:], buf[:], AF.Square, accum_out=ln_s2[:, tt:tt + 1]),
                    deps=[tl] + ([last.get(("lnuse", tt))] if last.get(("lnuse", tt)) else [])
                    + ([last.get("xln_rd")] if last.get("xln_rd") else []),
                    mark=True)
                ts2 = E("vector", lambda e, buf=buf, tt=tt: e.reduce_sum(
                    ln_s1[:, tt:tt + 1], buf[:], axis=AX.X), deps=[tl], mark=True)
                P.drain("vector")
                E("vector", lambda e, tt=tt: e.tensor_scalar_mul(
                    ln_mean[:, tt:tt + 1], ln_s1[:, tt:tt + 1], 1.0 / D_MODEL), deps=[ts2])
                E("vector", lambda e, tt=tt: e.tensor_scalar_mul(
                    ln_var[:, tt:tt + 1], ln_s2[:, tt:tt + 1], 1.0 / D_MODEL), deps=[ts1])
                P.drain("vector")
                E("vector", lambda e, tt=tt: e.tensor_tensor(
                    ln_msq[:, tt:tt + 1], ln_mean[:, tt:tt + 1], ln_mean[:, tt:tt + 1], ALU.mult))
                P.drain("vector")
                E("vector", lambda e, tt=tt: e.tensor_sub(
                    ln_var[:, tt:tt + 1], ln_var[:, tt:tt + 1], ln_msq[:, tt:tt + 1]))
                P.drain("vector")
                tve = E("vector", lambda e, tt=tt: e.tensor_scalar_add(
                    ln_var[:, tt:tt + 1], ln_var[:, tt:tt + 1], 1e-5), mark=True)
                tsq = E("scalar", lambda e, tt=tt: e.activation(
                    ln_rstd[:, tt:tt + 1], ln_var[:, tt:tt + 1], AF.Sqrt), deps=[tve], mark=True)
                trc = E("vector", lambda e, tt=tt: e.reciprocal(
                    ln_rstd[:, tt:tt + 1], ln_rstd[:, tt:tt + 1]), deps=[tsq], mark=True)
                P.drain("vector")
                txl = E("vector", lambda e, buf=buf, tt=tt: e.tensor_scalar(
                    xln[:], buf[:], ln_mean[:, tt:tt + 1], ln_rstd[:, tt:tt + 1],
                    ALU.subtract, ALU.mult),
                    deps=[trc] + ([last.get("xlnrd")] if last.get("xlnrd") else []), mark=True)
                last[("lnuse", tt)] = txl
                txb = E("vector", lambda e: e.tensor_copy(out=xln_bf[:], in_=xln[:]),
                        deps=[txl], mark=True)
                last["xln_rd"] = txb
                if dbg and ch == 0 and tt == 3:
                    E("sync", lambda e: e.dma_start(out=dbg_xln, in_=xln[:]), deps=[txb], mark=True, dma=True)
                    E("sync", lambda e: e.dma_start(out=dbg_cols[:, 0:4], in_=ln_s1[:]), deps=[txb], mark=True, dma=True)
                    E("sync", lambda e: e.dma_start(out=dbg_cols[:, 4:8], in_=ln_s2[:]), deps=[txb], mark=True, dma=True)
                    E("sync", lambda e: e.dma_start(out=dbg_cols[:, 8:12], in_=ln_mean[:]), deps=[txb], mark=True, dma=True)
                    E("sync", lambda e: e.dma_start(out=dbg_cols[:, 12:16], in_=ln_var[:]), deps=[txb], mark=True, dma=True)
                    E("sync", lambda e: e.dma_start(out=dbg_cols[:, 16:20], in_=ln_rstd[:]), deps=[txb], mark=True, dma=True)
                last[("xt", tt % 2)] = txb
                tev = None
                for kd in range(8):
                    pv = pTR[kd % 2]
                    ttr = E("tensor", lambda e, kd=kd, pv=pv: e.transpose(
                        pv[:], xln_bf[:, kd * 128:(kd + 1) * 128], eyebf[:]),
                        deps=[txb, tA, t_wconv_last] +
                        ([last.get(("pTRp", kd % 2))] if last.get(("pTRp", kd % 2)) else []),
                        mark=True)
                    tev = E("scalar", lambda e, kd=kd, tt=tt, pv=pv: e.copy(
                        out=xfm[:, kd * CHUNK + tt * 128: kd * CHUNK + (tt + 1) * 128],
                        in_=pv[:]), deps=[ttr], mark=True)
                    last[("pTRp", kd % 2)] = tev
                    t_xfm.append(tev)
                last["xlnrd"] = ttr
                last[("xt", tt % 2)] = ttr

            t_xc = {}
            t_ssm_last = None
            for dt in range(16):
                wbuf = wins[dt % 2]
                twl = E("sync", lambda e, dt=dt, wbuf=wbuf: e.dma_start(
                    out=wbuf[:],
                    in_=winbf_d.rearrange("(k p) m -> p k m", p=128)[:, :, dt * 128:(dt + 1) * 128]),
                    deps=[t_wconv_last] + ([last.get(("win", dt % 2))] if last.get(("win", dt % 2)) else []),
                    mark=True, dma=True, stream=("win", dt % 2))
                tmm = None
                for k in range(8):
                    tmm = E("tensor", lambda e, k=k, wbuf=wbuf: e.matmul(
                        pXI[:], wbuf[:, k * 128:(k + 1) * 128], xfm[:, k * CHUNK:(k + 1) * CHUNK],
                        start=(k == 0), stop=(k == 7)),
                        deps=([twl] + t_xfm + [last.get("pXI")] if k == 0 else []),
                        mark=(k == 7))
                last[("win", dt % 2)] = tmm
                tev = E("scalar", lambda e: e.copy(out=xin_tmp[:, 3:3 + CHUNK], in_=pXI[:]),
                        deps=[tmm] + ([last.get("xin_rd")] if last.get("xin_rd") else []), mark=True)
                last["pXI"] = tev
                if cch == 0:
                    ttl = E("vector", lambda e: e.memset(xin_tmp[:, 0:3], 0.0), deps=[tev], mark=True)
                else:
                    ttl = E("vector", lambda e, dt=dt: e.tensor_copy(
                        out=xin_tmp[:, 0:3], in_=tails[:, dt * 3:(dt + 1) * 3]), deps=[tev], mark=True)
                P.drain("vector")
                tc0 = E("vector", lambda e, dt=dt: e.tensor_scalar(
                    sp_t1, xin_tmp[:, 0:CHUNK], convw[:, dt * 4:dt * 4 + 1],
                    convb[:, dt:dt + 1], ALU.mult, ALU.add), deps=[tev, ttl], mark=True)
                tcl = tc0
                for j in range(1, 4):
                    tcl = E("vector", lambda e, dt=dt, j=j: e.scalar_tensor_tensor(
                        sp_t1, xin_tmp[:, j:j + CHUNK], convw[:, dt * 4 + j:dt * 4 + j + 1],
                        sp_t1, ALU.mult, ALU.add), mark=(j == 3))
                tnt = E("vector", lambda e, dt=dt: e.tensor_copy(
                    out=tails[:, dt * 3:(dt + 1) * 3], in_=xin_tmp[:, CHUNK:CHUNK + 3]),
                    deps=[tcl], mark=True)
                last["xin_rd"] = tnt
                txc = E("scalar", lambda e, dt=dt: e.activation(
                    xc[:, dt * CHUNK:(dt + 1) * CHUNK], sp_t1, AF.Silu),
                    deps=[tcl, t_wconv_last] + ([last.get(("xc", dt))] if last.get(("xc", dt)) else []),
                    mark=True)
                t_xc[dt] = txc
                t_ssm_last = E("tensor", lambda e, dt=dt: e.matmul(
                    pSSM[0:96, :], wx[:, dt * 96:(dt + 1) * 96], xc[:, dt * CHUNK:(dt + 1) * CHUNK],
                    start=(dt == 0), stop=(dt == 15), skip_group_check=True),
                    deps=[txc] + ([last.get("pSSM")] if dt == 0 and last.get("pSSM") else []),
                    mark=(dt == 15))
                twr = E("sync", lambda e, dt=dt: e.dma_start(
                    out=wins[dt % 2][:],
                    in_=winbf_d.rearrange("(k p) m -> p k m", p=128)[:, :, 2048 + dt * 128:2048 + (dt + 1) * 128]),
                    deps=[t_wconv_last] + ([last.get(("win", dt % 2))] if last.get(("win", dt % 2)) else []),
                    mark=True, dma=True, stream=("win", dt % 2))
                tmr = None
                for k in range(8):
                    tmr = E("tensor", lambda e, k=k, dt=dt: e.matmul(
                        pRES[:], wins[dt % 2][:, k * 128:(k + 1) * 128],
                        xfm[:, k * CHUNK:(k + 1) * CHUNK], start=(k == 0), stop=(k == 7)),
                        deps=([twr] + ([last.get("pRES")] if last.get("pRES") else [])
                              if k == 0 else []), mark=(k == 7))
                last[("win", dt % 2)] = tmr
                tsr = E("scalar", lambda e, dt=dt: e.activation(
                    ygated[:, dt * CHUNK:(dt + 1) * CHUNK], pRES[:], AF.Silu),
                    deps=[tmr] + ([last.get(("yg", dt))] if last.get(("yg", dt)) else []), mark=True)
                last["pRES"] = tsr
                last[("sres", dt)] = tsr

            teB = E("scalar", lambda e: e.copy(out=sBC[:, 0:CHUNK], in_=pSSM[0:16, :]),
                    deps=[t_ssm_last] + ([last.get("sBC_rd")] if last.get("sBC_rd") else []), mark=True)
            teC = E("scalar", lambda e: e.copy(out=sBC[:, CHUNK:2 * CHUNK], in_=pSSM[32:48, :]), mark=True)
            teD = E("scalar", lambda e: e.copy(out=sdraw, in_=pSSM[64:65, :]), mark=True)
            last["pSSM"] = teD
            tbd = E("tensor", lambda e: e.matmul(pBC[:], ones1[:], sdraw, start=True, stop=True),
                    deps=[teD] + ([last.get("pBC")] if last.get("pBC") else []), mark=True)
            tp1 = E("scalar", lambda e: e.activation(sp_t1, pBC[:], AF.Abs), deps=[tbd], mark=True)
            tp2 = E("scalar", lambda e: e.activation(sp_t2, sp_t1, AF.Exp, scale=-1.0), mark=True)
            tp3 = E("scalar", lambda e: e.activation(sp_t2, sp_t2, AF.Ln, bias=consts[:, 0:1]),
                    mark=True)
            tp4 = E("vector", lambda e: e.tensor_scalar_max(sp_t1, pBC[:], 0.0),
                    deps=[tbd, tp2], mark=True)
            tdr = E("vector", lambda e: e.tensor_add(drep[:], sp_t1, sp_t2),
                    deps=[tp3, tp4] + ([last.get("drep_rd")] if last.get("drep_rd") else []), mark=True)
            last["pBC"] = tp4

            t_rep = {}
            for n in range(16):
                tb1 = E("tensor", lambda e, n=n: e.matmul(
                    pBC[:], oh16[:, n * 128:(n + 1) * 128], sBC[:, 0:CHUNK], start=True, stop=True),
                    deps=[teB, last["pBC"], tdr], mark=True)
                tb2 = E("vector", lambda e, n=n: e.tensor_tensor(
                    dbrep[:, n * CHUNK:(n + 1) * CHUNK], pBC[:], drep[:], ALU.mult),
                    deps=[tb1] + ([last.get(("dbrep", n))] if last.get(("dbrep", n)) else []), mark=True)
                tc1 = E("tensor", lambda e, n=n: e.matmul(
                    pBC[:], oh16[:, n * 128:(n + 1) * 128], sBC[:, CHUNK:2 * CHUNK], start=True, stop=True),
                    deps=[teC, tb2], mark=True)
                tc2 = E("scalar", lambda e, n=n: e.copy(
                    out=crep[:, n * CHUNK:(n + 1) * CHUNK], in_=pBC[:]),
                    deps=[tc1] + ([last.get(("crep", n))] if last.get(("crep", n)) else []), mark=True)
                last["pBC"] = tc2
                t_rep[n] = (tb2, tc2)
            last["sBC_rd"] = t_rep[15][1]
            last["pSSM"] = t_rep[15][1]

            t_drep_rd = None
            for dt in range(16):
                t_y_last = None
                for n in range(16):
                    par = n % 3
                    hb = n % 4
                    ta = E("scalar", lambda e, dt=dt, n=n, par=par: e.activation(
                        abar[par][:], drep[:], AF.Exp, scale=a_all[:, dt * 16 + n:dt * 16 + n + 1]),
                        deps=[tdr] + ([last.get(("abar_rd", par))] if last.get(("abar_rd", par)) else []),
                        mark=True)
                    tu = E("gpsimd", lambda e, dt=dt, n=n, par=par: e.tensor_tensor(
                        ubuf[par][:], xc[:, dt * CHUNK:(dt + 1) * CHUNK],
                        dbrep[:, n * CHUNK:(n + 1) * CHUNK], ALU.mult),
                        deps=[t_xc[dt], t_rep[n][0]] +
                        ([last.get(("ubuf_rd", par))] if last.get(("ubuf_rd", par)) else []),
                        mark=True)
                    th = E("vector", lambda e, dt=dt, n=n, par=par, hb=hb: e.tensor_tensor_scan(
                        hslab[:, hb * CHUNK:(hb + 1) * CHUNK], abar[par][:], ubuf[par][:],
                        carry[:, dt * 16 + n:dt * 16 + n + 1], ALU.mult, ALU.add),
                        deps=[ta, tu], mark=True)
                    last[("abar_rd", par)] = th
                    last[("ubuf_rd", par)] = th
                    tt_ = E("vector", lambda e, dt=dt, n=n, par=par, hb=hb: e.tensor_tensor(
                        ubuf[par][:], hslab[:, hb * CHUNK:(hb + 1) * CHUNK],
                        crep[:, n * CHUNK:(n + 1) * CHUNK], ALU.mult),
                        deps=[th, t_rep[n][1]], mark=True)
                    t_y_last = E("tensor", lambda e, n=n, par=par: e.matmul(
                        pY[:], eyef[:], ubuf[par][:], start=(n == 0), stop=(n == 15)),
                        deps=[tt_] + ([last.get("pY")] if n == 0 and last.get("pY") else []),
                        mark=True)
                    last[("ubuf_rd", par)] = t_y_last
                    if n % 4 == 3 and n < 15:
                        g = n // 4
                        E("vector", lambda e, dt=dt, g=g: e.tensor_copy(
                            out=carry[:, dt * 16 + g * 4:dt * 16 + (g + 1) * 4],
                            in_=hslab[:, CHUNK - 1:4 * CHUNK:CHUNK]))
                        P.drain("vector")
                tcar = E("vector", lambda e, dt=dt: e.tensor_copy(
                    out=carry[:, dt * 16 + 12:(dt + 1) * 16],
                    in_=hslab[:, CHUNK - 1:4 * CHUNK:CHUNK]), mark=True)
                P.drain("vector")
                if dt == 15:
                    prev_scan_done = tcar
                tq1 = E("vector", lambda e, dt=dt: e.tensor_scalar_mul(
                    xcd_scr[:], xc[:, dt * CHUNK:(dt + 1) * CHUNK], dp[:, dt:dt + 1]),
                    deps=[t_xc[dt]] + ([last.get("xcd_rd")] if last.get("xcd_rd") else []), mark=True)
                tq2 = E("vector", lambda e: e.tensor_add(xcd_scr[:], xcd_scr[:], pY[:]),
                        deps=[tq1, t_y_last], mark=True)
                last["pY"] = tq2
                tq3 = E("vector", lambda e, dt=dt: e.tensor_tensor(
                    ygated[:, dt * CHUNK:(dt + 1) * CHUNK], xcd_scr[:],
                    ygated[:, dt * CHUNK:(dt + 1) * CHUNK], ALU.mult),
                    deps=[tq2, last[("sres", dt)]], mark=True)
                last["xcd_rd"] = tq3
                last[("yg", dt)] = tq3
                last[("xc", dt)] = tq1
                last["drep_rd"] = ta
            t_xr = []
            for tt in range(4):
                row0 = tok0 + tt * 128
                t_xr.append(E("sync", lambda e, tt=tt, row0=row0: e.dma_start(
                    out=xres[tt][:], in_=x_d[row0:row0 + 128, :]),
                    deps=[last.get(("xres", tt))] if last.get(("xres", tt)) else [],
                    mark=True, dma=True))
            yg_deps = [last[("yg", dtk)] for dtk in range(16)]
            for m in range(8):
                wbuf = wouts[m % 2]
                twl = E("sync", lambda e, m=m, wbuf=wbuf: e.dma_start(
                    out=wbuf[:],
                    in_=woutbf_d.rearrange("(k p) m -> p k m", p=128)[:, :, m * 128:(m + 1) * 128]),
                    deps=[t_wconv_last] + ([last.get(("wout", m % 2))] if last.get(("wout", m % 2)) else []),
                    mark=True, dma=True, stream=("wo", m % 2))
                tmm = None
                for k in range(16):
                    tmm = E("tensor", lambda e, k=k, m=m, wbuf=wbuf: e.matmul(
                        pW[:], wbuf[:, k * 128:(k + 1) * 128], ygated[:, k * CHUNK:(k + 1) * CHUNK],
                        start=(k == 0), stop=(k == 15)),
                        deps=([twl] + yg_deps + ([last.get("pW")] if last.get("pW") else [])
                              if k == 0 else []),
                        mark=(k == 15))
                last[("wout", m % 2)] = tmm
                tev = E("scalar", lambda e: e.copy(out=yout_fm[:], in_=pW[:]),
                        deps=[tmm] + ([last.get("yout_rd")] if last.get("yout_rd") else []), mark=True)
                last["pW"] = tev
                ttr = None
                for tt in range(4):
                    pf = pTRF[tt % 2]
                    ttr = E("tensor", lambda e, tt=tt, pf=pf: e.transpose(
                        pf[:], yout_fm[:, tt * 128:(tt + 1) * 128], eyef[:]),
                        deps=[tev] + ([last.get(("pTRFr", tt % 2))] if last.get(("pTRFr", tt % 2)) else []),
                        mark=True)
                    tad = E("vector", lambda e, tt=tt, m=m, pf=pf: e.tensor_tensor(
                        xres[tt][:, m * 128:(m + 1) * 128], xres[tt][:, m * 128:(m + 1) * 128],
                        pf[:], ALU.add), deps=[ttr, t_xr[tt]], mark=True)
                    last[("pTRFr", tt % 2)] = tad
                    last[("xadd", tt, m)] = tad
                last["yout_rd"] = ttr
            for tt in range(4):
                row0 = tok0 + tt * 128
                tst = E("sync", lambda e, tt=tt, row0=row0: e.dma_start(
                    out=x1_d[row0:row0 + 128, :], in_=xres[tt][:]),
                    deps=[last[("xadd", tt, m)] for m in range(8)], mark=True, dma=True)
                last[("xres", tt)] = tst
                last_stores.append(tst)





def host_prep_A(inputs):
    W_x = np.asarray(inputs["L0_attn_W_x"], np.float32)
    wx_pad = np.zeros((D_INNER, 96), np.float32)
    wx_pad[:, 0:16] = W_x[:, 1:17]
    wx_pad[:, 32:48] = W_x[:, 17:33]
    wx_pad[:, 64] = W_x[:, 0]
    convw = np.ascontiguousarray(
        np.asarray(inputs["L0_attn_conv_w"], np.float32).reshape(16, 128, 4)
        .transpose(1, 0, 2).reshape(128, 64))
    convb = np.ascontiguousarray(np.asarray(inputs["L0_attn_conv_b"], np.float32).reshape(16, 128).T)
    alog = np.ascontiguousarray(
        np.asarray(inputs["L0_attn_A_log"], np.float32).reshape(16, 128, 16)
        .transpose(1, 0, 2).reshape(128, 256))
    dp = np.ascontiguousarray(np.asarray(inputs["L0_attn_D_param"], np.float32).reshape(16, 128).T)
    oh16 = np.zeros((16, 2048), np.float32)
    for n in range(16):
        oh16[n, n * 128:(n + 1) * 128] = 1.0
    consts = np.zeros((128, 4), np.float32)
    consts[:, 0] = 1.0
    common = {
        "W_in": np.ascontiguousarray(np.asarray(inputs["L0_attn_W_in"], np.float32)),
        "convw": convw, "convb": convb, "Wx": wx_pad, "Alog": alog, "Dp": dp,
        "W_out": np.ascontiguousarray(np.asarray(inputs["L0_attn_W_out"], np.float32)),
        "eye_bf": np.eye(128, dtype=bf16),
        "eye_f": np.eye(128, dtype=np.float32),
        "ones1": np.ones((1, 128), np.float32),
        "oh16": oh16, "consts": consts,
    }
    x = np.asarray(inputs["L0_attn_x"], np.float32)
    in_maps = []
    for c in range(N_CORES):
        m = dict(common)
        m["x"] = np.ascontiguousarray(x[2 * c:2 * c + 2].reshape(T_CORE, D_MODEL))
        in_maps.append(m)
    return in_maps


def emit_stage_B(nc, P, stack, d, dbg=False):
    x1_d = d["x1"]
    wq_d = d["WqT"]
    rt_d = d["RtT"]
    lv_d = d["LV"]
    wo_d = d["WoT"]
    bias_d = d["wout_b"]
    eyebf_d = d["eye_bf"]
    eyef_d = d["eye_f"]
    onesw_d = d["onesw"]
    x2_d = d["x2"]
    wqbf_d = d["wqbf"]
    rtbf_d = d["rtbf"]
    lvbf_d = d["lvbf"]
    wobf_d = d["wobf"]
    for _k in d:
        pass
    S = stack.enter_context
    sb = lambda _n, _s, _t: nc.sbuf_tensor("B_" + _n, _s, _t)
    ps = lambda _n, _s, _t: nc.psum_tensor("B_" + _n, _s, _t)

    eyebf = S(sb("s_eyebf", [128, 128], BF16))
    eyef = S(sb("s_eyef", [128, 128], F32))
    onesw = S(sb("s_onesw", [128, 128], F32))
    bias = S(sb("s_bias", [128, 8], F32))
    wq_sb = S(sb("wq_sb", [128, 8 * 1024], BF16))
    rt_sb = S(sb("rt_sb", [128, 8 * 256], BF16))
    lv_sb = S(sb("lv_sb", [128, 2 * 1024], BF16))
    wo_sb = S(sb("wo_sb", [128, 8 * 1024], BF16))

    xt = [S(sb(f"xt{i}", [128, D_MODEL], F32)) for i in range(2)]
    ln_s1 = S(sb("ln_s1", [128, 4], F32))
    ln_s2 = S(sb("ln_s2", [128, 4], F32))
    ln_mean = S(sb("ln_mean", [128, 4], F32))
    ln_var = S(sb("ln_var", [128, 4], F32))
    ln_msq = S(sb("ln_msq", [128, 4], F32))
    ln_rstd = S(sb("ln_rstd", [128, 4], F32))
    xln = [S(sb("xln0", [128, D_MODEL], F32)), S(sb("xln1", [128, D_MODEL], F32))]
    xln_bf = [S(sb("xln_bf0", [128, D_MODEL], BF16)), S(sb("xln_bf1", [128, D_MODEL], BF16))]
    xfm = S(sb("xfm", [128, 8 * CHUNK], BF16))
    q_bf = S(sb("q_bf", [128, 8 * CHUNK], BF16))
    ret_bf = S(sb("ret_bf", [128, 8 * CHUNK], BF16))
    gfm = [S(sb(f"gfm{i}", [128, CHUNK], F32)) for i in range(2)]
    gtm = S(sb("gtm", [128, 256], F32))
    gl = S(sb("gl", [128, 256], F32))
    lp = S(sb("lp", [128, 256], F32))
    lp_bf = S(sb("lp_bf", [128, 256], BF16))
    lpfm = S(sb("lpfm", [128, 2 * CHUNK], BF16))
    youtB = S(sb("youtB", [128, 8 * CHUNK], F32))
    xres = [S(sb(f"xres{i}", [128, D_MODEL], F32)) for i in range(4)]

    pQ = S(ps("pQ", [128, 512], F32))
    pG = S(ps("pG", [128, 512], F32))
    pR = S(ps("pR", [128, 512], F32))
    pW = S(ps("pW", [128, 512], F32))
    pTR = [S(ps("pTRa", [128, 128], BF16)), S(ps("pTRb", [128, 128], BF16))]
    pTRF = [S(ps("pTRFa", [128, 128], F32)), S(ps("pTRFb", [128, 128], F32))]

    E = P.emit
    last = {}

    t_consts = []
    for dst, srcd in [(eyebf, eyebf_d), (eyef, eyef_d), (onesw, onesw_d), (bias, bias_d)]:
        t_consts.append(E("sync", lambda e, dst=dst, srcd=srcd: e.dma_start(out=dst[:], in_=srcd),
                          mark=True, dma=True))

    stage_f = youtB[:, 0:2048]
    stage_b_ = youtB[:, 2048:3072].bitcast(BF16)
    pieces = []
    for k in range(8):
        pieces.append((wq_d[k * 128:(k + 1) * 128, :], wqbf_d[k * 128:(k + 1) * 128, :], 1024))
    for k in range(8):
        pieces.append((rt_d[k * 128:(k + 1) * 128, :], rtbf_d[k * 128:(k + 1) * 128, :], 256))
    for k in range(2):
        pieces.append((lv_d[k * 128:(k + 1) * 128, :], lvbf_d[k * 128:(k + 1) * 128, :], 1024))
    for k in range(8):
        pieces.append((wo_d[k * 128:(k + 1) * 128, :], wobf_d[k * 128:(k + 1) * 128, :], 1024))
    t_conv = emit_w_convert(P, E, pieces, stage_f, stage_b_)
    t_res = []
    for dst, srcd, kk in [(wq_sb, wqbf_d, 8), (rt_sb, rtbf_d, 8), (lv_sb, lvbf_d, 2), (wo_sb, wobf_d, 8)]:
        t_res.append(E("sync", lambda e, dst=dst, srcd=srcd, kk=kk: e.dma_start(
            out=dst[:], in_=srcd.rearrange("(k p) m -> p k m", p=128)), deps=[t_conv],
            mark=True, dma=True))

    ctx = {"xt": xt, "ln_s1": ln_s1, "ln_s2": ln_s2, "ln_mean": ln_mean, "ln_var": ln_var,
           "ln_msq": ln_msq, "ln_rstd": ln_rstd, "ln_sq": xln[0], "xln": xln, "xln_bf": xln_bf,
           "xfm": xfm, "eyebf": eyebf, "eyef": eyef, "pTR": pTR, "pTRF": pTRF, "xres": xres}

    for ch in range(8):
        tok0 = ch * CHUNK
        t_xfm = emit_ln_to_fm(P, E, last, ctx, x1_d, tok0)
        t_q = []
        for m in range(8):
            tmm = None
            for k in range(8):
                tmm = E("tensor", lambda e, k=k, m=m: e.matmul(
                    pQ[:], wq_sb[:, k * 1024 + m * 128: k * 1024 + (m + 1) * 128],
                    xfm[:, k * CHUNK:(k + 1) * CHUNK], start=(k == 0), stop=(k == 7)),
                    deps=(t_xfm + t_res + ([last.get("pQ")] if last.get("pQ") else [])
                          if k == 0 else []), mark=(k == 7))
            tev = E("scalar", lambda e, m=m: e.copy(
                out=q_bf[:, m * CHUNK:(m + 1) * CHUNK], in_=pQ[:]),
                deps=[tmm] + ([last.get(("q", m))] if last.get(("q", m)) else []), mark=True)
            last["pQ"] = tev
            t_q.append(tev)
        t_g = []
        for mg in range(2):
            tmm = None
            for k in range(8):
                tmm = E("tensor", lambda e, k=k, mg=mg: e.matmul(
                    pG[:], rt_sb[:, k * 256 + mg * 128: k * 256 + (mg + 1) * 128],
                    q_bf[:, k * CHUNK:(k + 1) * CHUNK], start=(k == 0), stop=(k == 7)),
                    deps=(t_q + ([last.get("pG")] if last.get("pG") else []) if k == 0 else []),
                    mark=(k == 7))
            tev = E("scalar", lambda e, mg=mg: e.activation(
                gfm[mg][:], pG[:], AF.Sigmoid),
                deps=[tmm] + ([last.get(("g", mg))] if last.get(("g", mg)) else []), mark=True)
            last["pG"] = tev
            t_g.append(tev)
        for m in range(8):
            last[("q", m)] = t_g[-1]
        t_lp_ev = []
        for tt in range(4):
            tgt = None
            for mg in range(2):
                pf = pTRF[mg % 2]
                ttr = E("tensor", lambda e, tt=tt, mg=mg, pf=pf: e.transpose(
                    pf[:], gfm[mg][:, tt * 128:(tt + 1) * 128], eyef[:]),
                    deps=[t_g[mg]] + ([last.get(("pTRFg", mg % 2))] if last.get(("pTRFg", mg % 2)) else []),
                    mark=True)
                tgt = E("vector", lambda e, mg=mg, pf=pf: e.tensor_copy(
                    out=gtm[:, mg * 128:(mg + 1) * 128], in_=pf[:]),
                    deps=[ttr] + ([last.get("gtm_rd")] if last.get("gtm_rd") else []), mark=True)
                last[("pTRFg", mg % 2)] = tgt
            tgl = E("vector", lambda e: e.tensor_scalar(
                gl[:], gtm[:], -1.0, 1.0, ALU.mult, ALU.add), deps=[tgt], mark=True)
            P.drain("vector")
            E("vector", lambda e: e.tensor_single_scalar(
                lp[:, 0:128], onesw[:], gtm[:, 0:1], ALU.subtract), deps=t_consts)
            tl0 = E("vector", lambda e: e.tensor_scalar_mul(
                lp[:, 128:256], onesw[:], gtm[:, 0:1]), mark=True)
            tlv = tl0
            for lv_ in range(1, 8):
                n = 1 << lv_
                half = 128 >> lv_
                node0 = n - 1
                P.drain("vector")
                lpv = lp[:].rearrange("p (n two h) -> p n two h", n=n, two=2, h=half)
                E("vector", lambda e, n=n, half=half, node0=node0, lpv=lpv: e.tensor_tensor(
                    lpv[:, :, 1, :], lpv[:, :, 1, :],
                    gtm[:, node0:node0 + n].unsqueeze(2).broadcast_to((128, n, half)), ALU.mult))
                P.drain("vector")
                tlv = E("vector", lambda e, n=n, half=half, node0=node0, lpv=lpv: e.tensor_tensor(
                    lpv[:, :, 0, :], lpv[:, :, 0, :],
                    gl[:, node0:node0 + n].unsqueeze(2).broadcast_to((128, n, half)), ALU.mult),
                    mark=True)
            last["gtm_rd"] = tlv
            tlb = E("vector", lambda e: e.tensor_copy(out=lp_bf[:], in_=lp[:]), deps=[tlv], mark=True)
            for kl in range(2):
                pv = pTR[kl % 2]
                ttr = E("tensor", lambda e, kl=kl, pv=pv: e.transpose(
                    pv[:], lp_bf[:, kl * 128:(kl + 1) * 128], eyebf[:]),
                    deps=[tlb] + ([last.get(("pTRlp", kl % 2))] if last.get(("pTRlp", kl % 2)) else []),
                    mark=True)
                tev = E("scalar", lambda e, kl=kl, tt=tt, pv=pv: e.copy(
                    out=lpfm[:, kl * CHUNK + tt * 128: kl * CHUNK + (tt + 1) * 128],
                    in_=pv[:]), deps=[ttr], mark=True)
                last[("pTRlp", kl % 2)] = tev
                t_lp_ev.append(tev)
        t_ret = []
        for m in range(8):
            tmm = None
            for k in range(2):
                tmm = E("tensor", lambda e, k=k, m=m: e.matmul(
                    pR[:], lv_sb[:, k * 1024 + m * 128: k * 1024 + (m + 1) * 128],
                    lpfm[:, k * CHUNK:(k + 1) * CHUNK], start=(k == 0), stop=(k == 1)),
                    deps=(t_lp_ev + ([last.get("pR")] if last.get("pR") else [])
                          if k == 0 else []), mark=(k == 1))
            tev = E("scalar", lambda e, m=m: e.copy(
                out=ret_bf[:, m * CHUNK:(m + 1) * CHUNK], in_=pR[:]),
                deps=[tmm] + ([last.get(("ret", m))] if last.get(("ret", m)) else []), mark=True)
            last["pR"] = tev
            t_ret.append(tev)
        t_yev = []
        for m in range(8):
            tmm = None
            for k in range(8):
                tmm = E("tensor", lambda e, k=k, m=m: e.matmul(
                    pW[:], wo_sb[:, k * 1024 + m * 128: k * 1024 + (m + 1) * 128],
                    ret_bf[:, k * CHUNK:(k + 1) * CHUNK], start=(k == 0), stop=(k == 7)),
                    deps=(t_ret + ([last.get("pW")] if last.get("pW") else []) if k == 0 else []),
                    mark=(k == 7))
            tev = E("scalar", lambda e, m=m: e.activation(
                youtB[:, m * CHUNK:(m + 1) * CHUNK], pW[:], AF.Identity, bias=bias[:, m:m + 1]),
                deps=[tmm] + ([last.get(("src_fm_rd", m))] if last.get(("src_fm_rd", m)) else []),
                mark=True)
            last["pW"] = tev
            t_yev.append(tev)
        for m in range(8):
            last[("ret", m)] = t_yev[-1]
        src_views = [youtB[:, m * CHUNK:(m + 1) * CHUNK] for m in range(8)]
        emit_residual_add_store(P, E, last, ctx, x1_d, x2_d, tok0, src_views, t_yev)




def emit_stage_C(nc, P, stack, d, dbg=False):
    x2_d = d["x2"]
    bwt_d = d["BwT"]
    cwt_d = d["CwT"]
    dvec_d = d["Dvec"]
    apar_d = d["apar"]
    ang1_d = d["ang1"]
    ang2_d = d["ang2"]
    eyebf_d = d["eye_bf"]
    eyef_d = d["eye_f"]
    ones1_d = d["ones1"]
    x3_d = d["x3"]
    bwtbf_d = d["bwtbf"]
    cwtbf_d = d["cwtbf"]
    for _k in d:
        pass
    dbg_w = d.get("dbg_w")
    dbg_tre = d.get("dbg_tre")
    dbg_tim = d.get("dbg_tim")
    dbg_bu = d.get("dbg_bu")
    dbg_sre = d.get("dbg_sre")
    dbg_h = d.get("dbg_h")
    dbg_oc = d.get("dbg_oc")
    dbg_vre = d.get("dbg_vre")
    dbg_vim = d.get("dbg_vim")
    S = stack.enter_context
    sb = lambda _n, _s, _t: nc.sbuf_tensor("C_" + _n, _s, _t)
    ps = lambda _n, _s, _t: nc.psum_tensor("C_" + _n, _s, _t)

    eyebf = S(sb("s_eyebf", [128, 128], BF16))
    eyef = S(sb("s_eyef", [128, 128], F32))
    ones1 = S(sb("s_ones1", [1, 128], F32))
    apar = S(sb("s_apar", [128, 2], F32))
    womega = S(sb("womega", [128, 16], F32))
    bwt_sb = S(sb("bwt_sb", [128, 8 * 512], BF16))
    cwt_sb = S(sb("cwt_sb", [128, 4 * 1024], BF16))
    dbc = S(sb("dbc", [128, D_MODEL], F32))
    Tre = [S(sb(f"Tre{i}", [128, CHUNK], F32)) for i in range(4)]
    Tim = [S(sb(f"Tim{i}", [128, CHUNK], F32)) for i in range(4)]
    Tcre = [S(sb(f"Tcre{i}", [128, CHUNK], F32)) for i in range(4)]
    Tcim = [S(sb(f"Tcim{i}", [128, CHUNK], F32)) for i in range(4)]
    tsc1 = S(sb("tsc1", [128, 1024], F32))
    tsc2 = S(sb("tsc2", [128, 1024], F32))
    qcol = S(sb("qcol", [128, 8], F32))
    ocols = [S(sb(f"ocols{i}", [128, 6], F32)) for i in range(4)]

    xt = [S(sb(f"xt{i}", [128, D_MODEL], F32)) for i in range(2)]
    ln_s1 = S(sb("ln_s1", [128, 4], F32))
    ln_s2 = S(sb("ln_s2", [128, 4], F32))
    ln_mean = S(sb("ln_mean", [128, 4], F32))
    ln_var = S(sb("ln_var", [128, 4], F32))
    ln_msq = S(sb("ln_msq", [128, 4], F32))
    ln_rstd = S(sb("ln_rstd", [128, 4], F32))
    xln = [S(sb("xln0", [128, D_MODEL], F32)), S(sb("xln1", [128, D_MODEL], F32))]
    xln_bf = [S(sb("xln_bf0", [128, D_MODEL], BF16)), S(sb("xln_bf1", [128, D_MODEL], BF16))]
    xfm = S(sb("xfm", [128, 8 * CHUNK], BF16))
    xlntm = [S(sb(f"xlntm{i}", [128, D_MODEL], F32)) for i in range(4)]
    bu = S(sb("bu", [128, 4 * CHUNK], F32))
    vre = S(sb("vre", [128, 4 * CHUNK], F32))
    vim = S(sb("vim", [128, 4 * CHUNK], F32))
    angs = vre[:, 0:2048]
    ksin = vim[:, 0:1024]
    sre = S(sb("sre", [128, 4 * CHUNK], F32))
    sim = S(sb("sim", [128, 4 * CHUNK], F32))
    carryS = S(sb("carryS", [128, 8], F32))
    onesb = S(sb("onesb", [128, CHUNK], F32))
    hre_bf = S(sb("hre_bf", [128, 4 * CHUNK], BF16))
    youtC = S(sb("youtC", [128, 8 * CHUNK], F32))
    dscr = S(sb("dscr", [128, D_MODEL], F32))
    xres = [S(sb(f"xres{i}", [128, D_MODEL], F32)) for i in range(4)]

    pB = S(ps("pB", [128, 512], F32))
    pY = S(ps("pY", [128, 512], F32))
    pW = S(ps("pW", [128, 512], F32))
    pTR = [S(ps("pTRa", [128, 128], BF16)), S(ps("pTRb", [128, 128], BF16))]
    pTRF = [S(ps("pTRFa", [128, 128], F32)), S(ps("pTRFb", [128, 128], F32))]

    E = P.emit
    last = {}

    t_consts = []
    for dst, srcd in [(eyebf, eyebf_d), (eyef, eyef_d), (ones1, ones1_d), (apar, apar_d)]:
        t_consts.append(E("sync", lambda e, dst=dst, srcd=srcd: e.dma_start(out=dst[:], in_=srcd),
                          mark=True, dma=True))
    t_consts.append(E("sync", lambda e: e.dma_start(
        out=angs[:, 0:1024], in_=ang1_d.rearrange("(k p) m -> p k m", p=128)), mark=True, dma=True))
    t_consts.append(E("sync", lambda e: e.dma_start(
        out=angs[:, 1024:2048], in_=ang2_d.rearrange("(k p) m -> p k m", p=128)), mark=True, dma=True))

    stage_f = youtC[:, 0:2048]
    stage_bv = youtC[:, 2048:3072].bitcast(BF16)
    pieces = []
    for k in range(8):
        pieces.append((bwt_d[k * 128:(k + 1) * 128, :], bwtbf_d[k * 128:(k + 1) * 128, :], 512))
    for k in range(4):
        pieces.append((cwt_d[k * 128:(k + 1) * 128, :], cwtbf_d[k * 128:(k + 1) * 128, :], 1024))
    t_conv = emit_w_convert(P, E, pieces, stage_f, stage_bv)
    t_res = []
    for dst, srcd in [(bwt_sb, bwtbf_d), (cwt_sb, cwtbf_d)]:
        t_res.append(E("sync", lambda e, dst=dst, srcd=srcd: e.dma_start(
            out=dst[:], in_=srcd.rearrange("(k p) m -> p k m", p=128)), deps=[t_conv],
            mark=True, dma=True))
    tks2 = []
    for k in range(2):
        t1 = E("scalar", lambda e, k=k: e.activation(
            ksin[:, k * 512:(k + 1) * 512], angs[:, k * 512:(k + 1) * 512], AF.Sin),
            deps=t_consts, mark=True)
        t2 = E("scalar", lambda e, k=k: e.activation(
            tsc1[:, 0:512], angs[:, 1024 + k * 512: 1024 + (k + 1) * 512], AF.Sin),
            deps=[d for d in (tks2[-1] if tks2 else None,) if d], mark=True)
        t3 = E("vector", lambda e, k=k: e.tensor_add(
            ksin[:, k * 512:(k + 1) * 512], ksin[:, k * 512:(k + 1) * 512], tsc1[:, 0:512]),
            deps=[t1, t2], mark=True)
        t4 = E("vector", lambda e, k=k: e.tensor_scalar_mul(
            ksin[:, k * 512:(k + 1) * 512], ksin[:, k * 512:(k + 1) * 512], -1.0), mark=True)
        tks2.append(t4)
    t_om = []
    for m in range(4):
        tmm = None
        for k in range(2):
            tmm = E("tensor", lambda e, k=k, m=m: e.matmul(
                pY[:, 0:1], ksin[:, k * 512 + m * 128: k * 512 + (m + 1) * 128],
                apar[:, k:k + 1], start=(k == 0), stop=(k == 1)),
                deps=(tks2 + t_consts + ([last.get("pYom")] if last.get("pYom") else [])
                      if k == 0 else []), mark=(k == 1))
        tev = E("vector", lambda e, m=m: e.tensor_copy(out=womega[:, m:m + 1], in_=pY[:, 0:1]),
                deps=[tmm], mark=True)
        last["pYom"] = tev
        t_om.append(tev)
    P.drain("vector")
    tw1 = E("vector", lambda e: e.tensor_tensor(womega[:, 12:16], womega[:, 0:4], womega[:, 0:4], ALU.mult),
            deps=t_om, mark=True)
    P.drain("vector")
    E("vector", lambda e: e.tensor_scalar_add(womega[:, 12:16], womega[:, 12:16], 1.0))
    P.drain("vector")
    trc = E("vector", lambda e: e.reciprocal(womega[:, 12:16], womega[:, 12:16]), mark=True)
    P.drain("vector")
    E("vector", lambda e: e.tensor_scalar(tsc1[:, 0:4], womega[:, 0:4], -2.0, None, ALU.mult, ALU.bypass))
    P.drain("vector")
    E("vector", lambda e: e.tensor_tensor(womega[:, 8:12], tsc1[:, 0:4], womega[:, 12:16], ALU.mult))
    P.drain("vector")
    E("vector", lambda e: e.tensor_tensor(tsc1[:, 4:8], womega[:, 0:4], womega[:, 0:4], ALU.mult))
    P.drain("vector")
    E("vector", lambda e: e.tensor_scalar(tsc1[:, 4:8], tsc1[:, 4:8], -1.0, 1.0, ALU.mult, ALU.add))
    P.drain("vector")
    teig = E("vector", lambda e: e.tensor_tensor(womega[:, 4:8], tsc1[:, 4:8], womega[:, 12:16], ALU.mult),
             mark=True)
    P.drain("vector")
    t_tab = []
    for nt in range(4):
        tz1 = E("vector", lambda e, nt=nt: e.memset(Tre[nt][:, 0:1], 1.0), deps=[teig])
        tz2 = E("vector", lambda e, nt=nt: e.memset(Tim[nt][:, 0:1], 0.0))
        P.drain("vector")
        E("vector", lambda e, nt=nt: e.tensor_copy(out=Tre[nt][:, 1:2], in_=womega[:, 4 + nt:5 + nt]))
        E("vector", lambda e, nt=nt: e.tensor_copy(out=Tim[nt][:, 1:2], in_=womega[:, 8 + nt:9 + nt]))
        E("vector", lambda e, nt=nt: e.tensor_copy(out=qcol[:, 0:1], in_=womega[:, 4 + nt:5 + nt]))
        tq = E("vector", lambda e, nt=nt: e.tensor_copy(out=qcol[:, 1:2], in_=womega[:, 8 + nt:9 + nt]),
               mark=True)
        P.drain("vector")
        for j in range(1, 9):
            w = 1 << j
            E("vector", lambda e: e.tensor_tensor(qcol[:, 2:3], qcol[:, 0:1], qcol[:, 0:1], ALU.mult))
            E("vector", lambda e: e.tensor_tensor(qcol[:, 3:4], qcol[:, 1:2], qcol[:, 1:2], ALU.mult))
            E("vector", lambda e: e.tensor_tensor(qcol[:, 4:5], qcol[:, 0:1], qcol[:, 1:2], ALU.mult))
            P.drain("vector")
            E("vector", lambda e: e.tensor_sub(qcol[:, 0:1], qcol[:, 2:3], qcol[:, 3:4]))
            E("vector", lambda e: e.tensor_scalar_mul(qcol[:, 1:2], qcol[:, 4:5], 2.0))
            P.drain("vector")
            E("vector", lambda e, nt=nt, w=w: e.tensor_scalar_mul(tsc1[:, 0:w], Tre[nt][:, 0:w], qcol[:, 0:1]))
            E("vector", lambda e, nt=nt, w=w: e.tensor_scalar_mul(tsc2[:, 0:w], Tim[nt][:, 0:w], qcol[:, 1:2]))
            P.drain("vector")
            E("vector", lambda e, nt=nt, w=w: e.tensor_sub(Tre[nt][:, w:2 * w], tsc1[:, 0:w], tsc2[:, 0:w]))
            E("vector", lambda e, nt=nt, w=w: e.tensor_scalar_mul(tsc1[:, 0:w], Tre[nt][:, 0:w], qcol[:, 1:2]))
            E("vector", lambda e, nt=nt, w=w: e.tensor_scalar_mul(tsc2[:, 0:w], Tim[nt][:, 0:w], qcol[:, 0:1]))
            P.drain("vector")
            E("vector", lambda e, nt=nt, w=w: e.tensor_add(Tim[nt][:, w:2 * w], tsc1[:, 0:w], tsc2[:, 0:w]))
            P.drain("vector")
        P.drain("vector")
        E("vector", lambda e, nt=nt: e.tensor_scalar_mul(qcol[:, 2:3], Tre[nt][:, 511:512], 1.0))
        E("vector", lambda e, nt=nt: e.tensor_scalar_mul(qcol[:, 3:4], Tim[nt][:, 511:512], 1.0))
        P.drain("vector")
        E("vector", lambda e, nt=nt: e.tensor_tensor(qcol[:, 4:5], qcol[:, 2:3], womega[:, 4 + nt:5 + nt], ALU.mult))
        E("vector", lambda e, nt=nt: e.tensor_tensor(qcol[:, 5:6], qcol[:, 3:4], womega[:, 8 + nt:9 + nt], ALU.mult))
        E("vector", lambda e, nt=nt: e.tensor_tensor(qcol[:, 6:7], qcol[:, 2:3], womega[:, 8 + nt:9 + nt], ALU.mult))
        E("vector", lambda e, nt=nt: e.tensor_tensor(qcol[:, 7:8], qcol[:, 3:4], womega[:, 4 + nt:5 + nt], ALU.mult))
        P.drain("vector")
        E("vector", lambda e, nt=nt: e.tensor_sub(ocol[:, 2 * nt if False else 0:1], qcol[:, 4:5], qcol[:, 5:6]) if False else e.tensor_sub(tsc1[:, 0:1], qcol[:, 4:5], qcol[:, 5:6]))
        E("vector", lambda e, nt=nt: e.tensor_add(tsc1[:, 1:2], qcol[:, 6:7], qcol[:, 7:8]))
        P.drain("vector")
        E("vector", lambda e, nt=nt: e.tensor_copy(out=ocols[nt][:, 0:1], in_=tsc1[:, 0:1]))
        tt2 = E("vector", lambda e, nt=nt: e.tensor_copy(out=ocols[nt][:, 1:2], in_=tsc1[:, 1:2]), mark=True)
        P.drain("vector")
        E("vector", lambda e, nt=nt: e.tensor_tensor(qcol[:, 2:3], ocols[nt][:, 0:1], ocols[nt][:, 0:1], ALU.mult))
        E("vector", lambda e, nt=nt: e.tensor_tensor(qcol[:, 3:4], ocols[nt][:, 1:2], ocols[nt][:, 1:2], ALU.mult))
        E("vector", lambda e, nt=nt: e.tensor_tensor(qcol[:, 4:5], ocols[nt][:, 0:1], ocols[nt][:, 1:2], ALU.mult))
        P.drain("vector")
        E("vector", lambda e, nt=nt: e.tensor_sub(ocols[nt][:, 2:3], qcol[:, 2:3], qcol[:, 3:4]))
        E("vector", lambda e, nt=nt: e.tensor_scalar_mul(ocols[nt][:, 3:4], qcol[:, 4:5], 2.0))
        P.drain("vector")
        E("vector", lambda e, nt=nt: e.tensor_tensor(qcol[:, 2:3], ocols[nt][:, 2:3], ocols[nt][:, 0:1], ALU.mult))
        E("vector", lambda e, nt=nt: e.tensor_tensor(qcol[:, 3:4], ocols[nt][:, 3:4], ocols[nt][:, 1:2], ALU.mult))
        E("vector", lambda e, nt=nt: e.tensor_tensor(qcol[:, 4:5], ocols[nt][:, 2:3], ocols[nt][:, 1:2], ALU.mult))
        E("vector", lambda e, nt=nt: e.tensor_tensor(qcol[:, 5:6], ocols[nt][:, 3:4], ocols[nt][:, 0:1], ALU.mult))
        P.drain("vector")
        E("vector", lambda e, nt=nt: e.tensor_sub(ocols[nt][:, 4:5], qcol[:, 2:3], qcol[:, 3:4]))
        ttab = E("vector", lambda e, nt=nt: e.tensor_add(ocols[nt][:, 5:6], qcol[:, 4:5], qcol[:, 5:6]), mark=True)
        P.drain("vector")
        t_tab.append(ttab)
    tz = E("vector", lambda e: e.memset(onesb[:], 1.0), mark=True)
    tdl = E("sync", lambda e: e.dma_start(out=dscr[0:1, :], in_=dvec_d), mark=True, dma=True)
    tdm1 = E("tensor", lambda e: e.matmul(pB[:], ones1[:], dscr[0:1, 0:512], start=True, stop=True),
             deps=[tdl], mark=True)
    tdm1e = E("scalar", lambda e: e.copy(out=dbc[:, 0:512], in_=pB[:]), deps=[tdm1], mark=True)
    tdm2 = E("tensor", lambda e: e.matmul(pB[:], ones1[:], dscr[0:1, 512:1024], start=True, stop=True),
             deps=[tdm1e], mark=True)
    tdm2e = E("scalar", lambda e: e.copy(out=dbc[:, 512:1024], in_=pB[:]), deps=[tdm2], mark=True)
    last["pB"] = tdm2e

    ctx = {"xt": xt, "ln_s1": ln_s1, "ln_s2": ln_s2, "ln_mean": ln_mean, "ln_var": ln_var,
           "ln_msq": ln_msq, "ln_rstd": ln_rstd, "ln_sq": xln[0], "xln": xln, "xln_bf": xln_bf,
           "xfm": xfm, "eyebf": eyebf, "eyef": eyef, "pTR": pTR, "pTRF": pTRF, "xres": xres,
           "keep_tm": xlntm}

    for b in range(B_PER_CORE):
        tzc = E("vector", lambda e: e.memset(carryS[:], 0.0), mark=True)
        for cch in range(4):
            tok0 = b * SEQ + cch * CHUNK
            t_xfm = emit_ln_to_fm(P, E, last, ctx, x2_d, tok0)
            if cch == 0:
                tbl_re, tbl_im = Tre, Tim
                t_tbl = t_tab
            else:
                tbl_re, tbl_im = Tcre, Tcim
                t_tbl = []
                oc = cch - 1
                for nt in range(4):
                    d0 = [t_tab[nt]] + ([last.get(("tc", nt))] if last.get(("tc", nt)) else [])
                    E("vector", lambda e, nt=nt, oc=oc: e.tensor_scalar_mul(
                        tsc1[:, 0:512], Tre[nt][:], ocols[nt][:, 2 * oc:2 * oc + 1]), deps=d0)
                    E("vector", lambda e, nt=nt, oc=oc: e.tensor_scalar_mul(
                        tsc2[:, 0:512], Tim[nt][:], ocols[nt][:, 2 * oc + 1:2 * oc + 2]))
                    P.drain("vector")
                    E("vector", lambda e, nt=nt: e.tensor_sub(Tcre[nt][:], tsc1[:, 0:512], tsc2[:, 0:512]))
                    E("vector", lambda e, nt=nt, oc=oc: e.tensor_scalar_mul(
                        tsc1[:, 0:512], Tre[nt][:], ocols[nt][:, 2 * oc + 1:2 * oc + 2]))
                    E("vector", lambda e, nt=nt, oc=oc: e.tensor_scalar_mul(
                        tsc2[:, 0:512], Tim[nt][:], ocols[nt][:, 2 * oc:2 * oc + 1]))
                    P.drain("vector")
                    ttc = E("vector", lambda e, nt=nt: e.tensor_add(
                        Tcim[nt][:], tsc1[:, 0:512], tsc2[:, 0:512]), mark=True)
                    t_tbl.append(ttc)
            t_bu = []
            for mn in range(4):
                tmm = None
                for k in range(8):
                    tmm = E("tensor", lambda e, k=k, mn=mn: e.matmul(
                        pB[:], bwt_sb[:, k * 512 + mn * 128: k * 512 + (mn + 1) * 128],
                        xfm[:, k * CHUNK:(k + 1) * CHUNK], start=(k == 0), stop=(k == 7)),
                        deps=(t_xfm + t_res + [last.get("pB")] if k == 0 else []), mark=(k == 7))
                tev = E("scalar", lambda e, mn=mn: e.copy(
                    out=bu[:, mn * CHUNK:(mn + 1) * CHUNK], in_=pB[:]),
                    deps=[tmm] + ([last.get(("bu", mn))] if last.get(("bu", mn)) else []), mark=True)
                last["pB"] = tev
                t_bu.append(tev)
            t_v = []
            for mn in range(4):
                tv1 = E("vector", lambda e, mn=mn, T=tbl_re[mn]: e.tensor_tensor(
                    vre[:, mn * CHUNK:(mn + 1) * CHUNK], bu[:, mn * CHUNK:(mn + 1) * CHUNK],
                    T[:], ALU.mult),
                    deps=[t_bu[mn]] + t_tbl + ([last.get(("v", mn))] if last.get(("v", mn)) else []),
                    mark=True)
                tv2 = E("vector", lambda e, mn=mn, T=tbl_im[mn]: e.scalar_tensor_tensor(
                    vim[:, mn * CHUNK:(mn + 1) * CHUNK], bu[:, mn * CHUNK:(mn + 1) * CHUNK],
                    -1.0, T[:], ALU.mult, ALU.mult), mark=True)
                t_v.append(tv2)
            t_dmp = []
            if dbg and b == 0 and cch == 0:
                t_dmp.append(E("sync", lambda e: e.dma_start(out=dbg_vre, in_=vre[:]), deps=t_v, mark=True, dma=True))
                t_dmp.append(E("sync", lambda e: e.dma_start(out=dbg_vim, in_=vim[:]), deps=t_v, mark=True, dma=True))
            t_s = []
            for mn in range(4):
                ts1_ = E("vector", lambda e, mn=mn: e.tensor_tensor_scan(
                    sre[:, mn * CHUNK:(mn + 1) * CHUNK], onesb[:], vre[:, mn * CHUNK:(mn + 1) * CHUNK],
                    carryS[:, mn:mn + 1], ALU.mult, ALU.add), deps=[t_v[mn], tz, tzc] + t_dmp, mark=True)
                ts2_ = E("vector", lambda e, mn=mn: e.tensor_tensor_scan(
                    sim[:, mn * CHUNK:(mn + 1) * CHUNK], onesb[:], vim[:, mn * CHUNK:(mn + 1) * CHUNK],
                    carryS[:, 4 + mn:5 + mn], ALU.mult, ALU.add), mark=True)
                t_s.append(ts2_)
            P.drain("vector")
            tcar1 = E("vector", lambda e: e.tensor_copy(
                out=carryS[:, 0:4], in_=sre[:, CHUNK - 1:4 * CHUNK:CHUNK]), deps=t_s, mark=True)
            tcar2 = E("vector", lambda e: e.tensor_copy(
                out=carryS[:, 4:8], in_=sim[:, CHUNK - 1:4 * CHUNK:CHUNK]), mark=True)
            P.drain("vector")
            t_h = []
            for mn in range(4):
                th1 = E("vector", lambda e, mn=mn, T=tbl_re[mn]: e.tensor_tensor(
                    vre[:, mn * CHUNK:(mn + 1) * CHUNK], T[:],
                    sre[:, mn * CHUNK:(mn + 1) * CHUNK], ALU.mult), deps=[tcar1, tcar2], mark=True)
                th2 = E("vector", lambda e, mn=mn, T=tbl_im[mn]: e.scalar_tensor_tensor(
                    vim[:, mn * CHUNK:(mn + 1) * CHUNK], T[:],
                    -1.0, sim[:, mn * CHUNK:(mn + 1) * CHUNK], ALU.mult, ALU.mult), mark=True)
                th3 = E("vector", lambda e, mn=mn: e.tensor_tensor(
                    hre_bf[:, mn * CHUNK:(mn + 1) * CHUNK], vre[:, mn * CHUNK:(mn + 1) * CHUNK],
                    vim[:, mn * CHUNK:(mn + 1) * CHUNK], ALU.add),
                    deps=[th2] + ([last.get(("h", mn))] if last.get(("h", mn)) else []), mark=True)
                last[("v", mn)] = th3
                t_h.append(th3)
            t_yev = []
            for m in range(8):
                tmm = None
                for k in range(4):
                    tmm = E("tensor", lambda e, k=k, m=m: e.matmul(
                        pW[:], cwt_sb[:, k * 1024 + m * 128: k * 1024 + (m + 1) * 128],
                        hre_bf[:, k * CHUNK:(k + 1) * CHUNK], start=(k == 0), stop=(k == 3)),
                        deps=(t_h + ([last.get("pW")] if last.get("pW") else []) if k == 0 else []),
                        mark=(k == 3))
                tev = E("scalar", lambda e, m=m: e.copy(
                    out=youtC[:, m * CHUNK:(m + 1) * CHUNK], in_=pW[:]),
                    deps=[tmm] + ([last.get(("src_fm_rd", m))] if last.get(("src_fm_rd", m)) else []),
                    mark=True)
                last["pW"] = tev
                t_yev.append(tev)
            for mn in range(4):
                last[("h", mn)] = t_yev[-1]
                last[("bu", mn)] = t_v[mn]
                last[("tc", mn)] = t_h[mn]
            if dbg and b == 0 and cch == 0:
                for nap, tile in [(dbg_w, womega[:]), (dbg_tre, Tre[0][:]), (dbg_tim, Tim[0][:]),
                                  (dbg_bu, bu[:]), (dbg_sre, sre[:]), (dbg_h, hre_bf[:]),
                                  (dbg_oc, ocols[0][:])]:
                    E("sync", lambda e, nap=nap, tile=tile: e.dma_start(out=nap, in_=tile),
                      deps=t_yev, mark=True, dma=True)
            src_views = [youtC[:, m * CHUNK:(m + 1) * CHUNK] for m in range(8)]
            stores = emit_residual_add_store(P, E, last, ctx, x2_d, x3_d, tok0, src_views, t_yev,
                                             du=(xlntm, dbc, dscr))



def emit_stage_D(nc, P, stack, d, dbg=False):
    x3_d = d["x3"]
    rw_d = d["rw"]
    sgw_d = d["sgw"]
    suw_d = d["suw"]
    sdw_d = d["sdw"]
    eyebf_d = d["eye_bf"]
    eyef_d = d["eye_f"]
    ones1_d = d["ones1"]
    onescol_d = d["onescol"]
    iota64_d = d["iota64"]
    trils_d = d["trilS"]
    gidx_d = d["gidx"]
    pout_d = d["pout"]
    xnbf_d = d["xnbf"]
    counts_d = d["counts"]
    ek_d = d["ek"]
    pl_d = d["pl"]
    wt_d = d["wt"]
    mapl_d = d["mapl"]
    rwbf_d = d["rwbf"]
    sgwbf_d = d["sgwbf"]
    suwbf_d = d["suwbf"]
    sdwbf_d = d["sdwbf"]
    for _k in d:
        pass
    S = stack.enter_context
    sb = lambda _n, _s, _t: nc.sbuf_tensor("D_" + _n, _s, _t)
    ps = lambda _n, _s, _t: nc.psum_tensor("D_" + _n, _s, _t)

    eyebf = S(sb("s_eyebf", [128, 128], BF16))
    eyef = S(sb("s_eyef", [128, 128], F32))
    ones1 = S(sb("s_ones1", [1, 128], F32))
    onescol = S(sb("s_onescol", [128, 1], F32))
    iota64 = S(sb("s_iota64", [128, 64], F32))
    trils = S(sb("s_trils", [128, 128], F32))
    gidx = S(sb("s_gidx", [128, 32], F32))
    rw_sb = S(sb("rw_sb", [128, 8 * 64], F32))
    sgw_sb = S(sb("sgw_sb", [128, 8 * 512], BF16))
    suw_sb = S(sb("suw_sb", [128, 8 * 512], BF16))
    sdw_sb = S(sb("sdw_sb", [128, 4 * 1024], BF16))

    xt = [S(sb(f"xt{i}", [128, D_MODEL], F32)) for i in range(2)]
    ln_s1 = S(sb("ln_s1", [128, 4], F32))
    ln_s2 = S(sb("ln_s2", [128, 4], F32))
    ln_mean = S(sb("ln_mean", [128, 4], F32))
    ln_var = S(sb("ln_var", [128, 4], F32))
    ln_msq = S(sb("ln_msq", [128, 4], F32))
    ln_rstd = S(sb("ln_rstd", [128, 4], F32))
    xln = [S(sb("xln0", [128, D_MODEL], F32)), S(sb("xln1", [128, D_MODEL], F32))]
    xln_bf = [S(sb("xln_bf0", [128, D_MODEL], BF16)), S(sb("xln_bf1", [128, D_MODEL], BF16))]
    xfm = S(sb("xfm", [128, 8 * CHUNK], BF16))
    xfm32 = S(sb("xfm32", [128, 8 * CHUNK], F32))
    sgsil = S(sb("sgsil", [128, 4 * CHUNK], F32))
    supf = S(sb("supf", [128, 4 * CHUNK], F32))
    sgu_bf = S(sb("sgu_bf", [128, 4 * CHUNK], BF16))
    sh_fm = S(sb("sh_fm", [128, 8 * CHUNK], F32))
    xres = [S(sb(f"xres{i}", [128, D_MODEL], F32)) for i in range(4)]
    scores = S(sb("scores", [128, 64], F32))
    gs = S(sb("gs", [128, 64], F32))
    gmask = S(sb("gmask", [128, 8], F32))
    ms = S(sb("ms", [128, 64], F32))
    hits = S(sb("hits", [128, 8 * 64], F32))
    sel = S(sb("sel", [128, 64], F32))
    cumpos = S(sb("cumpos", [128, 64], F32))
    carry_sb = S(sb("carry_sb", [1, 64], F32))
    ekb = S(sb("ekb", [128, 8], F32))
    plb = S(sb("plb", [128, 8], F32))
    wb = S(sb("wb", [128, 8], F32))
    wcols = S(sb("wcols", [128, 4], F32))
    ttracc = S(sb("ttracc", [128, 1], F32))
    scr64 = S(sb("scr64", [128, 64], F32))
    c32k = S(sb("c32k", [128, 8], F32))
    validb = S(sb("validb", [128, 8], F32))
    slotf = S(sb("slotf", [128, 8], F32))
    sloti = S(sb("sloti", [128, 8], I32))
    init1e9 = S(sb("init1e9", [128, 257], F32))

    pSG = S(ps("pSG", [128, 512], F32))
    pUP = pSG
    pW = S(ps("pW", [128, 512], F32))
    pSMALL = S(ps("pSMALL", [128, 512], F32))
    pRT = pSMALL[:, 0:64]
    pPOS = pSMALL[:, 64:128]
    pCARt = S(ps("pCAR", [1, 64], F32))
    pCAR = pCARt[:]
    pTR = [S(ps("pTRa", [128, 128], BF16)), S(ps("pTRb", [128, 128], BF16))]
    pTRF = [S(ps("pTRFa", [128, 128], F32)), S(ps("pTRFb", [128, 128], F32))]

    E = P.emit
    last = {}

    t_consts = []
    for dst, srcd in [(eyebf, eyebf_d), (eyef, eyef_d), (ones1, ones1_d), (onescol, onescol_d),
                      (iota64, iota64_d), (trils, trils_d), (gidx, gidx_d)]:
        t_consts.append(E("sync", lambda e, dst=dst, srcd=srcd: e.dma_start(out=dst[:], in_=srcd),
                          mark=True, dma=True))
    tmz = E("vector", lambda e: e.memset(init1e9[:], 1.0e9), mark=True)
    tmi = E("sync", lambda e: e.dma_start(
        out=mapl_d.rearrange("(a b) c -> a (b c)", a=128), in_=init1e9[:]), deps=[tmz],
        mark=True, dma=True)
    tcz = E("vector", lambda e: e.memset(c32k[:], 32768.0), mark=True)

    stage_f = sh_fm[:, 0:2048]
    stage_bv = sh_fm[:, 2048:3072].bitcast(BF16)
    pieces = []
    for k in range(8):
        pieces.append((sgw_d[k * 128:(k + 1) * 128, :], sgwbf_d[k * 128:(k + 1) * 128, :], 512))
    for k in range(8):
        pieces.append((suw_d[k * 128:(k + 1) * 128, :], suwbf_d[k * 128:(k + 1) * 128, :], 512))
    for k in range(4):
        pieces.append((sdw_d[k * 128:(k + 1) * 128, :], sdwbf_d[k * 128:(k + 1) * 128, :], 1024))
    t_conv = emit_w_convert(P, E, pieces, stage_f, stage_bv)
    t_res = []
    t_res.append(E("sync", lambda e: e.dma_start(
        out=rw_sb[:], in_=rw_d.rearrange("(k p) m -> p k m", p=128)), mark=True, dma=True))
    for dst, srcd in [(sgw_sb, sgwbf_d), (suw_sb, suwbf_d), (sdw_sb, sdwbf_d)]:
        t_res.append(E("sync", lambda e, dst=dst, srcd=srcd: e.dma_start(
            out=dst[:], in_=srcd.rearrange("(k p) m -> p k m", p=128)), deps=[t_conv],
            mark=True, dma=True))

    ctx = {"xt": xt, "ln_s1": ln_s1, "ln_s2": ln_s2, "ln_mean": ln_mean, "ln_var": ln_var,
           "ln_msq": ln_msq, "ln_rstd": ln_rstd, "ln_sq": xln[0], "xln": xln, "xln_bf": xln_bf,
           "xfm": xfm, "eyebf": eyebf, "eyef": eyef, "pTR": pTR, "pTRF": pTRF, "xres": xres,
           "store_bf_rows": xnbf_d, "xfm32": xfm32}

    t_car_last = None
    D_ = P.drain
    for ch in range(8):
        tok0 = ch * CHUNK
        t_xfm = emit_ln_to_fm(P, E, last, ctx, x3_d, tok0)
        t_sg = []
        for m in range(4):
            tmm = None
            for k in range(8):
                tmm = E("tensor", lambda e, k=k, m=m: e.matmul(
                    pSG[:], sgw_sb[:, k * 512 + m * 128: k * 512 + (m + 1) * 128],
                    xfm[:, k * CHUNK:(k + 1) * CHUNK], start=(k == 0), stop=(k == 7)),
                    deps=(t_xfm + t_res + [t for t in (last.get("pSG"), last.get("pUP")) if t]
                          if k == 0 else []), mark=(k == 7))
            tev = E("scalar", lambda e, m=m: e.activation(
                sgsil[:, m * CHUNK:(m + 1) * CHUNK], pSG[:], AF.Silu),
                deps=[tmm] + ([last.get(("sg", m))] if last.get(("sg", m)) else []), mark=True)
            last["pSG"] = tev
            last["pUP"] = tev
            t_sg.append(tev)
        t_up = []
        for m in range(4):
            tmm = None
            for k in range(8):
                tmm = E("tensor", lambda e, k=k, m=m: e.matmul(
                    pUP[:], suw_sb[:, k * 512 + m * 128: k * 512 + (m + 1) * 128],
                    xfm[:, k * CHUNK:(k + 1) * CHUNK], start=(k == 0), stop=(k == 7)),
                    deps=([last.get("pUP"), last.get("pSG")] if k == 0 else []), mark=(k == 7))
            tev = E("scalar", lambda e, m=m: e.copy(
                out=supf[:, m * CHUNK:(m + 1) * CHUNK], in_=pUP[:]),
                deps=[tmm] + ([last.get(("up", m))] if last.get(("up", m)) else []), mark=True)
            last["pUP"] = tev
            last["pSG"] = tev
            t_up.append(tev)
        t_gu = []
        for m in range(4):
            tgu = E("vector", lambda e, m=m: e.tensor_tensor(
                sgu_bf[:, m * CHUNK:(m + 1) * CHUNK], sgsil[:, m * CHUNK:(m + 1) * CHUNK],
                supf[:, m * CHUNK:(m + 1) * CHUNK], ALU.mult),
                deps=[t_sg[m], t_up[m]] + ([last.get(("gu", m))] if last.get(("gu", m)) else []),
                mark=True)
            t_gu.append(tgu)
        for m in range(4):
            last[("sg", m)] = t_gu[m]
            last[("up", m)] = t_gu[m]
        t_yev = []
        for m in range(8):
            tmm = None
            for k in range(4):
                tmm = E("tensor", lambda e, k=k, m=m: e.matmul(
                    pW[:], sdw_sb[:, k * 1024 + m * 128: k * 1024 + (m + 1) * 128],
                    sgu_bf[:, k * CHUNK:(k + 1) * CHUNK], start=(k == 0), stop=(k == 3)),
                    deps=(t_gu + ([last.get("pW")] if last.get("pW") else []) if k == 0 else []),
                    mark=(k == 3))
            tev = E("scalar", lambda e, m=m: e.copy(
                out=sh_fm[:, m * CHUNK:(m + 1) * CHUNK], in_=pW[:]),
                deps=[tmm] + ([last.get(("src_fm_rd", m))] if last.get(("src_fm_rd", m)) else []),
                mark=True)
            last["pW"] = tev
            t_yev.append(tev)
        for m in range(4):
            last[("gu", m)] = t_yev[-1]

        for tt in range(4):
            tile = ch * 4 + tt
            tmm = None
            for k in range(8):
                tmm = E("tensor", lambda e, k=k, tt=tt: e.matmul(
                    pRT, xfm32[:, k * CHUNK + tt * 128: k * CHUNK + (tt + 1) * 128],
                    rw_sb[:, k * 64:(k + 1) * 64], start=(k == 0), stop=(k == 7)),
                    deps=(t_xfm + ([last.get("pRT")] if last.get("pRT") else []) if k == 0 else []),
                    mark=(k == 7))
            tsc = E("scalar", lambda e: e.activation(scores[:], pRT, AF.Sigmoid),
                    deps=[tmm] + ([last.get("scores_rd")] if last.get("scores_rd") else []),
                    mark=True)
            last["pRT"] = tsc
            D_("vector")
            E("vector", lambda e: e.tensor_tensor(
                gs[:, 0:32].rearrange("p (g x) -> p g x", g=8),
                scores[:].rearrange("p (g x) -> p g x", g=8)[:, :, 0:4],
                scores[:].rearrange("p (g x) -> p g x", g=8)[:, :, 4:8], ALU.max), deps=[tsc])
            D_("vector")
            E("vector", lambda e: e.tensor_tensor(
                gs[:, 32:48].rearrange("p (g x) -> p g x", g=8),
                gs[:, 0:32].rearrange("p (g x) -> p g x", g=8)[:, :, 0:2],
                gs[:, 0:32].rearrange("p (g x) -> p g x", g=8)[:, :, 2:4], ALU.max))
            D_("vector")
            E("vector", lambda e: e.tensor_tensor(
                gs[:, 48:56].rearrange("p (g x) -> p g x", g=8),
                gs[:, 32:48].rearrange("p (g x) -> p g x", g=8)[:, :, 0:1],
                gs[:, 32:48].rearrange("p (g x) -> p g x", g=8)[:, :, 1:2], ALU.max))
            D_("vector")
            E("vector", lambda e: e.memset(gmask[:], 0.0))
            D_("vector")
            for it in range(4):
                E("vector", lambda e: e.reduce_max(gs[:, 56:57], gs[:, 48:56], axis=AX.X))
                D_("vector")
                E("vector", lambda e: e.tensor_single_scalar(gs[:, 32:40], gs[:, 48:56],
                                                             gs[:, 56:57], ALU.is_ge))
                D_("vector")
                E("vector", lambda e: e.tensor_add(gmask[:], gmask[:], gs[:, 32:40]))
                E("vector", lambda e: e.scalar_tensor_tensor(
                    gs[:, 48:56], gs[:, 32:40], -1.0e9, gs[:, 48:56], ALU.mult, ALU.add))
                D_("vector")
            tms = E("vector", lambda e: e.tensor_tensor(
                ms[:].rearrange("p (g x) -> p g x", g=8),
                scores[:].rearrange("p (g x) -> p g x", g=8),
                gmask[:].unsqueeze(2).broadcast_to((128, 8, 8)), ALU.mult), mark=True)
            last["scores_rd"] = tms
            D_("vector")
            tselz = E("vector", lambda e: e.memset(sel[:], 0.0), mark=True)
            D_("vector")
            for kk in range(8):
                E("vector", lambda e: e.reduce_max(gs[:, 56:57], ms[:], axis=AX.X))
                D_("vector")
                E("vector", lambda e, kk=kk: e.tensor_single_scalar(
                    hits[:, kk * 64:(kk + 1) * 64], ms[:], gs[:, 56:57], ALU.is_ge))
                D_("vector")
                E("vector", lambda e, kk=kk: e.tensor_copy(out=wb[:, kk:kk + 1], in_=gs[:, 56:57]))
                E("vector", lambda e, kk=kk: e.tensor_add(
                    sel[:], sel[:], hits[:, kk * 64:(kk + 1) * 64]))
                E("vector", lambda e, kk=kk: e.scalar_tensor_tensor(
                    ms[:], hits[:, kk * 64:(kk + 1) * 64], -1.0e9, ms[:], ALU.mult, ALU.add))
                D_("vector")
                E("vector", lambda e, kk=kk: e.tensor_tensor(
                    scr64[:], hits[:, kk * 64:(kk + 1) * 64], iota64[:], ALU.mult), deps=t_consts)
                D_("vector")
                E("vector", lambda e, kk=kk: e.reduce_sum(ekb[:, kk:kk + 1], scr64[:], axis=AX.X))
                D_("vector")
            tsel = E("vector", lambda e: e.tensor_copy(out=scr64[:], in_=sel[:]), mark=True)
            if tile == 0:
                tce = E("vector", lambda e: e.memset(carry_sb[:], 0.0), mark=True)
            else:
                tce = E("scalar", lambda e: e.copy(out=carry_sb[:], in_=pCAR),
                        deps=([t_car_last] if t_car_last else []) +
                             ([last.get("carry_rd")] if last.get("carry_rd") else []), mark=True)
            tp1 = E("tensor", lambda e: e.matmul(pPOS, trils[:], sel[:], start=True, stop=False,
                                                 skip_group_check=True),
                    deps=[tsel, tce] + t_consts + ([last.get("pPOS")] if last.get("pPOS") else []),
                    mark=True)
            tp2 = E("tensor", lambda e: e.matmul(pPOS, ones1[:], carry_sb[:], start=False,
                                                 stop=True, skip_group_check=True),
                    deps=[tce], mark=True)
            tcp = E("scalar", lambda e: e.copy(out=cumpos[:], in_=pPOS),
                    deps=[tp2] + ([last.get("cumpos_rd")] if last.get("cumpos_rd") else []),
                    mark=True)
            last["pPOS"] = tcp
            t_car_last = E("tensor", lambda e, tile=tile: e.matmul(
                pCAR, onescol[:], sel[:], start=(tile == 0), stop=(tile == 31),
                skip_group_check=True),
                deps=[tsel, tcp] + t_consts, mark=True)
            last["carry_rd"] = t_car_last
            for kk in range(8):
                E("vector", lambda e, kk=kk: e.tensor_tensor(
                    scr64[:], hits[:, kk * 64:(kk + 1) * 64], cumpos[:], ALU.mult), deps=[tcp])
                D_("vector")
                E("vector", lambda e, kk=kk: e.reduce_sum(plb[:, kk:kk + 1], scr64[:], axis=AX.X))
                D_("vector")
            E("vector", lambda e: e.reduce_sum(wcols[:, 0:1], wb[:], axis=AX.X))
            D_("vector")
            E("vector", lambda e: e.tensor_scalar_max(wcols[:, 1:2], wcols[:, 0:1], 1e-9))
            D_("vector")
            E("vector", lambda e: e.reciprocal(wcols[:, 1:2], wcols[:, 1:2]))
            D_("vector")
            twt = E("vector", lambda e: e.tensor_scalar(
                wb[:], wb[:], wcols[:, 1:2], 2.5, ALU.mult, ALU.mult), mark=True)
            D_("vector")
            tval = E("vector", lambda e: e.tensor_single_scalar(validb[:], plb[:], 512.0, ALU.is_lt),
                     mark=True)
            D_("vector")
            E("vector", lambda e: e.tensor_scalar(slotf[:], ekb[:], 512.0, None, ALU.mult, ALU.bypass))
            D_("vector")
            E("vector", lambda e: e.tensor_add(slotf[:], slotf[:], plb[:]))
            D_("vector")
            E("vector", lambda e: e.tensor_scalar_add(slotf[:], slotf[:], -32768.0))
            D_("vector")
            E("vector", lambda e: e.tensor_tensor(slotf[:], slotf[:], validb[:], ALU.mult), deps=[tval])
            D_("vector")
            E("vector", lambda e: e.tensor_scalar_add(slotf[:], slotf[:], 32768.0))
            D_("vector")
            tsl = E("vector", lambda e: e.tensor_copy(out=sloti[:], in_=slotf[:]), mark=True)
            tscat = None
            for kk in range(8):
                tscat = E("gpsimd", lambda e, kk=kk, tile=tile: e.indirect_dma_start(
                    out=mapl_d, out_offset=IndirectOffsetOnAxis(sloti[:, kk:kk + 1], 0),
                    in_=gidx[:, tile:tile + 1], in_offset=None),
                    deps=[tsl, tmi] + t_consts, mark=True, dma=True)
            last["sloti_rd"] = tscat
            row0 = tile * 128
            tse = E("sync", lambda e, row0=row0: e.dma_start(out=ek_d[row0:row0 + 128, :], in_=ekb[:]),
                    deps=[t_car_last], mark=True, dma=True)
            tsp = E("sync", lambda e, row0=row0: e.dma_start(out=pl_d[row0:row0 + 128, :], in_=plb[:]),
                    deps=[tval], mark=True, dma=True)
            tsw = E("sync", lambda e, row0=row0: e.dma_start(out=wt_d[row0:row0 + 128, :], in_=wb[:]),
                    deps=[twt], mark=True, dma=True)
            last["scores_rd"] = tse
            last["cumpos_rd"] = tsp
            last["wb_guard"] = tsw
            P.wait("vector", tse)
            P.wait("vector", tsp)
            P.wait("vector", tsw)
            P.wait("vector", tscat)
        src_views = [sh_fm[:, m * CHUNK:(m + 1) * CHUNK] for m in range(8)]
        emit_residual_add_store(P, E, last, ctx, x3_d, pout_d, tok0, src_views, t_yev)

    tcc = E("scalar", lambda e: e.copy(out=carry_sb[:], in_=pCAR), deps=[t_car_last], mark=True)
    E("sync", lambda e: e.dma_start(out=counts_d, in_=carry_sb[:]), deps=[tcc], mark=True, dma=True)




def build_stage_E():
    nc = bass.Bass("TRN2", target_bir_lowering=False, debug=False)
    stack = contextlib.ExitStack()

    cnt_my_d = nc.dram_tensor("cnt_my", [8, 8], F32, kind="ExternalInput").ap()
    cnt_all_d = nc.dram_tensor("cnt_all", [8, 64], F32, kind="ExternalInput").ap()
    premask_d = nc.dram_tensor("premask", [8, 1], F32, kind="ExternalInput").ap()
    lowincl_d = nc.dram_tensor("lowincl", [8, 8], F32, kind="ExternalInput").ap()
    mapl_all_d = nc.dram_tensor("mapl_all", [8 * 32896, 1], F32, kind="ExternalInput").ap()
    xall_d = nc.dram_tensor("xall", [32769, D_MODEL], BF16, kind="ExternalInput").ap()
    gw_d = nc.dram_tensor("gw", [8 * D_MODEL, 512], F32, kind="ExternalInput").ap()
    uw_d = nc.dram_tensor("uw", [8 * D_MODEL, 512], F32, kind="ExternalInput").ap()
    dw_d = nc.dram_tensor("dw", [8 * 512, D_MODEL], F32, kind="ExternalInput").ap()
    ek_d = nc.dram_tensor("ek", [T_CORE, 8], F32, kind="ExternalInput").ap()
    pl_d = nc.dram_tensor("pl", [T_CORE, 8], F32, kind="ExternalInput").ap()
    wt_d = nc.dram_tensor("wt", [T_CORE, 8], F32, kind="ExternalInput").ap()
    eyebf_d = nc.dram_tensor("eye_bf", [128, 128], BF16, kind="ExternalInput").ap()
    ones1_d = nc.dram_tensor("ones1", [1, 128], F32, kind="ExternalInput").ap()
    ppcol_d = nc.dram_tensor("ppcol", [128, 4], F32, kind="ExternalInput").ap()
    geb512_d = nc.dram_tensor("geb512", [128, 8], F32, kind="ExternalInput").ap()
    iota64_d = nc.dram_tensor("iota64", [128, 64], F32, kind="ExternalInput").ap()
    wout_d = nc.dram_tensor("wout", [T_CORE, D_MODEL], BF16, kind="ExternalOutput").ap()
    slotid_d = nc.dram_tensor("slotid", [T_CORE, 8], F32, kind="ExternalOutput").ap()
    wv_d = nc.dram_tensor("wv", [T_CORE, 8], F32, kind="ExternalOutput").ap()

    S = stack.enter_context
    sb = nc.sbuf_tensor
    ps = nc.psum_tensor

    eyebf = S(sb("s_eyebf", [128, 128], BF16))
    ones1 = S(sb("s_ones1", [1, 128], F32))
    ppcol = S(sb("s_ppcol", [128, 4], F32))
    geb512 = S(sb("s_geb512", [128, 8], F32))
    iota64 = S(sb("s_iota64", [128, 64], F32))
    cnt_my = S(sb("sb_cnt_my", [8, 8], F32))
    cnt_all = S(sb("sb_cnt_all", [8, 64], F32))
    premask = S(sb("sb_premask", [8, 1], F32))
    lowincl = S(sb("sb_lowincl", [8, 8], F32))
    cum_sb = S(sb("cum_sb", [8, 8], F32))
    cumb = S(sb("cumb", [128, 7 * 8], F32))
    countb = S(sb("countb", [128, 7 * 8], F32))
    offb64 = S(sb("offb64", [128, 64], F32))
    offrow = S(sb("offrow", [1, 64], F32))
    rowflat = S(sb("rowflat", [1, 112], F32))
    hitk = S(sb("hitk", [128, 8], F32))
    offs = S(sb("offs", [128, 8], F32))
    ksum = S(sb("ksum", [128, 8], F32))
    scr8 = S(sb("scr8", [128, 8], F32))
    gcols = S(sb("gcols", [128, 4 * 8], F32))
    gci = S(sb("gci", [128, 4 * 8], I32))
    mvf = S(sb("mvf", [128, 1], F32))
    gwe = [S(sb(f"gwe{i}", [128, 8 * 512], BF16)) for i in range(2)]
    uwe = [S(sb(f"uwe{i}", [128, 8 * 512], BF16)) for i in range(2)]
    dwe = [S(sb(f"dwe{i}", [128, 8 * 512], BF16)) for i in range(2)]
    disp_tm = [S(sb(f"disp_tm{i}", [128, D_MODEL], BF16)) for i in range(2)]
    disp_fm = S(sb("disp_fm", [128, 8 * 512], BF16))
    gsil = S(sb("gsil", [128, 4 * 512], F32))
    upf = S(sb("upf", [128, 4 * 512], F32))
    gu_bf = S(sb("gu_bf", [128, 4 * 512], BF16))
    wout_fm = S(sb("wout_fm", [128, 8 * 512], BF16))
    wout_tm = S(sb("wout_tm", [128, D_MODEL], BF16))
    ekb = S(sb("ekb", [128, 8], F32))
    plb = S(sb("plb", [128, 8], F32))
    wb = S(sb("wb", [128, 8], F32))
    ohk = S(sb("ohk", [128, 64], F32))
    scr64 = S(sb("scr64", [128, 64], F32))
    offcol = S(sb("offcol", [128, 1], F32))
    posg = S(sb("posg", [128, 8], F32))
    validb = S(sb("validb", [128, 8], F32))
    slotf = S(sb("slotf", [128, 8], F32))
    wstg = [S(sb(f"wstg{i}", [128, 8 * 512], F32)) for i in range(3)]

    pA = S(ps("pA", [128, 512], F32))
    pB2 = S(ps("pB2", [128, 512], F32))
    pC8 = S(ps("pC8", [8, 64], F32))
    pR1 = S(ps("pR1", [1, 64], F32))
    pBC = S(ps("pBC", [128, 64], F32))
    pTR = S(ps("pTR", [128, 128], BF16))

    P = Prog(nc, stack)
    E = P.emit
    last = {}
    D_ = P.drain

    t_consts = []
    for dst, srcd in [(eyebf, eyebf_d), (ones1, ones1_d), (ppcol, ppcol_d), (geb512, geb512_d),
                      (iota64, iota64_d), (cnt_my, cnt_my_d), (cnt_all, cnt_all_d),
                      (premask, premask_d), (lowincl, lowincl_d)]:
        t_consts.append(E("sync", lambda e, dst=dst, srcd=srcd: e.dma_start(out=dst[:], in_=srcd),
                          mark=True, dma=True))


    tq1 = E("tensor", lambda e: e.matmul(pC8[:, 0:8], lowincl[:], cnt_my[:], start=True, stop=True),
            deps=t_consts, mark=True)
    tq2 = E("scalar", lambda e: e.copy(out=cum_sb[:], in_=pC8[:, 0:8]), deps=[tq1], mark=True)
    tq3 = E("tensor", lambda e: e.matmul(pR1[:], premask[:], cnt_all[:], start=True, stop=True),
            deps=t_consts, mark=True)
    tq4 = E("scalar", lambda e: e.copy(out=offrow[:], in_=pR1[:]), deps=[tq3], mark=True)
    trow1 = E("sync", lambda e: e.dma_start(out=rowflat[0:1, 0:56], in_=cum_sb[0:7, :]),
              deps=[tq2], mark=True, dma=True)
    trow2 = E("sync", lambda e: e.dma_start(out=rowflat[0:1, 56:112], in_=cnt_my[0:7, :]),
              deps=t_consts, mark=True, dma=True)
    tb_toks = []
    tb1 = E("tensor", lambda e: e.matmul(pBC[:, 0:56], ones1[:], rowflat[0:1, 0:56],
                                         start=True, stop=True), deps=[trow1, trow2], mark=True)
    tb2 = E("scalar", lambda e: e.copy(out=cumb[:], in_=pBC[:, 0:56]), deps=[tb1], mark=True)
    tb3 = E("tensor", lambda e: e.matmul(pBC[:, 0:56], ones1[:], rowflat[0:1, 56:112],
                                         start=True, stop=True), deps=[tb2], mark=True)
    tb4 = E("scalar", lambda e: e.copy(out=countb[:], in_=pBC[:, 0:56]), deps=[tb3], mark=True)
    tb_toks.append(tb4)
    tb5 = E("tensor", lambda e: e.matmul(pBC[:], ones1[:], offrow[:], start=True, stop=True),
            deps=[tq4, tb4], mark=True)
    tb6 = E("scalar", lambda e: e.copy(out=offb64[:], in_=pBC[:]), deps=[tb5], mark=True)

    t_gci = []
    for jp in range(4):
        tz1 = E("vector", lambda e: e.memset(offs[:], 0.0), deps=tb_toks, mark=True)
        E("vector", lambda e: e.memset(ksum[:], 0.0))
        D_("vector")
        for k in range(7):
            E("vector", lambda e, k=k, jp=jp: e.tensor_single_scalar(
                hitk[:], cumb[:, k * 8:(k + 1) * 8], ppcol[:, jp:jp + 1], ALU.is_le))
            D_("vector")
            E("vector", lambda e, k=k: e.tensor_add(ksum[:], ksum[:], hitk[:]))
            E("vector", lambda e, k=k: e.tensor_tensor(scr8[:], hitk[:], countb[:, k * 8:(k + 1) * 8],
                                                       ALU.mult))
            D_("vector")
            E("vector", lambda e: e.tensor_add(offs[:], offs[:], scr8[:]))
            D_("vector")
        E("vector", lambda e, jp=jp: e.tensor_scalar(
            offs[:], offs[:], -1.0, ppcol[:, jp:jp + 1], ALU.mult, ALU.add))
        D_("vector")
        E("vector", lambda e: e.tensor_scalar(ksum[:], ksum[:], 32896.0, None, ALU.mult, ALU.bypass))
        D_("vector")
        E("vector", lambda e: e.tensor_add(ksum[:], ksum[:], geb512[:]))
        D_("vector")
        tgc = E("vector", lambda e, jp=jp: e.tensor_add(gcols[:, jp * 8:(jp + 1) * 8], ksum[:], offs[:]),
                mark=True)
        t_gci.append(tgc)
    D_("vector")
    tgi = E("vector", lambda e: e.tensor_copy(out=gci[:], in_=gcols[:]), deps=t_gci, mark=True)
    D_("vector")
    t_gidx = []
    for jp in range(4):
        for el in range(8):
            tg = E("gpsimd", lambda e, jp=jp, el=el: e.indirect_dma_start(
                out=mvf[:], out_offset=None, in_=mapl_all_d,
                in_offset=IndirectOffsetOnAxis(gci[:, jp * 8 + el:jp * 8 + el + 1], 0)),
                deps=[tgi] + ([last.get("mvf")] if last.get("mvf") else []), mark=True, dma=True)
            tmn = E("vector", lambda e, jp=jp, el=el: e.tensor_scalar_min(
                gcols[:, jp * 8 + el:jp * 8 + el + 1], mvf[:], 32768.0), deps=[tg], mark=True)
            last["mvf"] = tmn
            t_gidx.append(tmn)
    D_("vector")
    tgi2 = E("vector", lambda e: e.tensor_copy(out=gci[:], in_=gcols[:]), deps=t_gidx, mark=True)

    def emit_slot_tile(tile):
        row0 = tile * 128
        tl1 = E("sync", lambda e, row0=row0: e.dma_start(out=ekb[:], in_=ek_d[row0:row0 + 128, :]),
                deps=[last.get("ekb_rd")] if last.get("ekb_rd") else [], mark=True, dma=True,
                stream="sl1")
        tl2 = E("sync", lambda e, row0=row0: e.dma_start(out=plb[:], in_=pl_d[row0:row0 + 128, :]),
                mark=True, dma=True, stream="sl2")
        tl3 = E("sync", lambda e, row0=row0: e.dma_start(out=wb[:], in_=wt_d[row0:row0 + 128, :]),
                mark=True, dma=True, stream="sl3")
        for kk in range(8):
            E("vector", lambda e, kk=kk: e.tensor_single_scalar(
                ohk[:], iota64[:], ekb[:, kk:kk + 1], ALU.is_equal), deps=[tl1, tb6])
            D_("vector")
            E("vector", lambda e: e.tensor_tensor(scr64[:], ohk[:], offb64[:], ALU.mult))
            D_("vector")
            E("vector", lambda e: e.reduce_sum(offcol[:], scr64[:], axis=AX.X))
            D_("vector")
            E("vector", lambda e, kk=kk: e.tensor_tensor(
                posg[:, kk:kk + 1], plb[:, kk:kk + 1], offcol[:], ALU.add), deps=[tl2])
            D_("vector")
        tvd = E("vector", lambda e: e.tensor_single_scalar(validb[:], posg[:], 512.0, ALU.is_lt),
                mark=True)
        D_("vector")
        E("vector", lambda e: e.tensor_scalar(slotf[:], ekb[:], 512.0, None, ALU.mult, ALU.bypass))
        D_("vector")
        E("vector", lambda e: e.tensor_add(slotf[:], slotf[:], posg[:]))
        D_("vector")
        E("vector", lambda e: e.tensor_scalar_add(slotf[:], slotf[:], -32768.0))
        D_("vector")
        E("vector", lambda e: e.tensor_tensor(slotf[:], slotf[:], validb[:], ALU.mult))
        D_("vector")
        tsl = E("vector", lambda e: e.tensor_scalar_add(slotf[:], slotf[:], 32768.0), mark=True)
        D_("vector")
        twv = E("vector", lambda e: e.tensor_tensor(wb[:], wb[:], validb[:], ALU.mult),
                deps=[tl3, tvd], mark=True)
        ts1 = E("sync", lambda e, row0=row0: e.dma_start(out=slotid_d[row0:row0 + 128, :], in_=slotf[:]),
                deps=[tsl], mark=True, dma=True, stream="ss1")
        ts2 = E("sync", lambda e, row0=row0: e.dma_start(out=wv_d[row0:row0 + 128, :], in_=wb[:]),
                deps=[twv], mark=True, dma=True, stream="ss2")
        last["ekb_rd"] = ts2
        P.wait("vector", ts1)
        P.wait("vector", ts2)

    for el in range(8):
        wslot = el % 2
        twl1 = E("sync", lambda e, el=el: e.dma_start(
            out=wstg[0][:],
            in_=gw_d[el * 1024:(el + 1) * 1024, :].rearrange("(k p) m -> p k m", p=128)),
            deps=([last.get("wstg_g")] if last.get("wstg_g") else []),
            mark=True, dma=True, stream="egw")
        tw1 = E("vector", lambda e, el=el: e.tensor_copy(out=gwe[el % 2][:], in_=wstg[0][:]),
                deps=[twl1] + ([last.get(("gwe", el % 2))] if last.get(("gwe", el % 2)) else []),
                mark=True)
        last["wstg_g"] = tw1
        twl2 = E("sync", lambda e, el=el: e.dma_start(
            out=wstg[1][:],
            in_=uw_d[el * 1024:(el + 1) * 1024, :].rearrange("(k p) m -> p k m", p=128)),
            deps=([last.get("wstg_u")] if last.get("wstg_u") else []),
            mark=True, dma=True, stream="euw")
        tw2 = E("vector", lambda e, el=el: e.tensor_copy(out=uwe[el % 2][:], in_=wstg[1][:]),
                deps=[twl2] + ([last.get(("uwe", el % 2))] if last.get(("uwe", el % 2)) else []),
                mark=True)
        last["wstg_u"] = tw2
        twl3 = E("sync", lambda e, el=el: e.dma_start(
            out=wstg[2][:],
            in_=dw_d[el * 512:(el + 1) * 512, :].rearrange("(k p) m -> p k m", p=128)),
            deps=([last.get("wstg_d")] if last.get("wstg_d") else []),
            mark=True, dma=True, stream="edw")
        tw3 = E("vector", lambda e, el=el: e.tensor_copy(out=dwe[el % 2][:], in_=wstg[2][:]),
                deps=[twl3] + ([last.get(("dwe", el % 2))] if last.get(("dwe", el % 2)) else []),
                mark=True)
        last["wstg_d"] = tw3
        t_disp = []
        for jp in range(4):
            tg = E("gpsimd", lambda e, jp=jp, el=el: e.indirect_dma_start(
                out=disp_tm[jp % 2][:], out_offset=None, in_=xall_d,
                in_offset=IndirectOffsetOnAxis(gci[:, jp * 8 + el:jp * 8 + el + 1], 0)),
                deps=[tgi2] + ([last.get(("disp_tm", jp % 2))] if last.get(("disp_tm", jp % 2)) else []),
                mark=True, dma=True, stream=("dsp", jp % 2))
            ttr = None
            for kd in range(8):
                ttr = E("tensor", lambda e, kd=kd, jp=jp: e.transpose(
                    pTR[:], disp_tm[jp % 2][:, kd * 128:(kd + 1) * 128], eyebf[:]),
                    deps=[tg] + ([last.get("pTRe")] if last.get("pTRe") else []), mark=True)
                tev = E("scalar", lambda e, kd=kd, jp=jp: e.copy(
                    out=disp_fm[:, kd * 512 + jp * 128: kd * 512 + (jp + 1) * 128],
                    in_=pTR[:]), deps=[ttr], mark=True)
                last["pTRe"] = tev
                t_disp.append(tev)
            last[("disp_tm", jp % 2)] = ttr
        t_ge = []
        for m in range(4):
            tmm = None
            for k in range(8):
                tmm = E("tensor", lambda e, k=k, m=m, el=el: e.matmul(
                    pA[:], gwe[el % 2][:, k * 512 + m * 128: k * 512 + (m + 1) * 128],
                    disp_fm[:, k * 512:(k + 1) * 512], start=(k == 0), stop=(k == 7)),
                    deps=([tw1] + t_disp + ([last.get("pA")] if last.get("pA") else [])
                          if k == 0 else []), mark=(k == 7))
            tev = E("scalar", lambda e, m=m: e.activation(
                gsil[:, m * 512:(m + 1) * 512], pA[:], AF.Silu),
                deps=[tmm] + ([last.get(("gsil", m))] if last.get(("gsil", m)) else []),
                mark=True)
            last["pA"] = tev
            t_ge.append(tev)
        t_ue = []
        for m in range(4):
            tmm = None
            for k in range(8):
                tmm = E("tensor", lambda e, k=k, m=m, el=el: e.matmul(
                    pB2[:], uwe[el % 2][:, k * 512 + m * 128: k * 512 + (m + 1) * 128],
                    disp_fm[:, k * 512:(k + 1) * 512], start=(k == 0), stop=(k == 7)),
                    deps=([tw2] + ([last.get("pB2")] if last.get("pB2") else []) if k == 0 else []),
                    mark=(k == 7))
            tev = E("scalar", lambda e, m=m: e.copy(
                out=upf[:, m * 512:(m + 1) * 512], in_=pB2[:]),
                deps=[tmm] + ([last.get(("upf", m))] if last.get(("upf", m)) else []),
                mark=True)
            last["pB2"] = tev
            t_ue.append(tev)
        t_gue = []
        for m in range(4):
            tgu = E("vector", lambda e, m=m: e.tensor_tensor(
                gu_bf[:, m * 512:(m + 1) * 512], gsil[:, m * 512:(m + 1) * 512],
                upf[:, m * 512:(m + 1) * 512], ALU.mult),
                deps=[t_ge[m], t_ue[m]] + ([last.get(("gue", m))] if last.get(("gue", m)) else []),
                mark=True)
            t_gue.append(tgu)
        for m in range(4):
            last[("gsil", m)] = t_gue[m]
            last[("upf", m)] = t_gue[m]
        last[("gwe", el % 2)] = t_ge[-1]
        last[("uwe", el % 2)] = t_ue[-1]
        t_we = []
        for md in range(8):
            tmm = None
            for k in range(4):
                tmm = E("tensor", lambda e, k=k, md=md, el=el: e.matmul(
                    pA[:], dwe[el % 2][:, k * 1024 + md * 128: k * 1024 + (md + 1) * 128],
                    gu_bf[:, k * 512:(k + 1) * 512], start=(k == 0), stop=(k == 3)),
                    deps=([tw3] + t_gue + ([last.get("pA")] if last.get("pA") else [])
                          if k == 0 else []), mark=(k == 3))
            tev = E("scalar", lambda e, md=md: e.copy(
                out=wout_fm[:, md * 512:(md + 1) * 512], in_=pA[:]),
                deps=[tmm] + ([last.get("wfm_all")] if last.get("wfm_all") else []), mark=True)
            last["pA"] = tev
            t_we.append(tev)
        for m in range(4):
            last[("gue", m)] = t_we[-1]
        last[("dwe", el % 2)] = t_we[-1]
        for jp in range(4):
            t_evs = []
            for md in range(8):
                ttr = E("tensor", lambda e, md=md, jp=jp: e.transpose(
                    pTR[:], wout_fm[:, md * 512 + jp * 128: md * 512 + (jp + 1) * 128], eyebf[:]),
                    deps=[t_we[md]] + ([last.get("pTRe")] if last.get("pTRe") else []), mark=True)
                tev = E("scalar", lambda e, md=md: e.copy(
                    out=wout_tm[:, md * 128:(md + 1) * 128], in_=pTR[:]),
                    deps=[ttr] + ([last.get("wtm_rd")] if last.get("wtm_rd") else []), mark=True)
                last["pTRe"] = tev
                t_evs.append(tev)
                if md == 7:
                    last[("wfm", jp)] = ttr
            tst = E("sync", lambda e, jp=jp, el=el: e.dma_start(
                out=wout_d[el * 512 + jp * 128: el * 512 + (jp + 1) * 128, :],
                in_=wout_tm[:]), deps=t_evs, mark=True, dma=True)
            last["wtm_rd"] = tst
        last["wfm_all"] = last["pTRe"]
        for tile in range(el * 4, (el + 1) * 4):
            emit_slot_tile(tile)

    P.final_waits()

    with nc.Block() as block:
        P.finalize(block)
    stack.close()
    return nc


def host_prep_E(inputs, counts_list, mapl_list, xnbf_list, ek_list, pl_list, wt_list):
    counts_all = np.concatenate(counts_list, axis=0)
    mapl_all = np.concatenate(mapl_list, axis=0)
    zero_row = np.zeros((1, D_MODEL), bf16)
    xall = np.concatenate([np.asarray(x) for x in xnbf_list] + [zero_row], axis=0)
    lowincl = (np.arange(8)[:, None] <= np.arange(8)[None, :]).astype(np.float32)
    iota64 = np.broadcast_to(np.arange(64, dtype=np.float32), (128, 64)).copy()
    ppcol = np.zeros((128, 4), np.float32)
    for jp in range(4):
        ppcol[:, jp] = jp * 128 + np.arange(128)
    gw = np.asarray(inputs["L1_ffn_gate_w"], np.float32)
    uw = np.asarray(inputs["L1_ffn_up_w"], np.float32)
    dw = np.asarray(inputs["L1_ffn_down_w"], np.float32)
    in_maps = []
    for c in range(N_CORES):
        m = {
            "cnt_my": np.ascontiguousarray(counts_all[:, c * 8:(c + 1) * 8]),
            "cnt_all": counts_all,
            "premask": (np.arange(8)[:, None] < c).astype(np.float32),
            "lowincl": lowincl,
            "mapl_all": mapl_all,
            "xall": xall,
            "gw": np.ascontiguousarray(gw[c * 8:(c + 1) * 8].reshape(8 * D_MODEL, 512)),
            "uw": np.ascontiguousarray(uw[c * 8:(c + 1) * 8].reshape(8 * D_MODEL, 512)),
            "dw": np.ascontiguousarray(dw[c * 8:(c + 1) * 8].reshape(8 * 512, D_MODEL)),
            "ek": np.asarray(ek_list[c]), "pl": np.asarray(pl_list[c]), "wt": np.asarray(wt_list[c]),
            "eye_bf": np.eye(128, dtype=bf16),
            "ones1": np.ones((1, 128), np.float32),
            "ppcol": ppcol,
            "geb512": np.broadcast_to(((c * 8 + np.arange(8)) * 512).astype(np.float32), (128, 8)).copy(),
            "iota64": iota64,
        }
        in_maps.append(m)
    return in_maps


def run_stage_E(inputs, stage_d_results):
    nc = build_stage_E()
    in_maps = host_prep_E(
        inputs,
        [np.asarray(r["counts"]) for r in stage_d_results],
        [np.asarray(r["mapl"]) for r in stage_d_results],
        [np.asarray(r["xnbf"]) for r in stage_d_results],
        [np.asarray(r["ek"]) for r in stage_d_results],
        [np.asarray(r["pl"]) for r in stage_d_results],
        [np.asarray(r["wt"]) for r in stage_d_results])
    res = run_bass_kernel_spmd(nc, in_maps, core_ids=list(range(N_CORES)))
    return res.results



def build_stage_F():
    nc = bass.Bass("TRN2", target_bir_lowering=False, debug=False)
    stack = contextlib.ExitStack()

    woutall_d = nc.dram_tensor("woutall", [8 * T_CORE 
+ 1, D_MODEL], BF16, kind="ExternalInput").ap()
    slotid_d = nc.dram_tensor("slotid", [T_CORE, 8], F32, kind="ExternalInput").ap()
    wv_d = nc.dram_tensor("wv", [T_CORE, 8], F32, kind="ExternalInput").ap()
    pout_d = nc.dram_tensor("pout", [T_CORE, D_MODEL], F32, kind="ExternalInput").ap()
    out_d = nc.dram_tensor("out", [T_CORE, D_MODEL], F32, kind="ExternalOutput").ap()

    S = stack.enter_context
    sb = nc.sbuf_tensor

    slotb = S(sb("slotb", [128, 8], F32))
    sloti = S(sb("sloti", [128, 8], I32))
    wvb = S(sb("wvb", [128, 8], F32))
    gk = [S(sb(f"gk{i}", [128, D_MODEL], BF16)) for i in range(4)]
    gkf = [S(sb(f"gkf{i}", [128, D_MODEL], F32)) for i in range(4)]
    acc = S(sb("acc", [128, D_MODEL], F32))
    pob = S(sb("pob", [128, D_MODEL], F32))

    P = Prog(nc, stack)
    E = P.emit
    last = {}
    D_ = P.drain

    for tile in range(32):
        row0 = tile * 128
        tl1 = E("sync", lambda e, row0=row0: e.dma_start(out=slotb[:], in_=slotid_d[row0:row0 + 128, :]),
                deps=[last.get("slot_rd")] if last.get("slot_rd") else [], mark=True, dma=True,
                stream="fsl")
        tl2 = E("sync", lambda e, row0=row0: e.dma_start(out=wvb[:], in_=wv_d[row0:row0 + 128, :]),
                deps=[last.get("wv_rd")] if last.get("wv_rd") else [], mark=True, dma=True,
                stream="fwv")
        tl3 = E("sync", lambda e, row0=row0: e.dma_start(out=pob[:], in_=pout_d[row0:row0 + 128, :]),
                deps=[last.get("pob_rd")] if last.get("pob_rd") else [], mark=True, dma=True,
                stream="fpo")
        tsi = E("vector", lambda e: e.tensor_copy(out=sloti[:], in_=slotb[:]),
                deps=[tl1] + [last[("gk", j)] for j in range(4) if last.get(("gk", j))],
                mark=True)
        last["slot_rd"] = tsi
        D_("vector")
        tacc = None
        for kk in range(8):
            tg = E("gpsimd", lambda e, kk=kk: e.indirect_dma_start(
                out=gk[kk % 4][:], out_offset=None, in_=woutall_d,
                in_offset=IndirectOffsetOnAxis(sloti[:, kk:kk + 1], 0)),
                deps=[tsi] + ([last.get(("gk", kk % 4))] if last.get(("gk", kk % 4)) else []),
                mark=True, dma=True, stream=("gk", kk % 4))
            tcv = E("scalar", lambda e, kk=kk: e.copy(out=gkf[kk % 4][:], in_=gk[kk % 4][:]),
                    deps=[tg] + ([last.get(("gkf", kk % 4))] if last.get(("gkf", kk % 4)) else []),
                    mark=True)
            last[("gk", kk % 4)] = tcv
            if kk == 0:
                tacc = E("vector", lambda e, kk=kk: e.tensor_scalar_mul(
                    acc[:], gkf[kk % 4][:], wvb[:, kk:kk + 1]), deps=[tl2, tcv], mark=True)
            else:
                tacc = E("vector", lambda e, kk=kk: e.scalar_tensor_tensor(
                    acc[:], gkf[kk % 4][:], wvb[:, kk:kk + 1], acc[:], ALU.mult, ALU.add),
                    deps=[tcv], mark=True)
            last[("gkf", kk % 4)] = tacc
        last["wv_rd"] = tacc
        tfin = E("vector", lambda e: e.tensor_tensor(acc[:], acc[:], pob[:], ALU.add),
                 deps=[tl3, tacc], mark=True)
        last["pob_rd"] = tfin
        tst = E("sync", lambda e, row0=row0: e.dma_start(out=out_d[row0:row0 + 128, :], in_=acc[:]),
                deps=[tfin], mark=True, dma=True, stream="fst")
        P.wait("vector", tst)

    P.final_waits()

    with nc.Block() as block:
        P.finalize(block)
    stack.close()
    return nc


def run_stage_F(inputs, stage_e_results, pout_list):
    nc = build_stage_F()
    zero_row = np.zeros((1, D_MODEL), bf16)
    woutall = np.concatenate([np.asarray(r["wout"]) for r in stage_e_results] + [zero_row], axis=0)
    in_maps = []
    for c in range(N_CORES):
        in_maps.append({
            "woutall": woutall,
            "slotid": np.asarray(stage_e_results[c]["slotid"]),
            "wv": np.asarray(stage_e_results[c]["wv"]),
            "pout": np.asarray(pout_list[c]),
        })
    res = run_bass_kernel_spmd(nc, in_maps, core_ids=list(range(N_CORES)))
    return [np.asarray(r["out"]) for r in res.results]




def build_fused_ABCD():
    nc = bass.Bass("TRN2", target_bir_lowering=False, debug=False)
    master = contextlib.ExitStack()
    F = F32; B_ = BF16
    d = {}
    def din(name, shape, dt=F32):
        d[name] = nc.dram_tensor(name, shape, dt, kind="ExternalInput").ap()
    def dout(name, shape, dt=F32):
        d[name] = nc.dram_tensor(name, shape, dt, kind="ExternalOutput").ap()
    def dint(name, shape, dt=F32):
        d[name] = nc.dram_tensor(name, shape, dt, kind="Internal").ap()
    din("x", [T_CORE, D_MODEL]); din("W_in", [D_MODEL, 2 * D_INNER])
    din("convw", [128, 64]); din("convb", [128, 16]); din("Wx", [D_INNER, 96])
    din("Alog", [128, 256]); din("Dp", [128, 16]); din("W_out", [D_INNER, D_MODEL])
    din("eye_bf", [128, 128], B_); din("eye_f", [128, 128]); din("ones1", [1, 128])
    din("oh16", [16, 2048]); din("consts", [128, 4])
    din("WqT", [D_MODEL, D_MODEL]); din("RtT", [D_MODEL, 256]); din("LV", [256, D_MODEL])
    din("WoT", [D_MODEL, D_MODEL]); din("wout_b", [128, 8]); din("onesw", [128, 128])
    din("BwT", [D_MODEL, 512]); din("CwT", [512, D_MODEL]); din("Dvec", [1, D_MODEL])
    din("apar", [128, 2]); din("ang1", [256, 512]); din("ang2", [256, 512])
    din("rw", [D_MODEL, 64]); din("sgw", [D_MODEL, 512]); din("suw", [D_MODEL, 512])
    din("sdw", [512, D_MODEL]); din("onescol", [128, 1]); din("iota64", [128, 64])
    din("trilS", [128, 128]); din("gidx", [128, 32])
    dout("pout", [T_CORE, D_MODEL]); dout("xnbf", [T_CORE, D_MODEL], B_)
    dout("counts", [1, 64]); dout("ek", [T_CORE, 8]); dout("pl", [T_CORE, 8])
    dout("wt", [T_CORE, 8]); dout("mapl", [32896, 1])
    dint("x1", [T_CORE, D_MODEL]); dint("x2", [T_CORE, D_MODEL]); dint("x3", [T_CORE, D_MODEL])
    dint("winbf", [D_MODEL, 2 * D_INNER], B_); dint("woutbf", [D_INNER, D_MODEL], B_)
    dint("wqbf", [D_MODEL, D_MODEL], B_); dint("rtbf", [D_MODEL, 256], B_)
    dint("lvbf", [256, D_MODEL], B_); dint("wobf", [D_MODEL, D_MODEL], B_)
    dint("bwtbf", [D_MODEL, 512], B_); dint("cwtbf", [512, D_MODEL], B_)
    dint("rwbf", [D_MODEL, 64], B_); dint("sgwbf", [D_MODEL, 512], B_)
    dint("suwbf", [D_MODEL, 512], B_); dint("sdwbf", [512, D_MODEL], B_)
    P = Prog(nc, master)
    for emit_fn in (emit_stage_A, emit_stage_B, emit_stage_C, emit_stage_D):
        sub = contextlib.ExitStack()
        emit_fn(nc, P, sub, d)
        P.barrier()
        sub.close()
    P.final_waits()
    with nc.Block() as block:
        P.finalize(block)
    master.close()
    return nc


def host_prep_ABCD(inputs):
    a_maps = host_prep_A(inputs)
    rt = np.asarray(inputs["L0_ffn_routers"], np.float32)
    rt_pad = np.zeros((D_MODEL, 256), np.float32)
    rt_pad[:, 0:255] = rt.T
    j = np.arange(256, dtype=np.int64)[:, None]
    m = np.arange(512, dtype=np.int64)[None, :]
    def ang(r):
        rm = np.mod(r, 512)
        rm = np.where(rm >= 256, rm - 512, rm)
        return (rm.astype(np.float64) * (2 * np.pi / 512)).astype(np.float32)
    ang1 = ang((j + 1) * m)
    ang2 = ang(j * m)
    ang2[0, :] = 0.0
    apar = np.zeros((128, 2), np.float32)
    apar[:, 0] = np.asarray(inputs["L1_attn_a_params"], np.float32)[0:128]
    apar[:, 1] = np.asarray(inputs["L1_attn_a_params"], np.float32)[128:256]
    iota64 = np.broadcast_to(np.arange(64, dtype=np.float32), (128, 64)).copy()
    trils = (np.arange(128)[:, None] < np.arange(128)[None, :]).astype(np.float32)
    common = {
        "WqT": np.ascontiguousarray(np.asarray(inputs["L0_ffn_Wq"], np.float32).T),
        "RtT": rt_pad,
        "LV": np.ascontiguousarray(np.asarray(inputs["L0_ffn_leaf_values"], np.float32)),
        "WoT": np.ascontiguousarray(np.asarray(inputs["L0_ffn_Wout_w"], np.float32).T),
        "wout_b": np.ascontiguousarray(np.asarray(inputs["L0_ffn_Wout_b"], np.float32).reshape(8, 128).T),
        "onesw": np.ones((128, 128), np.float32),
        "BwT": np.ascontiguousarray(np.asarray(inputs["L1_attn_B_w"], np.float32).T),
        "CwT": np.ascontiguousarray(np.asarray(inputs["L1_attn_C_w"], np.float32).T),
        "Dvec": np.ascontiguousarray(np.asarray(inputs["L1_attn_D"], np.float32).reshape(1, D_MODEL)),
        "apar": apar, "ang1": ang1, "ang2": ang2,
        "rw": np.ascontiguousarray(np.asarray(inputs["L1_ffn_router_w"], np.float32)),
        "sgw": np.ascontiguousarray(np.asarray(inputs["L1_ffn_shared_gate_w"], np.float32)),
        "suw": np.ascontiguousarray(np.asarray(inputs["L1_ffn_shared_up_w"], np.float32)),
        "sdw": np.ascontiguousarray(np.asarray(inputs["L1_ffn_shared_down_w"], np.float32)),
        "onescol": np.ones((128, 1), np.float32),
        "iota64": iota64, "trilS": trils,
    }
    in_maps = []
    for c in range(N_CORES):
        mm = dict(a_maps[c])
        mm.update(common)
        g = np.zeros((128, 32), np.float32)
        for tile in range(32):
            g[:, tile] = c * T_CORE + tile * 128 + np.arange(128)
        mm["gidx"] = g
        in_maps.append(mm)
    return in_maps


def run_fused_ABCD(inputs):
    nc = build_fused_ABCD()
    in_maps = host_prep_ABCD(inputs)
    res = run_bass_kernel_spmd(nc, in_maps, core_ids=list(range(N_CORES)))
    return res.results


def kernel(**inputs):
    rd = run_fused_ABCD(inputs)
    re_ = run_stage_E(inputs, rd)
    out_shards = run_stage_F(inputs, re_, [np.asarray(r["pout"]) for r in rd])
    out = np.stack([s.reshape(B_PER_CORE, SEQ, D_MODEL) for s in out_shards])
    return np.ascontiguousarray(out.reshape(16, SEQ, D_MODEL))
